# version 13
# baseline (speedup 1.0000x reference)
import math
import numpy as np
import ml_dtypes
import concourse.bass as bass
import concourse.mybir as mybir
from concourse.bass_utils import run_bass_kernel_spmd
from concourse.alu_op_type import AluOpType
from contextlib import ExitStack

F32 = mybir.dt.float32
BF16 = mybir.dt.bfloat16
AF = mybir.ActivationFunctionType
AX = mybir.AxisListType.X

ENGINES = ("tensor", "vector", "scalar", "gpsimd", "sync")


class Prog:

    def __init__(self, nc, ctx):
        self.nc = nc
        self.ctx = ctx
        self.items = {e: [] for e in ENGINES}
        self.sems = {}
        self.cnt = {}

    def sem(self, name):
        if name not in self.sems:
            self.sems[name] = self.ctx.enter_context(self.nc.semaphore(name))
            self.cnt[name] = 0
        return name

    def op(self, eng, fn, inc=None, inc16=None):
        tok = None
        incs = []
        if inc is not None:
            self.sem(inc)
            self.cnt[inc] += 1
            incs.append((inc, 1))
            tok = (inc, self.cnt[inc])
        if inc16 is not None:
            self.sem(inc16)
            self.cnt[inc16] += 16
            incs.append((inc16, 16))
            tok = (inc16, self.cnt[inc16])
        self.items[eng].append(("op", fn, incs))
        return tok

    def wait(self, eng, tok):
        if tok is None:
            return
        sem, val = tok
        self.sem(sem)
        if val > 0:
            self.items[eng].append(("wait", sem, val))

    def wait_all(self, eng, toks):
        best = {}
        for t in toks:
            if t is None:
                continue
            s, v = t
            best[s] = max(best.get(s, 0), v)
        for s, v in sorted(best.items()):
            self.wait(eng, (s, v))

    def now(self, sem):
        self.sem(sem)
        return (sem, self.cnt[sem])

    def barrier(self, name, dma_sems=(), engines=("tensor", "vector", "scalar")):
        toks = [self.now(s) for s in dma_sems if s in self.cnt]
        for e in engines:
            toks.append(self.op(e, lambda eg: eg.drain(), inc=self.sem(name)))
        for e in ENGINES:
            self.wait_all(e, toks)

    def emit(self, block):
        for eng in ENGINES:
            dec = getattr(block, eng)
            items = self.items[eng]

            def make(items):
                def _(e):
                    for it in items:
                        if it[0] == "wait":
                            e.wait_ge(self.sems[it[1]], it[2])
                        else:
                            inst = it[1](e)
                            for s, k in it[2]:
                                inst.then_inc(self.sems[s], k)
                return _
            dec(make(items))


class Tok:
    def __init__(self):
        self.w = []
        self.r = []

    def wrote(self, tok):
        self.w = [tok] if tok is not None else []
        self.r = []

    def read(self, tok):
        if tok is not None:
            self.r.append(tok)

    def raw(self):
        return list(self.w)

    def war(self):
        return list(self.w) + list(self.r)




def _align_up(x, a):
    return (x + a - 1) & ~(a - 1)


class SbArena:

    def __init__(self, apf, cap_bytes):
        self.apf = apf
        self.apb = apf.bitcast(BF16)
        self.cap = cap_bytes
        self.off = 0
        self.peak = 0

    def alloc(self, parts, cols, dtype):
        sz = cols * (4 if dtype is F32 else 2)
        off = _align_up(self.off, 64)
        self.off = off + sz
        self.peak = max(self.peak, self.off)
        assert self.off <= self.cap, (self.off, self.cap)
        if dtype is F32:
            return self.apf[0:parts, off // 4: off // 4 + cols]
        return self.apb[0:parts, off // 2: off // 2 + cols]

    def mark(self):
        return self.off

    def reset(self, m):
        self.off = m


def strided(slice_ap, extra, dims):
    part = list(slice_ap.ap[0])
    return bass.AP(slice_ap.tensor, slice_ap.offset + extra,
                   [part] + [list(d) for d in dims])


def default_cfg():
    return dict(S=8192, D=2048, INTER=8192, RINTER=1024)


def build(cfg, debug=False):
    S = cfg["S"]
    D = cfg["D"]
    INTER = cfg["INTER"]
    RINTER = cfg["RINTER"]
    NCH = D // 128
    NG = S // 512
    NO = S // 4
    NOG = NO // 512
    NIC = INTER // 128
    NRIC = RINTER // 128
    SCALE = 1.0 / math.sqrt(128.0)
    ESHIFT = -20.0
    VW = (S // 128) * 129

    nc = bass.Bass("TRN2", target_bir_lowering=False)

    xb = nc.dram_tensor("xb", [S, D], F32, kind="ExternalInput").ap()
    xo = nc.dram_tensor("xo", [NO, D], F32, kind="ExternalInput").ap()
    wq = nc.dram_tensor("wq", [D, 2048], F32, kind="ExternalInput").ap()
    wk = nc.dram_tensor("wk", [D, 512], F32, kind="ExternalInput").ap()
    wv = nc.dram_tensor("wv", [D, 512], F32, kind="ExternalInput").ap()
    wo = nc.dram_tensor("wo", [2048, D], F32, kind="ExternalInput").ap()
    wg = nc.dram_tensor("wg", [D, INTER], F32, kind="ExternalInput").ap()
    wu = nc.dram_tensor("wu", [D, INTER], F32, kind="ExternalInput").ap()
    wd = nc.dram_tensor("wd", [INTER, D], F32, kind="ExternalInput").ap()
    rg = nc.dram_tensor("rg", [D, RINTER], F32, kind="ExternalInput").ap()
    ru = nc.dram_tensor("ru", [D, RINTER], F32, kind="ExternalInput").ap()
    rd = nc.dram_tensor("rd", [RINTER, D], F32, kind="ExternalInput").ap()
    rw = nc.dram_tensor("rw", [D, 8], F32, kind="ExternalInput").ap()
    cosf = nc.dram_tensor("cosf", [64, S], F32, kind="ExternalInput").ap()
    sinf = nc.dram_tensor("sinf", [64, S], F32, kind="ExternalInput").ap()
    coso = nc.dram_tensor("coso", [64, NO], F32, kind="ExternalInput").ap()
    sino = nc.dram_tensor("sino", [64, NO], F32, kind="ExternalInput").ap()
    maskt = nc.dram_tensor("maskt", [16, 128, 512], BF16, kind="ExternalInput").ap()
    ident = nc.dram_tensor("ident", [128, 128], BF16, kind="ExternalInput").ap()
    y = nc.dram_tensor("y", [NO, D], F32, kind="ExternalOutput").ap()

    wq_b = nc.dram_tensor("wq_b", [16, 128, D], BF16, kind="Internal").ap()
    wo_b = nc.dram_tensor("wo_b", [2048, D], BF16, kind="Internal").ap()
    wg_p = nc.dram_tensor("wg_p", [NIC, 128, NCH * 128], BF16, kind="Internal").ap()
    wu_p = nc.dram_tensor("wu_p", [NIC, 128, NCH * 128], BF16, kind="Internal").ap()
    rg_p = nc.dram_tensor("rg_p", [NRIC, 128, NCH * 128], BF16, kind="Internal").ap()
    ru_p = nc.dram_tensor("ru_p", [NRIC, 128, NCH * 128], BF16, kind="Internal").ap()
    wd_b = nc.dram_tensor("wd_b", [INTER, D], BF16, kind="Internal").ap()
    rd_b = nc.dram_tensor("rd_b", [RINTER, D], BF16, kind="Internal").ap()
    k_d = nc.dram_tensor("k_d", [4, 128, S], BF16, kind="Internal").ap()
    v_d = nc.dram_tensor("v_d", [4, 128, VW], BF16, kind="Internal").ap()
    q_d = nc.dram_tensor("q_d", [16, 128, NO], BF16, kind="Internal").ap()
    xn_d = nc.dram_tensor("xn_d", [NCH, 128, NO], BF16, kind="Internal").ap()
    attn_d = nc.dram_tensor("attn_d", [2048, NO], BF16, kind="Internal").ap()

    dbg = {}
    if debug:
        for name, t in [("d_k", k_d), ("d_v", v_d), ("d_q", q_d),
                        ("d_xn", xn_d), ("d_attn", attn_d)]:
            dbg[name] = nc.dram_tensor(name, list(t.shape), BF16,
                                       kind="ExternalOutput").ap()

    with ExitStack() as octx:
        prog = Prog(nc, octx)
        ARENA_COLS = 49152
        arena_t = octx.enter_context(nc.sbuf_tensor("arena", [128, ARENA_COLS], F32))
        A = SbArena(arena_t.ap(), ARENA_COLS * 4)
        pa = [octx.enter_context(nc.psum_tensor(f"pa{i}", [128, 1024], F32))
              for i in range(4)]
        paf = [p.ap() for p in pa]
        pab = [p.ap().bitcast(BF16) for p in pa]

        idb = A.alloc(128, 128, BF16)
        c_scale = A.alloc(128, 1, F32)
        c_shift = A.alloc(128, 1, F32)
        m_persist = A.mark()
        pst_in = [A.alloc(128, 2048, F32) for i in range(2)]
        pst_out = [A.alloc(128, 2048, BF16) for i in range(2)]
        m_prep = A.mark()

        T = {}
        def tk(name):
            if name not in T:
                T[name] = Tok()
            return T[name]

        t_id = prog.op("sync", lambda e: e.dma_start(out=idb[:, :], in_=ident),
                       inc16="d_cst")
        prog.op("vector", lambda e: e.memset(c_scale[:, :], SCALE))
        t_cst = prog.op("vector", lambda e: e.memset(c_shift[:, :], ESHIFT), inc="dv")

        prep_jobs = []

        def prep_matrix(src, dst, nrows, ncols, out_ap_fn=None):
            for c in range(nrows // 128):
                nq = (ncols + 2047) // 2048
                for qq in range(nq):
                    w = min(2048, ncols - qq * 2048)
                    if out_ap_fn is None:
                        dst_ap = dst[c * 128:(c + 1) * 128, qq * 2048:qq * 2048 + w]
                    else:
                        dst_ap = out_ap_fn(c, qq, w)
                    src_ap = src[c * 128:(c + 1) * 128, qq * 2048:qq * 2048 + w]
                    prep_jobs.append((src_ap, dst_ap, w))

        def pump_prep(n):
            for _ in range(n):
                if pump_prep.i >= len(prep_jobs):
                    return
                src_ap, dst_ap, w = prep_jobs[pump_prep.i]
                pump_prep.i += 1
                par = pump_prep.i % 2
                sin = pst_in[par]
                sout = pst_out[par]
                prog.wait_all("gpsimd", tk(f"pstin{par}").war())
                tin = prog.op(
                    "gpsimd",
                    lambda e, src_ap=src_ap, w=w, sin=sin: e.dma_start(
                        out=sin[:, 0:w], in_=src_ap),
                    inc16=f"g_in{par}")
                tk(f"pstin{par}").wrote(tin)
                prog.wait("gpsimd", tin)
                prog.wait_all("gpsimd", tk(f"pstout{par}").war())
                tcv = prog.op(
                    "gpsimd",
                    lambda e, w=w, sin=sin, sout=sout: e.tensor_copy(
                        sout[:, 0:w], sin[:, 0:w]),
                    inc="gp")
                tk(f"pstin{par}").read(tcv)
                tk(f"pstout{par}").wrote(tcv)
                prog.wait("sync", tcv)
                tout = prog.op(
                    "sync",
                    lambda e, dst_ap=dst_ap, w=w, sout=sout: e.dma_start(
                        out=dst_ap, in_=sout[:, 0:w]),
                    inc16=f"g_out{par}")
                tk(f"pstout{par}").read(tout)
        pump_prep.i = 0

        def packed_out(dstp):
            def fn(c, qq, w):
                ni = w // 128
                t = dstp[qq * 16:qq * 16 + ni, :, c * 128:(c + 1) * 128]
                return t.rearrange("i f d -> f i d")
            return fn

        prep_matrix(wq, None, D, 2048, out_ap_fn=packed_out(wq_b))
        t_prep_wq = (("g_out0", 8 * 16), ("g_out1", 8 * 16))
        prog.sem("g_out0")
        prog.sem("g_out1")
        prep_matrix(wo, wo_b, 2048, D)
        prep_matrix(wg, None, D, INTER, out_ap_fn=packed_out(wg_p))
        prep_matrix(wu, None, D, INTER, out_ap_fn=packed_out(wu_p))
        prep_matrix(rg, None, D, RINTER, out_ap_fn=packed_out(rg_p))
        prep_matrix(ru, None, D, RINTER, out_ap_fn=packed_out(ru_p))
        prep_matrix(wd, wd_b, INTER, D)
        prep_matrix(rd, rd_b, RINTER, D)

        if True:
            A.reset(m_prep)
            wkf = [A.alloc(128, 512, F32) for i in range(2)]
            wkb = A.alloc(128, NCH * 512, BF16)
            wvb = A.alloc(128, NCH * 512, BF16)
            LNP = 3
            xbuf = [A.alloc(128, D, F32) for i in range(LNP)]
            xnt = [A.alloc(128, D, BF16) for i in range(LNP)]
            xnfm = [A.alloc(128, NCH * 512, BF16) for i in range(2)]
            cosb = [A.alloc(64, 512, F32) for i in range(2)]
            sinb = [A.alloc(64, 512, F32) for i in range(2)]
            tmpA = A.alloc(128, 512, F32)
            tmpB = A.alloc(128, 512, F32)
            krop = [A.alloc(128, 512, BF16) for i in range(2)]
            vtmp = A.alloc(128, 512, BF16)
            vhat = [A.alloc(128, 516, BF16) for i in range(2)]
            stt = [A.alloc(128, 24, F32) for i in range(LNP)]
            mvb = [A.alloc(128, 2, F32) for i in range(LNP)]
            rstd = [A.alloc(128, 1, F32) for i in range(LNP)]
            negm = [A.alloc(128, 1, F32) for i in range(LNP)]
            nbias = [A.alloc(128, 1, F32) for i in range(LNP)]
            ptx = pab[0]
            pk = [paf[1][:, 0:512], paf[1][:, 512:1024]]
            pv = paf[2][:, 0:512]
            ptv = pab[2][:, 1024:2048]
            prope = [paf[3][:, 0:512], paf[3][:, 512:1024]]

            for mat, dstb in ((wk, wkb), (wv, wvb)):
                for c in range(NCH):
                    par = c % 2
                    prog.wait_all("sync", tk(f"wkf{par}").war())
                    tin = prog.op("sync", lambda e, mat=mat, c=c, par=par: e.dma_start(
                        out=wkf[par][:, :], in_=mat[c * 128:(c + 1) * 128, :]),
                        inc16=f"d_wk{par}")
                    tk(f"wkf{par}").wrote(tin)
                    prog.wait("vector", tin)
                    tcv = prog.op("vector", lambda e, dstb=dstb, c=c, par=par:
                                  e.tensor_copy(dstb[:, c * 512:(c + 1) * 512],
                                                wkf[par][:, :]), inc="dv")
                    tk(f"wkf{par}").read(tcv)
            t_wkv = prog.now("dv")

            for i in range(2):
                vh = vhat[i]
                ap1 = strided(vh, 128, [(129, 4)])
                prog.op("vector", lambda e, ap1=ap1: e.memset(ap1, 1.0), inc="dv")

            def ln_subtile(src_rows, n, dst_xnfm, s4):
                par = n % LNP
                xB, xT = xbuf[par], xnt[par]
                sT, mV, rS, nM, nB = stt[par], mvb[par], rstd[par], negm[par], nbias[par]
                prog.wait_all("sync", tk(f"xbuf{par}").war())
                tin = prog.op("sync", lambda e, src_rows=src_rows, xB=xB: e.dma_start(
                    out=xB[:, :], in_=src_rows), inc16=f"d_x{par}")
                tk(f"xbuf{par}").wrote(tin)
                prog.wait("vector", tin)
                prog.wait_all("vector", tk(f"lnchain{par}").war())
                for i in range(4):
                    lst = prog.op("vector", lambda e, i=i, xB=xB, sT=sT: e.bn_stats(
                        sT[:, 6 * i:6 * i + 6], xB[:, 512 * i:512 * i + 512]),
                        inc="dv" if i == 3 else None)
                tk(f"xbuf{par}").read(lst)
                prog.wait("vector", lst)
                tag = prog.op("vector", lambda e, sT=sT, mV=mV: e.bn_aggr(
                    mV[:, :], sT[:, :]), inc="dv")
                prog.wait("vector", tag)
                prog.op("vector", lambda e, mV=mV, nM=nM: e.tensor_scalar(
                    nM[:, :], mV[:, 0:1], -1.0, None, AluOpType.mult))
                tve = prog.op("vector", lambda e, mV=mV: e.tensor_scalar(
                    mV[:, 1:2], mV[:, 1:2], 1e-5, None, AluOpType.add), inc="dv")
                prog.wait("scalar", tve)
                tsq = prog.op("scalar", lambda e, mV=mV, rS=rS: e.activation(
                    rS[:, :], mV[:, 1:2], AF.Sqrt), inc="ac")
                prog.wait("vector", tsq)
                trc = prog.op("vector", lambda e, rS=rS: e.reciprocal(
                    rS[:, :], rS[:, :]), inc="dv")
                prog.wait("vector", trc)
                tnb = prog.op("vector", lambda e, nM=nM, rS=rS, nB=nB: e.tensor_tensor(
                    nB[:, :], nM[:, :], rS[:, :], AluOpType.mult), inc="dv")
                prog.wait("scalar", tnb)
                prog.wait_all("scalar", tk(f"xnt{par}").war())
                taf = prog.op("scalar", lambda e, xB=xB, xT=xT, rS=rS, nB=nB:
                              e.activation(
                    xT[:, :], xB[:, :], AF.Identity, bias=nB[:, :],
                    scale=rS[:, :]), inc="ac")
                tk(f"xbuf{par}").read(taf)
                tk(f"xnt{par}").wrote(taf)
                tk(f"lnchain{par}").wrote(taf)
                prog.wait("tensor", taf)
                prog.wait("tensor", t_id)
                prog.wait_all("tensor", tk("ptx").war())
                for c in range(NCH):
                    ttr = prog.op("tensor", lambda e, c=c, xT=xT: e.transpose(
                        ptx[:, 128 * c:128 * c + 128], xT[:, 128 * c:128 * c + 128],
                        idb[:, :]), inc="pe" if c == NCH - 1 else None)
                tk(f"xnt{par}").read(ttr)
                tk("ptx").wrote(ttr)
                prog.wait("vector", ttr)
                out_ap = strided(dst_xnfm, 128 * s4, [(512, NCH), (1, 128)])
                in_ap = strided(ptx, 0, [(128, NCH), (1, 128)])
                tev = prog.op("vector", lambda e, out_ap=out_ap, in_ap=in_ap:
                              e.tensor_copy(out_ap, in_ap), inc="dv")
                tk("ptx").read(tev)
                return tev

            nsub = 0
            for g in range(NG + NOG):
                pump_prep(2)
                own = g >= NG
                gl = g - NG if own else g
                gpar = g % 2
                fmB = xnfm[gpar]
                prog.wait_all("vector", tk(f"xnfm{gpar}").war())
                evtoks = []
                for s4 in range(4):
                    r0 = gl * 512 + s4 * 128
                    rows = (xo if own else xb)[r0:r0 + 128, :]
                    evtoks.append(ln_subtile(rows, nsub, fmB, s4))
                    nsub += 1
                tk(f"xnfm{gpar}").wrote(evtoks[-1])

                if own:
                    prog.wait_all("scalar", tk(f"xnfm{gpar}").raw())
                    for c in range(NCH):
                        txn = prog.op("scalar", lambda e, c=c, gl=gl, fmB=fmB:
                                      e.dma_start(
                            out=xn_d[c][:, gl * 512:(gl + 1) * 512],
                            in_=fmB[:, 512 * c:512 * c + 512]), inc16=f"d_no{gpar}")
                        tk(f"xnfm{gpar}").read(txn)
                    continue

                cpar = g % 2
                prog.wait_all("sync", tk(f"cosb{cpar}").war())
                prog.op("sync", lambda e, g=g, cpar=cpar: e.dma_start(
                    out=cosb[cpar][:, :], in_=cosf[:, g * 512:(g + 1) * 512]),
                    inc16=f"d_c{cpar}")
                tc2 = prog.op("sync", lambda e, g=g, cpar=cpar: e.dma_start(
                    out=sinb[cpar][:, :], in_=sinf[:, g * 512:(g + 1) * 512]),
                    inc16=f"d_c{cpar}")
                tk(f"cosb{cpar}").wrote(tc2)

                for kvh in range(4):
                    if cfg.get("skip_kv"):
                        break
                    kp = (g * 4 + kvh) % 2
                    prog.wait_all("tensor", tk(f"xnfm{gpar}").raw())
                    prog.wait_all("tensor", tk(f"pk{kp}").war())
                    prog.wait("tensor", t_wkv)
                    for c in range(NCH):
                        tmm = prog.op("tensor", lambda e, c=c, kvh=kvh, kp=kp, fmB=fmB:
                                      e.matmul(
                            pk[kp][:, :],
                            wkb[:, 512 * c + 128 * kvh:512 * c + 128 * kvh + 128],
                            fmB[:, 512 * c:512 * c + 512],
                            start=(c == 0), stop=(c == NCH - 1)),
                            inc="pe" if c == NCH - 1 else None)
                    tk(f"xnfm{gpar}").read(tmm)
                    tk(f"pk{kp}").wrote(tmm)
                    prog.wait("vector", tmm)
                    prog.wait_all("vector", tk(f"cosb{cpar}").raw())
                    prog.wait_all("vector", tk(f"krop{kp}").war())
                    prog.wait_all("vector", tk("ropetmp").war())
                    cb, sn, kr = cosb[cpar], sinb[cpar], krop[kp]
                    pkk = pk[kp]
                    prp = prope[kp]
                    prog.wait_all("vector", tk(f"prope{kp}").war())
                    prog.op("vector", lambda e, pkk=pkk, cb=cb: e.tensor_tensor(
                        tmpA[0:64, :], pkk[0:64, :], cb[:, :], AluOpType.mult))
                    t2m = prog.op("vector", lambda e, pkk=pkk, sn=sn, prp=prp:
                                  e.tensor_tensor(
                        prp[64:128, :], pkk[64:128, :], sn[:, :], AluOpType.mult),
                        inc="dv")
                    prog.wait("vector", t2m)
                    prog.op("vector", lambda e, kr=kr, prp=prp: e.tensor_tensor(
                        kr[0:64, :], tmpA[0:64, :], prp[64:128, :],
                        AluOpType.subtract))
                    prog.op("vector", lambda e, pkk=pkk, cb=cb: e.tensor_tensor(
                        tmpA[64:128, :], pkk[64:128, :], cb[:, :], AluOpType.mult))
                    t4m = prog.op("vector", lambda e, pkk=pkk, sn=sn, prp=prp:
                                  e.tensor_tensor(
                        prp[0:64, :], pkk[0:64, :], sn[:, :], AluOpType.mult),
                        inc="dv")
                    prog.wait("vector", t4m)
                    trp = prog.op("vector", lambda e, kr=kr, prp=prp: e.tensor_tensor(
                        kr[64:128, :], tmpA[64:128, :], prp[0:64, :], AluOpType.add),
                        inc="dv")
                    tk("ropetmp").wrote(trp)
                    tk(f"prope{kp}").wrote(trp)
                    tk(f"pk{kp}").read(trp)
                    tk(f"cosb{cpar}").read(trp)
                    tk(f"krop{kp}").wrote(trp)
                    prog.wait("scalar", trp)
                    tko = prog.op("scalar", lambda e, kvh=kvh, g=g, kr=kr: e.dma_start(
                        out=k_d[kvh][:, g * 512:(g + 1) * 512], in_=kr[:, :]),
                        inc16=f"d_ko{kp}")
                    tk(f"krop{kp}").read(tko)

                    if cfg.get("skip_v"):
                        continue
                    prog.wait_all("tensor", tk("pv").war())
                    for c in range(NCH):
                        tvm = prog.op("tensor", lambda e, c=c, kvh=kvh, fmB=fmB:
                                      e.matmul(
                            pv[:, :],
                            wvb[:, 512 * c + 128 * kvh:512 * c + 128 * kvh + 128],
                            fmB[:, 512 * c:512 * c + 512],
                            start=(c == 0), stop=(c == NCH - 1)),
                            inc="pe" if c == NCH - 1 else None)
                    tk(f"xnfm{gpar}").read(tvm)
                    tk("pv").wrote(tvm)
                    prog.wait("vector", tvm)
                    prog.wait_all("vector", tk("vtmp").war())
                    tve = prog.op("vector", lambda e: e.tensor_copy(
                        vtmp[:, :], pv[:, :]), inc="dv")
                    tk("pv").read(tve)
                    tk("vtmp").wrote(tve)
                    prog.wait("tensor", tve)
                    prog.wait_all("tensor", tk("ptv").war())
                    for t in range(4):
                        ttv = prog.op("tensor", lambda e, t=t: e.transpose(
                            ptv[:, 128 * t:128 * t + 128], vtmp[:, 128 * t:128 * t + 128],
                            idb[:, :]), inc="pe" if t == 3 else None)
                    tk("vtmp").read(ttv)
                    tk("ptv").wrote(ttv)
                    vp = (g * 4 + kvh) % 2
                    vh = vhat[vp]
                    prog.wait("vector", ttv)
                    prog.wait_all("vector", tk(f"vhat{vp}").war())
                    out_ap = strided(vh, 0, [(129, 4), (1, 128)])
                    in_ap = strided(ptv, 0, [(128, 4), (1, 128)])
                    tvh = prog.op("vector", lambda e, out_ap=out_ap, in_ap=in_ap:
                                  e.tensor_copy(out_ap, in_ap), inc="dv")
                    tk("ptv").read(tvh)
                    tk(f"vhat{vp}").wrote(tvh)
                    prog.wait("scalar", tvh)
                    tvo = prog.op("scalar", lambda e, kvh=kvh, g=g, vh=vh: e.dma_start(
                        out=v_d[kvh][:, g * 516:(g + 1) * 516], in_=vh[:, :]),
                        inc16=f"d_vo{vp}")
                    tk(f"vhat{vp}").read(tvo)

        prog.barrier("b_p1a", dma_sems=[
            "d_x0", "d_x1", "d_c0", "d_c1", "d_wk0", "d_wk1",
            "d_ko0", "d_ko1", "d_vo0", "d_vo1", "d_no0", "d_no1"])

        if True:
            A.reset(m_prep)
            xno = A.alloc(128, NCH * NO, BF16)
            wqt = [A.alloc(128, NCH * 128, BF16) for i in range(2)]
            qrop = [A.alloc(128, 512, BF16) for i in range(2)]
            cosoB = A.alloc(64, NO, F32)
            sinoB = A.alloc(64, NO, F32)
            tmpA2 = A.alloc(128, 512, F32)
            tmpB2 = A.alloc(128, 512, F32)
            pq = [paf[0][:, 0:512], paf[0][:, 512:1024]]
            prope2 = paf[1][:, 0:512]

            prog.op("sync", lambda e: e.dma_start(out=cosoB[:, :], in_=coso),
                    inc16="d_qc")
            prog.op("sync", lambda e: e.dma_start(out=sinoB[:, :], in_=sino),
                    inc16="d_qc")
            for c in range(NCH):
                prog.op("sync", lambda e, c=c: e.dma_start(
                    out=xno[:, c * NO:(c + 1) * NO], in_=xn_d[c][:, :]), inc16="d_qx")
            t_qx = prog.now("d_qx")

            pump_prep(max(0, 16 - pump_prep.i))
            prog.wait("sync", t_prep_wq[0])
            prog.wait("sync", t_prep_wq[1])
            for h in range(16):
                pump_prep(3)
                hp = h % 2
                prog.wait_all("sync", tk(f"wqt{hp}").war())
                twq = prog.op("sync", lambda e, h=h, hp=hp: e.dma_start(
                    out=wqt[hp][:, :], in_=wq_b[h]), inc16=f"d_wq{hp}")
                tk(f"wqt{hp}").wrote(twq)
                for tg in range(NOG):
                    qp = (h * NOG + tg) % 2
                    prog.wait("tensor", twq)
                    prog.wait("tensor", t_qx)
                    prog.wait_all("tensor", tk(f"pq{qp}").war())
                    for c in range(NCH):
                        tqm = prog.op("tensor", lambda e, c=c, tg=tg, qp=qp, hp=hp:
                                      e.matmul(
                            pq[qp][:, :], wqt[hp][:, c * 128:(c + 1) * 128],
                            xno[:, c * NO + tg * 512: c * NO + tg * 512 + 512],
                            start=(c == 0), stop=(c == NCH - 1)),
                            inc="pe" if c == NCH - 1 else None)
                    tk(f"wqt{hp}").read(tqm)
                    tk(f"pq{qp}").wrote(tqm)
                    prog.wait("vector", tqm)
                    prog.wait(eng="vector", tok=("d_qc", 32))
                    prog.wait_all("vector", tk(f"qrop{qp}").war())
                    prog.wait_all("vector", tk("ropetmp2").war())
                    qr = qrop[qp]
                    pqq = pq[qp]
                    cslc = cosoB[:, tg * 512:(tg + 1) * 512]
                    sslc = sinoB[:, tg * 512:(tg + 1) * 512]
                    prog.op("vector", lambda e, pqq=pqq, cslc=cslc: e.tensor_tensor(
                        tmpA2[0:64, :], pqq[0:64, :], cslc, AluOpType.mult))
                    t2m = prog.op("vector", lambda e, pqq=pqq, sslc=sslc:
                                  e.tensor_tensor(
                        prope2[64:128, :], pqq[64:128, :], sslc, AluOpType.mult),
                        inc="dv")
                    prog.wait("vector", t2m)
                    prog.op("vector", lambda e, qr=qr: e.tensor_tensor(
                        qr[0:64, :], tmpA2[0:64, :], prope2[64:128, :],
                        AluOpType.subtract))
                    prog.op("vector", lambda e, pqq=pqq, cslc=cslc: e.tensor_tensor(
                        tmpA2[64:128, :], pqq[64:128, :], cslc, AluOpType.mult))
                    t4m = prog.op("vector", lambda e, pqq=pqq, sslc=sslc:
                                  e.tensor_tensor(
                        prope2[0:64, :], pqq[0:64, :], sslc, AluOpType.mult), inc="dv")
                    prog.wait("vector", t4m)
                    trp = prog.op("vector", lambda e, qr=qr: e.tensor_tensor(
                        qr[64:128, :], tmpA2[64:128, :], prope2[0:64, :],
                        AluOpType.add), inc="dv")
                    tk("ropetmp2").wrote(trp)
                    tk(f"pq{qp}").read(trp)
                    tk(f"qrop{qp}").wrote(trp)
                    prog.wait("scalar", trp)
                    tqo = prog.op("scalar", lambda e, h=h, tg=tg, qr=qr: e.dma_start(
                        out=q_d[h][:, tg * 512:(tg + 1) * 512], in_=qr[:, :]),
                        inc16=f"d_qo{qp}")
                    tk(f"qrop{qp}").read(tqo)

        prog.barrier("b_p1c", dma_sems=["d_qc", "d_qx", "d_wq0", "d_wq1",
                                        "d_qo0", "d_qo1"])
        if debug:
            for src, name in [(k_d, "d_k"), (v_d, "d_v"), (q_d, "d_q"),
                              (xn_d, "d_xn")]:
                prog.op("sync", lambda e, src=src, name=name: e.dma_start(
                    out=dbg[name], in_=src), inc16="d_dbg")

        if True:
            A.reset(m_prep)
            ksb = [A.alloc(128, 2 * S, BF16) for i in range(2)]
            vsb = [A.alloc(128, 2 * VW, BF16) for i in range(2)]
            msb = A.alloc(128, 16 * 512, BF16)
            qt = [A.alloc(128, 512, BF16) for i in range(3)]
            esb = [A.alloc(128, 1024, BF16) for i in range(2)]
            atn = [A.alloc(128, 128, BF16) for i in range(4)]
            atn4 = [A.alloc(128, 512, BF16) for i in range(2)]
            rcp = A.alloc(128, 4, F32)
            psS = [paf[0], paf[1]]
            psAV = [paf[2][:, 0:512], paf[2][:, 512:1024],
                    paf[3][:, 0:512], paf[3][:, 512:1024]]
            psTV = [pab[2][:, 0:1024], pab[2][:, 1024:2048],
                    pab[3][:, 0:1024], pab[3][:, 1024:2048]]

            for z in range(16):
                t_m = prog.op("sync", lambda e, z=z: e.dma_start(
                    out=msb[:, z * 512:(z + 1) * 512], in_=maskt[z]), inc16="d_m")
            t_mask = t_m

            t_kv_tok = [None, None]
            for pp in range(2):
                for j in range(2):
                    prog.op("sync", lambda e, pp=pp, j=j: e.dma_start(
                        out=ksb[pp][:, j * S:(j + 1) * S], in_=k_d[2 * pp + j][:, :]),
                        inc16=f"d_kv{pp}")
                    prog.op("sync", lambda e, pp=pp, j=j: e.dma_start(
                        out=vsb[pp][:, j * VW:(j + 1) * VW], in_=v_d[2 * pp + j][:, :]),
                        inc16=f"d_kv{pp}")
                t_kv_tok[pp] = prog.now(f"d_kv{pp}")
            for p in range(2):
                t_kv = t_kv_tok[p]
                ksb_p = ksb[p]
                vsb_p = vsb[p]

                units = [(g, h) for g in range(NOG) for h in range(8)]
                if p == 0:
                    prog.wait_all("sync", tk("qt0").war())
                    tq0 = prog.op("sync", lambda e, p=p: e.dma_start(
                        out=qt[0][:, :], in_=q_d[8 * p][:, 0:512]), inc16="d_q0")
                    tk("qt0").wrote(tq0)
                    prog.q_tok = tq0
                    prog.q_next = 1
                for ui, (g, h) in enumerate(units):
                    u = p * NOG * 8 + g * 8 + h
                    up = u % 3
                    gh = 8 * p + h
                    kvloc = h // 4
                    NKB = 16 * g + 16
                    pump_prep(3)
                    tql = prog.q_tok
                    if ui + 1 < len(units):
                        g2, h2 = units[ui + 1]
                        gh2, up2 = 8 * p + h2, (u + 1) % 3
                    elif p == 0:
                        g2, h2 = units[0]
                        gh2, up2 = 8 + h2, (u + 1) % 3
                    else:
                        g2 = None
                    if g2 is not None:
                        prog.wait_all("sync", tk(f"qt{up2}").war())
                        tq2 = prog.op("sync", lambda e, gh2=gh2, g2=g2, up2=up2:
                                      e.dma_start(
                            out=qt[up2][:, :],
                            in_=q_d[gh2][:, g2 * 512:(g2 + 1) * 512]),
                            inc16=f"d_q{up2}")
                        tk(f"qt{up2}").wrote(tq2)
                        prog.q_tok = tq2
                    prog.wait("tensor", tql)
                    prog.wait("tensor", t_kv)
                    avlast = None
                    NKB2 = NKB // 2

                    def col0_of(kb):
                        z = kb - 16 * g
                        return 0 if z < 4 else (z // 4) * 128

                    def emit_S(kb2):
                        sp = kb2 % 2
                        prog.wait_all("tensor", tk(f"psS{sp}").war())
                        for j in range(2):
                            kb = 2 * kb2 + j
                            c0 = col0_of(kb)
                            tsm = prog.op("tensor", lambda e, sp=sp, j=j, kb=kb, c0=c0,
                                          kvloc=kvloc, up=up, ksb_p=ksb_p: e.matmul(
                                psS[sp][:, j * 512 + c0:(j + 1) * 512],
                                ksb_p[:, kvloc * S + kb * 128:
                                      kvloc * S + kb * 128 + 128],
                                qt[up][:, c0:512], start=True, stop=True),
                                inc="pe" if j == 1 else None)
                        tk(f"qt{up}").read(tsm)
                        tk(f"psS{sp}").wrote(tsm)
                        return tsm

                    s_tok = {0: emit_S(0)}
                    for kb2 in range(NKB2):
                        sp = kb2 % 2
                        tsm = s_tok.pop(kb2)
                        prog.wait("scalar", tsm)
                        prog.wait_all("scalar", tk(f"esb{sp}").war())
                        prog.wait("scalar", t_cst)
                        extoks = [None, None]
                        mtoks = [None, None]
                        for j in range(2):
                            kb = 2 * kb2 + j
                            c0 = col0_of(kb)
                            tex = prog.op("scalar", lambda e, sp=sp, j=j, c0=c0:
                                          e.activation(
                                esb[sp][:, j * 512 + c0:(j + 1) * 512],
                                psS[sp][:, j * 512 + c0:(j + 1) * 512], AF.Exp,
                                bias=c_shift[:, :], scale=c_scale[:, :]), inc="ac")
                            extoks[j] = tex
                            z = kb - 16 * g
                            if z >= 0:
                                prog.wait("vector", tex)
                                prog.wait("vector", t_mask)
                                tmk = prog.op("vector", lambda e, sp=sp, j=j, z=z,
                                              c0=c0: e.tensor_tensor(
                                    esb[sp][:, j * 512 + c0:(j + 1) * 512],
                                    esb[sp][:, j * 512 + c0:(j + 1) * 512],
                                    msb[:, z * 512 + c0:(z + 1) * 512],
                                    AluOpType.mult), inc="dv")
                                mtoks[j] = tmk
                        tk(f"psS{sp}").read(extoks[1])
                        tk(f"esb{sp}").wrote(extoks[1])
                        for j in range(2):
                            if mtoks[j] is not None:
                                tk(f"esb{sp}").w.append(mtoks[j])
                        if kb2 + 1 < NKB2:
                            s_tok[kb2 + 1] = emit_S(kb2 + 1)
                        for j in range(2):
                            kb = 2 * kb2 + j
                            z = kb - 16 * g
                            qqmin = 0 if z < 4 else z // 4
                            prog.wait("tensor", extoks[j])
                            if mtoks[j] is not None:
                                prog.wait("tensor", mtoks[j])
                            for qq in range(qqmin, 4):
                                if kb == 0:
                                    prog.wait_all("tensor", tk(f"psAV{qq}").war())
                                last_kb = 16 * g + 4 * qq + 3
                                tav = prog.op("tensor", lambda e, sp=sp, j=j, qq=qq,
                                              kvloc=kvloc, kb=kb, last_kb=last_kb,
                                              vsb_p=vsb_p: e.matmul(
                                    psAV[qq][:, 0:129],
                                    esb[sp][:, j * 512 + qq * 128:
                                            j * 512 + qq * 128 + 128],
                                    vsb_p[:, kvloc * VW + kb * 129:
                                          kvloc * VW + kb * 129 + 129],
                                    start=(kb == 0), stop=(kb == last_kb)),
                                    inc="pe")
                                tk(f"psAV{qq}").wrote(tav)
                                tk(f"esb{sp}").read(tav)
                                avlast = tav
                    prog.wait("vector", avlast)
                    prog.wait_all("vector", tk("rcp").war())
                    for qq in range(4):
                        trr = prog.op("vector", lambda e, qq=qq: e.reciprocal(
                            rcp[:, qq:qq + 1], psAV[qq][:, 128:129]),
                            inc="dv" if qq == 3 else None)
                    tk("rcp").wrote(trr)
                    prog.wait("vector", trr)
                    a4 = atn4[u % 2]
                    prog.wait_all("vector", tk(f"atn4_{u % 2}").war())
                    ntoks = []
                    for qq in range(4):
                        prog.wait_all("vector", tk(f"atn{qq}").war())
                        tns = prog.op("vector", lambda e, qq=qq: e.tensor_scalar(
                            atn[qq][:, :], psAV[qq][:, 0:128], rcp[:, qq:qq + 1],
                            None, AluOpType.mult), inc="dv")
                        tk(f"psAV{qq}").read(tns)
                        tk("rcp").read(tns)
                        tk(f"atn{qq}").wrote(tns)
                        ntoks.append(tns)
                    for qq in range(4):
                        prog.wait("tensor", ntoks[qq])
                        prog.wait("tensor", t_id)
                        ttr = prog.op("tensor", lambda e, qq=qq: e.transpose(
                            psTV[qq][:, 0:128], atn[qq][:, :], idb[:, :]),
                            inc="pe")
                        tk(f"psAV{qq}").wrote(ttr)
                        tk(f"atn{qq}").read(ttr)
                        prog.wait("vector", ttr)
                        tev = prog.op("vector", lambda e, qq=qq, a4=a4:
                                      e.tensor_copy(
                            a4[:, qq * 128:(qq + 1) * 128], psTV[qq][:, 0:128]),
                            inc="dv")
                        tk(f"psAV{qq}").read(tev)
                        tk(f"atn4_{u % 2}").wrote(tev)
                    prog.wait("sync", tev)
                    tad = prog.op("sync", lambda e, g=g, gh=gh, a4=a4:
                                  e.dma_start(
                        out=attn_d[gh * 128:(gh + 1) * 128,
                                   g * 512:(g + 1) * 512],
                        in_=a4[:, :]), inc16=f"d_at{u % 2}")
                    tk(f"atn4_{u % 2}").read(tad)


        prog.barrier("b_p2", dma_sems=["d_q0", "d_q1", "d_kv0", "d_kv1", "d_m",
                                       "d_at0", "d_at1", "d_at2", "d_at3"])
        if debug:
            prog.op("sync", lambda e: e.dma_start(out=dbg["d_attn"], in_=attn_d),
                    inc16="d_dbg")

        pump_prep(len(prep_jobs))

        if True:
            A.reset(m_persist)
            xnoB = A.alloc(128, NCH * 512, BF16)
            act = A.alloc(128, NIC * 512, BF16)
            ract = A.alloc(128, NRIC * 512, BF16)
            acc = A.alloc(128, 4 * D, F32)
            WGN, WDN, WON = 3, 5, 5
            wgt = [A.alloc(128, NCH * 128, BF16) for i in range(WGN)]
            wut = [A.alloc(128, NCH * 128, BF16) for i in range(WGN)]
            wdt = [A.alloc(128, 512, BF16) for i in range(WDN)]
            wot = [A.alloc(128, 512, BF16) for i in range(WON)]
            afm = A.alloc(128, NCH * 512, BF16)
            xob = A.alloc(128, D, F32)
            outb = [A.alloc(128, 512, F32) for i in range(2)]
            sgb = A.alloc(128, 512, F32)
            sgb2 = A.alloc(128, 512, F32)
            rwb = A.alloc(128, NCH * 8, BF16)
            rwf = A.alloc(128, 8, F32)
            lgs = A.alloc(128, 8, F32)
            m1 = A.alloc(128, 1, F32)
            m2 = A.alloc(128, 1, F32)
            eqv = A.alloc(128, 8, F32)
            evv = A.alloc(128, 1, F32)
            svv = A.alloc(128, 1, F32)
            comb = A.alloc(128, 4, F32)
            psG = paf[0][:, 0:512]
            psU = paf[0][:, 512:1024]
            psD = [paf[1][:, 0:512], paf[1][:, 512:1024],
                   paf[2][:, 0:512], paf[2][:, 512:1024]]
            psR = paf[3][:, 512:1024]

            for _e in ("sync", "tensor", "vector", "scalar"):
                prog.wait(_e, prog.now("gp"))
                prog.wait(_e, prog.now("g_out0"))
                prog.wait(_e, prog.now("g_out1"))
                prog.wait(_e, prog.now("g_in0"))
                prog.wait(_e, prog.now("g_in1"))

            for c in range(NCH):
                prog.wait_all("sync", tk("rwf").war())
                tin = prog.op("sync", lambda e, c=c: e.dma_start(
                    out=rwf[:, :], in_=rw[c * 128:(c + 1) * 128, :]), inc16="d_rw")
                tk("rwf").wrote(tin)
                prog.wait("vector", tin)
                tcv = prog.op("vector", lambda e, c=c: e.tensor_copy(
                    rwb[:, c * 8:(c + 1) * 8], rwf[:, :]), inc="dv")
                tk("rwf").read(tcv)
            t_rwb = prog.now("dv")

            for blk in range(NOG):
                prog.wait_all("sync", tk("xnoB").war())
                for c in range(NCH):
                    txn = prog.op("sync", lambda e, c=c, blk=blk: e.dma_start(
                        out=xnoB[:, c * 512:(c + 1) * 512],
                        in_=xn_d[c][:, blk * 512:(blk + 1) * 512]), inc16="d_xb")
                tk("xnoB").wrote(txn)
                t_xn = prog.now("d_xb")

                for t4 in range(4):
                    prog.wait("tensor", t_xn)
                    prog.wait("tensor", t_rwb)
                    prog.wait_all("tensor", tk("psR").war())
                    for c in range(NCH):
                        trm = prog.op("tensor", lambda e, c=c, t4=t4: e.matmul(
                            psR[:, 0:8],
                            xnoB[:, c * 512 + t4 * 128: c * 512 + t4 * 128 + 128],
                            rwb[:, c * 8:(c + 1) * 8],
                            start=(c == 0), stop=(c == NCH - 1)),
                            inc="pe" if c == NCH - 1 else None)
                    tk("psR").wrote(trm)
                    tk("xnoB").read(trm)
                    prog.wait("vector", trm)
                    prog.wait_all("vector", tk("router").war())
                    tcp = prog.op("vector", lambda e: e.tensor_copy(
                        lgs[:, :], psR[:, 0:8]), inc="dv")
                    tk("psR").read(tcp)
                    prog.wait("vector", tcp)
                    tm1 = prog.op("vector", lambda e: e.reduce_max(
                        m1[:, :], lgs[:, :], axis=AX), inc="dv")
                    prog.wait("vector", tm1)
                    teq = prog.op("vector", lambda e: e.tensor_scalar(
                        eqv[:, :], lgs[:, :], m1[:, :], None, AluOpType.is_equal),
                        inc="dv")
                    prog.wait("vector", teq)
                    tms = prog.op("vector", lambda e: e.scalar_tensor_tensor(
                        eqv[:, :], eqv[:, :], -1e30, lgs[:, :],
                        AluOpType.mult, AluOpType.add), inc="dv")
                    prog.wait("vector", tms)
                    tm2 = prog.op("vector", lambda e: e.reduce_max(
                        m2[:, :], eqv[:, :], axis=AX), inc="dv")
                    prog.wait("vector", tm2)
                    tdd = prog.op("vector", lambda e: e.tensor_tensor(
                        m2[:, :], m2[:, :], m1[:, :], AluOpType.subtract), inc="dv")
                    prog.wait("scalar", tdd)
                    tev = prog.op("scalar", lambda e: e.activation(
                        evv[:, :], m2[:, :], AF.Exp), inc="ac")
                    prog.wait("vector", tev)
                    tsv = prog.op("vector", lambda e: e.tensor_scalar(
                        svv[:, :], evv[:, :], 1.0, None, AluOpType.add), inc="dv")
                    prog.wait("vector", tsv)
                    trc = prog.op("vector", lambda e: e.reciprocal(
                        svv[:, :], svv[:, :]), inc="dv")
                    prog.wait("vector", trc)
                    tcb = prog.op("vector", lambda e, t4=t4: e.scalar_tensor_tensor(
                        comb[:, t4:t4 + 1], evv[:, :], 1.0, svv[:, :],
                        AluOpType.add, AluOpType.mult), inc="dv")
                    tk("router").wrote(tcb)
                    tk(f"comb{t4}").wrote(tcb)

                def gateup(nic, gp_src, up_src, act_dst, actname):
                    prog.wait_all("vector", tk(actname).war())
                    for ic in range(nic):
                        wp = ic % WGN
                        wgt_b, wut_b = wgt[wp], wut[wp]
                        prog.wait_all("sync", tk(f"wgt{wp}").war())
                        prog.op("sync", lambda e, ic=ic, gp_src=gp_src, wgt_b=wgt_b:
                                e.dma_start(out=wgt_b, in_=gp_src[ic]),
                            inc16=f"d_g{wp}")
                        prog.op("sync", lambda e, ic=ic, up_src=up_src, wut_b=wut_b:
                                e.dma_start(out=wut_b, in_=up_src[ic]),
                            inc16=f"d_g{wp}")
                        tld = prog.now(f"d_g{wp}")
                        tk(f"wgt{wp}").wrote(tld)
                        prog.wait("tensor", tld)
                        prog.wait("tensor", t_xn)
                        prog.wait_all("tensor", tk("psG").war())
                        for c in range(NCH):
                            tgm = prog.op("tensor", lambda e, c=c, wgt_b=wgt_b:
                                          e.matmul(
                                psG[:, :], wgt_b[:, c * 128:(c + 1) * 128],
                                xnoB[:, c * 512:(c + 1) * 512],
                                start=(c == 0), stop=(c == NCH - 1)),
                                inc="pe" if c == NCH - 1 else None)
                        tk("psG").wrote(tgm)
                        prog.wait_all("tensor", tk("psU").war())
                        for c in range(NCH):
                            tum = prog.op("tensor", lambda e, c=c, wut_b=wut_b:
                                          e.matmul(
                                psU[:, :], wut_b[:, c * 128:(c + 1) * 128],
                                xnoB[:, c * 512:(c + 1) * 512],
                                start=(c == 0), stop=(c == NCH - 1)),
                                inc="pe" if c == NCH - 1 else None)
                        tk("psU").wrote(tum)
                        tk(f"wgt{wp}").read(tum)
                        prog.wait("scalar", tgm)
                        prog.wait_all("scalar", tk("sgb").war())
                        tsl = prog.op("scalar", lambda e: e.activation(
                            sgb[:, :], psG[:, :], AF.Silu), inc="ac")
                        tk("psG").read(tsl)
                        tk("sgb").wrote(tsl)
                        prog.wait("vector", tsl)
                        prog.wait("vector", tum)
                        tac = prog.op("vector", lambda e, ic=ic, act_dst=act_dst:
                                      e.tensor_tensor(
                            act_dst[:, ic * 512:(ic + 1) * 512], sgb[:, :], psU[:, :],
                            AluOpType.mult), inc="dv")
                        tk("psU").read(tac)
                        tk("sgb").read(tac)
                        tk(actname).wrote(tac)
                    return tac

                t_act = gateup(NIC, wg_p, wu_p, act, "act")
                t_ract = gateup(NRIC, rg_p, ru_p, ract, "ract")

                for od in range(4):
                    for ic in range(NIC):
                        wp = ic % WDN
                        wdt_b = wdt[wp]
                        prog.wait_all("scalar", tk(f"wdt{wp}").war())
                        tdl = prog.op("scalar", lambda e, ic=ic, od=od, wdt_b=wdt_b:
                                      e.dma_start(
                            out=wdt_b[:, :],
                            in_=wd_b[ic * 128:(ic + 1) * 128, od * 512:(od + 1) * 512]),
                            inc16=f"d_d{wp}")
                        tk(f"wdt{wp}").wrote(tdl)
                        prog.wait("tensor", tdl)
                        if ic == 0:
                            prog.wait("tensor", t_act)
                            for t4 in range(4):
                                prog.wait_all("tensor", tk(f"psD{t4}").war())
                        for t4 in range(4):
                            tdm = prog.op("tensor", lambda e, ic=ic, t4=t4, wdt_b=wdt_b:
                                          e.matmul(
                                psD[t4][:, :],
                                act[:, ic * 512 + t4 * 128: ic * 512 + t4 * 128 + 128],
                                wdt_b[:, :],
                                start=(ic == 0), stop=(ic == NIC - 1)),
                                inc="pe" if t4 == 3 else None)
                            tk(f"psD{t4}").wrote(tdm)
                            tk("act").read(tdm)
                        tk(f"wdt{wp}").read(tdm)
                    prog.wait("vector", tdm)
                    prog.wait_all("vector", tk("acc").war())
                    for t4 in range(4):
                        tevc = prog.op("vector", lambda e, t4=t4, od=od: e.tensor_copy(
                            acc[:, t4 * D + od * 512: t4 * D + od * 512 + 512],
                            psD[t4][:, :]), inc="dv")
                        tk(f"psD{t4}").read(tevc)
                    tk("acc").wrote(tevc)

                for od in range(4):
                    for ic in range(NRIC):
                        wp = ic % WDN
                        wdt_b = wdt[wp]
                        prog.wait_all("scalar", tk(f"wdt{wp}").war())
                        tdl = prog.op("scalar", lambda e, ic=ic, od=od, wdt_b=wdt_b:
                                      e.dma_start(
                            out=wdt_b[:, :],
                            in_=rd_b[ic * 128:(ic + 1) * 128, od * 512:(od + 1) * 512]),
                            inc16=f"d_d{wp}")
                        tk(f"wdt{wp}").wrote(tdl)
                        prog.wait("tensor", tdl)
                        if ic == 0:
                            prog.wait("tensor", t_ract)
                            for t4 in range(4):
                                prog.wait_all("tensor", tk(f"psD{t4}").war())
                        for t4 in range(4):
                            tdm = prog.op("tensor", lambda e, ic=ic, t4=t4, wdt_b=wdt_b:
                                          e.matmul(
                                psD[t4][:, :],
                                ract[:, ic * 512 + t4 * 128: ic * 512 + t4 * 128 + 128],
                                wdt_b[:, :],
                                start=(ic == 0), stop=(ic == NRIC - 1)),
                                inc="pe" if t4 == 3 else None)
                            tk(f"psD{t4}").wrote(tdm)
                            tk("ract").read(tdm)
                        tk(f"wdt{wp}").read(tdm)
                    prog.wait("vector", tdm)
                    for t4 in range(4):
                        prog.wait_all("vector", tk(f"comb{t4}").raw())
                        prog.wait_all("vector", tk("sgb2").war())
                        tsc = prog.op("vector", lambda e, t4=t4: e.tensor_scalar(
                            sgb2[:, :], psD[t4][:, :], comb[:, t4:t4 + 1], None,
                            AluOpType.mult), inc="dv")
                        tk(f"psD{t4}").read(tsc)
                        tk("sgb2").wrote(tsc)
                        prog.wait("vector", tsc)
                        prog.wait_all("vector", tk("acc").raw())
                        tadd = prog.op("vector", lambda e, t4=t4, od=od:
                                       e.tensor_tensor(
                            acc[:, t4 * D + od * 512: t4 * D + od * 512 + 512],
                            acc[:, t4 * D + od * 512: t4 * D + od * 512 + 512],
                            sgb2[:, :], AluOpType.add), inc="dv")
                        tk("sgb2").read(tadd)
                        tk("acc").wrote(tadd)

                prog.wait_all("sync", tk("afm").war())
                for c in range(NCH):
                    tal = prog.op("sync", lambda e, c=c, blk=blk: e.dma_start(
                        out=afm[:, c * 512:(c + 1) * 512],
                        in_=attn_d[c * 128:(c + 1) * 128,
                                   blk * 512:(blk + 1) * 512]), inc16="d_am")
                tk("afm").wrote(tal)
                t_afm = prog.now("d_am")

                for od in range(4):
                    for c in range(NCH):
                        wp = c % WON
                        wot_b = wot[wp]
                        prog.wait_all("scalar", tk(f"wot{wp}").war())
                        twl = prog.op("scalar", lambda e, c=c, od=od, wot_b=wot_b:
                                      e.dma_start(
                            out=wot_b[:, :],
                            in_=wo_b[c * 128:(c + 1) * 128, od * 512:(od + 1) * 512]),
                            inc16=f"d_o{wp}")
                        tk(f"wot{wp}").wrote(twl)
                        prog.wait("tensor", twl)
                        if c == 0:
                            prog.wait("tensor", t_afm)
                            for t4 in range(4):
                                prog.wait_all("tensor", tk(f"psD{t4}").war())
                        for t4 in range(4):
                            twm = prog.op("tensor", lambda e, c=c, t4=t4, wot_b=wot_b:
                                          e.matmul(
                                psD[t4][:, :],
                                afm[:, c * 512 + t4 * 128:
                                    c * 512 + t4 * 128 + 128],
                                wot_b[:, :],
                                start=(c == 0), stop=(c == NCH - 1)),
                                inc="pe" if t4 == 3 else None)
                            tk(f"psD{t4}").wrote(twm)
                            tk("afm").read(twm)
                        tk(f"wot{wp}").read(twm)
                    prog.wait("vector", twm)
                    prog.wait_all("vector", tk("acc").raw())
                    for t4 in range(4):
                        tadd = prog.op("vector", lambda e, t4=t4, od=od:
                                       e.tensor_tensor(
                            acc[:, t4 * D + od * 512: t4 * D + od * 512 + 512],
                            acc[:, t4 * D + od * 512: t4 * D + od * 512 + 512],
                            psD[t4][:, :], AluOpType.add), inc="dv")
                        tk(f"psD{t4}").read(tadd)
                    tk("acc").wrote(tadd)

                for t4 in range(4):
                    prog.wait_all("sync", tk("xob").war())
                    txl = prog.op("sync", lambda e, t4=t4, blk=blk: e.dma_start(
                        out=xob[:, :],
                        in_=xo[blk * 512 + t4 * 128: blk * 512 + t4 * 128 + 128, :]),
                        inc16="d_xo")
                    tk("xob").wrote(txl)
                    prog.wait("vector", txl)
                    prog.wait_all("vector", tk("acc").raw())
                    for od in range(4):
                        op_ = od % 2
                        prog.wait_all("vector", tk(f"outb{op_}").war())
                        tfin = prog.op("vector", lambda e, t4=t4, od=od, op_=op_:
                                       e.tensor_tensor(
                            outb[op_][:, :],
                            acc[:, t4 * D + od * 512: t4 * D + od * 512 + 512],
                            xob[:, od * 512:(od + 1) * 512],
                            AluOpType.add), inc="dv")
                        tk("xob").read(tfin)
                        tk("acc").read(tfin)
                        tk(f"outb{op_}").wrote(tfin)
                        prog.wait("scalar", tfin)
                        tyo = prog.op("scalar", lambda e, t4=t4, blk=blk, od=od,
                                      op_=op_: e.dma_start(
                            out=y[blk * 512 + t4 * 128: blk * 512 + t4 * 128 + 128,
                                  od * 512:(od + 1) * 512],
                            in_=outb[op_][:, :]), inc16=f"d_y{op_}")
                        tk(f"outb{op_}").read(tyo)

        prog.wait("sync", prog.now("d_y0"))
        prog.wait("sync", prog.now("d_y1"))
        if debug:
            prog.wait("sync", prog.now("d_dbg"))

        block = octx.enter_context(nc.Block())
        prog.emit(block)

    return nc



def _host_tables(cfg):
    S = cfg["S"]
    t = np.arange(S, dtype=np.float32)
    j = np.arange(0, 128, 2, dtype=np.float32)
    inv = (1.0 / (10000.0 ** (j / 128.0))).astype(np.float32)
    fr = t[:, None] * inv[None, :]
    cos = np.cos(fr).astype(np.float32)
    sin = np.sin(fr).astype(np.float32)
    masks = np.zeros((4, 16, 128, 512), dtype=ml_dtypes.bfloat16)
    for r in range(4):
        z = np.arange(16)[:, None, None]
        s = np.arange(128)[None, :, None]
        col = np.arange(512)[None, None, :]
        masks[r] = (128 * z + s <= 4 * col + r).astype(ml_dtypes.bfloat16)
    return cos, sin, masks


def make_in_maps(cfg, x, Wq, Wk, Wv, Wo, router_w, sh_gate, sh_up, sh_down,
                 r_gate, r_up, r_down):
    S = cfg["S"]
    cos, sin, masks = _host_tables(cfg)
    cosT = np.ascontiguousarray(cos.T)
    sinT = np.ascontiguousarray(sin.T)
    ident = np.eye(128, dtype=ml_dtypes.bfloat16)
    com = {
        "wq": np.ascontiguousarray(np.asarray(Wq, np.float32)),
        "wk": np.ascontiguousarray(np.asarray(Wk, np.float32)),
        "wv": np.ascontiguousarray(np.asarray(Wv, np.float32)),
        "wo": np.ascontiguousarray(np.asarray(Wo, np.float32)),
        "wg": np.ascontiguousarray(np.asarray(sh_gate, np.float32).T),
        "wu": np.ascontiguousarray(np.asarray(sh_up, np.float32).T),
        "wd": np.ascontiguousarray(np.asarray(sh_down, np.float32).T),
        "rg": np.ascontiguousarray(np.asarray(r_gate, np.float32).T),
        "ru": np.ascontiguousarray(np.asarray(r_up, np.float32).T),
        "rd": np.ascontiguousarray(np.asarray(r_down, np.float32).T),
        "rw": np.ascontiguousarray(np.asarray(router_w, np.float32).T),
        "cosf": cosT, "sinf": sinT, "ident": ident,
    }
    in_maps = []
    for c in range(8):
        b, r = c // 4, c % 4
        m = dict(com)
        m["xb"] = np.ascontiguousarray(x[b])
        m["xo"] = np.ascontiguousarray(x[b][r::4])
        m["coso"] = np.ascontiguousarray(cosT[:, r::4])
        m["sino"] = np.ascontiguousarray(sinT[:, r::4])
        m["maskt"] = np.ascontiguousarray(masks[r])
        in_maps.append(m)
    return in_maps


def kernel(L0_attn_x, L0_attn_Wq, L0_attn_Wk, L0_attn_Wv, L0_attn_Wo,
           L0_ffn_router_weight, L0_ffn_sh_gate, L0_ffn_sh_up, L0_ffn_sh_down,
           L0_ffn_r_gate, L0_ffn_r_up, L0_ffn_r_down):
    cfg = default_cfg()
    S, D = cfg["S"], cfg["D"]
    x = np.ascontiguousarray(np.asarray(L0_attn_x, dtype=np.float32))
    in_maps = make_in_maps(cfg, x, L0_attn_Wq, L0_attn_Wk, L0_attn_Wv, L0_attn_Wo,
                           L0_ffn_router_weight, L0_ffn_sh_gate, L0_ffn_sh_up,
                           L0_ffn_sh_down, L0_ffn_r_gate, L0_ffn_r_up, L0_ffn_r_down)
    nc = build(cfg)
    res = run_bass_kernel_spmd(nc, in_maps, core_ids=list(range(8)))
    out = np.empty((2, S, D), dtype=np.float32)
    for c in range(8):
        b, r = c // 4, c % 4
        out[b, r::4] = res.results[c]["y"]
    return out


# version 14
# speedup vs baseline: 1.0945x; 1.0945x over previous
import math
import numpy as np
import ml_dtypes
import concourse.bass as bass
import concourse.mybir as mybir
from concourse.bass_utils import run_bass_kernel_spmd
from concourse.alu_op_type import AluOpType
from contextlib import ExitStack

F32 = mybir.dt.float32
BF16 = mybir.dt.bfloat16
AF = mybir.ActivationFunctionType
AX = mybir.AxisListType.X

ENGINES = ("tensor", "vector", "scalar", "gpsimd", "sync")


class Prog:

    def __init__(self, nc, ctx):
        self.nc = nc
        self.ctx = ctx
        self.items = {e: [] for e in ENGINES}
        self.sems = {}
        self.cnt = {}

    def sem(self, name):
        if name not in self.sems:
            self.sems[name] = self.ctx.enter_context(self.nc.semaphore(name))
            self.cnt[name] = 0
        return name

    def op(self, eng, fn, inc=None, inc16=None):
        tok = None
        incs = []
        if inc is not None:
            self.sem(inc)
            self.cnt[inc] += 1
            incs.append((inc, 1))
            tok = (inc, self.cnt[inc])
        if inc16 is not None:
            self.sem(inc16)
            self.cnt[inc16] += 16
            incs.append((inc16, 16))
            tok = (inc16, self.cnt[inc16])
        self.items[eng].append(("op", fn, incs))
        return tok

    def wait(self, eng, tok):
        if tok is None:
            return
        sem, val = tok
        self.sem(sem)
        if val > 0:
            self.items[eng].append(("wait", sem, val))

    def wait_all(self, eng, toks):
        best = {}
        for t in toks:
            if t is None:
                continue
            s, v = t
            best[s] = max(best.get(s, 0), v)
        for s, v in sorted(best.items()):
            self.wait(eng, (s, v))

    def now(self, sem):
        self.sem(sem)
        return (sem, self.cnt[sem])

    def barrier(self, name, dma_sems=(), engines=("tensor", "vector", "scalar")):
        toks = [self.now(s) for s in dma_sems if s in self.cnt]
        for e in engines:
            toks.append(self.op(e, lambda eg: eg.drain(), inc=self.sem(name)))
        for e in ENGINES:
            self.wait_all(e, toks)

    def emit(self, block):
        for eng in ENGINES:
            dec = getattr(block, eng)
            items = self.items[eng]

            def make(items):
                def _(e):
                    for it in items:
                        if it[0] == "wait":
                            e.wait_ge(self.sems[it[1]], it[2])
                        else:
                            inst = it[1](e)
                            for s, k in it[2]:
                                inst.then_inc(self.sems[s], k)
                return _
            dec(make(items))


class Tok:
    def __init__(self):
        self.w = []
        self.r = []

    def wrote(self, tok):
        self.w = [tok] if tok is not None else []
        self.r = []

    def read(self, tok):
        if tok is not None:
            self.r.append(tok)

    def raw(self):
        return list(self.w)

    def war(self):
        return list(self.w) + list(self.r)




def _align_up(x, a):
    return (x + a - 1) & ~(a - 1)


class SbArena:

    def __init__(self, apf, cap_bytes):
        self.apf = apf
        self.apb = apf.bitcast(BF16)
        self.cap = cap_bytes
        self.off = 0
        self.peak = 0

    def alloc(self, parts, cols, dtype):
        sz = cols * (4 if dtype is F32 else 2)
        off = _align_up(self.off, 64)
        self.off = off + sz
        self.peak = max(self.peak, self.off)
        assert self.off <= self.cap, (self.off, self.cap)
        if dtype is F32:
            return self.apf[0:parts, off // 4: off // 4 + cols]
        return self.apb[0:parts, off // 2: off // 2 + cols]

    def mark(self):
        return self.off

    def reset(self, m):
        self.off = m


def strided(slice_ap, extra, dims):
    part = list(slice_ap.ap[0])
    return bass.AP(slice_ap.tensor, slice_ap.offset + extra,
                   [part] + [list(d) for d in dims])


def default_cfg():
    return dict(S=8192, D=2048, INTER=8192, RINTER=1024)


def build(cfg, debug=False):
    S = cfg["S"]
    D = cfg["D"]
    INTER = cfg["INTER"]
    RINTER = cfg["RINTER"]
    NCH = D // 128
    NG = S // 512
    NO = S // 4
    NOG = NO // 512
    NIC = INTER // 128
    NRIC = RINTER // 128
    SCALE = 1.0 / math.sqrt(128.0)
    ESHIFT = -20.0
    VW = (S // 128) * 129

    nc = bass.Bass("TRN2", target_bir_lowering=False)

    xb = nc.dram_tensor("xb", [S, D], F32, kind="ExternalInput").ap()
    xo = nc.dram_tensor("xo", [NO, D], F32, kind="ExternalInput").ap()
    wq = nc.dram_tensor("wq", [D, 2048], F32, kind="ExternalInput").ap()
    wk = nc.dram_tensor("wk", [D, 512], F32, kind="ExternalInput").ap()
    wv = nc.dram_tensor("wv", [D, 512], F32, kind="ExternalInput").ap()
    wo = nc.dram_tensor("wo", [2048, D], F32, kind="ExternalInput").ap()
    wg = nc.dram_tensor("wg", [D, INTER], F32, kind="ExternalInput").ap()
    wu = nc.dram_tensor("wu", [D, INTER], F32, kind="ExternalInput").ap()
    wd = nc.dram_tensor("wd", [INTER, D], F32, kind="ExternalInput").ap()
    rg = nc.dram_tensor("rg", [D, RINTER], F32, kind="ExternalInput").ap()
    ru = nc.dram_tensor("ru", [D, RINTER], F32, kind="ExternalInput").ap()
    rd = nc.dram_tensor("rd", [RINTER, D], F32, kind="ExternalInput").ap()
    rw = nc.dram_tensor("rw", [D, 8], F32, kind="ExternalInput").ap()
    cosf = nc.dram_tensor("cosf", [64, S], F32, kind="ExternalInput").ap()
    sinf = nc.dram_tensor("sinf", [64, S], F32, kind="ExternalInput").ap()
    coso = nc.dram_tensor("coso", [64, NO], F32, kind="ExternalInput").ap()
    sino = nc.dram_tensor("sino", [64, NO], F32, kind="ExternalInput").ap()
    maskt = nc.dram_tensor("maskt", [16, 128, 512], BF16, kind="ExternalInput").ap()
    ident = nc.dram_tensor("ident", [128, 128], BF16, kind="ExternalInput").ap()
    y = nc.dram_tensor("y", [NO, D], F32, kind="ExternalOutput").ap()

    wq_b = nc.dram_tensor("wq_b", [16, 128, D], BF16, kind="Internal").ap()
    wo_b = nc.dram_tensor("wo_b", [2048, D], BF16, kind="Internal").ap()
    wg_p = nc.dram_tensor("wg_p", [NIC, 128, NCH * 128], BF16, kind="Internal").ap()
    wu_p = nc.dram_tensor("wu_p", [NIC, 128, NCH * 128], BF16, kind="Internal").ap()
    rg_p = nc.dram_tensor("rg_p", [NRIC, 128, NCH * 128], BF16, kind="Internal").ap()
    ru_p = nc.dram_tensor("ru_p", [NRIC, 128, NCH * 128], BF16, kind="Internal").ap()
    wd_b = nc.dram_tensor("wd_b", [INTER, D], BF16, kind="Internal").ap()
    rd_b = nc.dram_tensor("rd_b", [RINTER, D], BF16, kind="Internal").ap()
    k_d = nc.dram_tensor("k_d", [4, 128, S], BF16, kind="Internal").ap()
    v_d = nc.dram_tensor("v_d", [4, 128, VW], BF16, kind="Internal").ap()
    q_d = nc.dram_tensor("q_d", [16, 128, NO], BF16, kind="Internal").ap()
    xn_d = nc.dram_tensor("xn_d", [NCH, 128, NO], BF16, kind="Internal").ap()
    attn_d = nc.dram_tensor("attn_d", [2048, NO], BF16, kind="Internal").ap()

    dbg = {}
    if debug:
        for name, t in [("d_k", k_d), ("d_v", v_d), ("d_q", q_d),
                        ("d_xn", xn_d), ("d_attn", attn_d)]:
            dbg[name] = nc.dram_tensor(name, list(t.shape), BF16,
                                       kind="ExternalOutput").ap()

    with ExitStack() as octx:
        prog = Prog(nc, octx)
        ARENA_COLS = 49152
        arena_t = octx.enter_context(nc.sbuf_tensor("arena", [128, ARENA_COLS], F32))
        A = SbArena(arena_t.ap(), ARENA_COLS * 4)
        pa = [octx.enter_context(nc.psum_tensor(f"pa{i}", [128, 1024], F32))
              for i in range(4)]
        paf = [p.ap() for p in pa]
        pab = [p.ap().bitcast(BF16) for p in pa]

        idb = A.alloc(128, 128, BF16)
        c_scale = A.alloc(128, 1, F32)
        c_shift = A.alloc(128, 1, F32)
        m_persist = A.mark()
        pst_in = [A.alloc(128, 2048, F32) for i in range(2)]
        pst_out = [A.alloc(128, 2048, BF16) for i in range(2)]
        m_prep = A.mark()

        T = {}
        def tk(name):
            if name not in T:
                T[name] = Tok()
            return T[name]

        t_id = prog.op("sync", lambda e: e.dma_start(out=idb[:, :], in_=ident),
                       inc16="d_cst")
        prog.op("vector", lambda e: e.memset(c_scale[:, :], SCALE))
        t_cst = prog.op("vector", lambda e: e.memset(c_shift[:, :], ESHIFT), inc="dv")

        prep_jobs = []

        def prep_matrix(src, dst, nrows, ncols, out_ap_fn=None):
            for c in range(nrows // 128):
                nq = (ncols + 2047) // 2048
                for qq in range(nq):
                    w = min(2048, ncols - qq * 2048)
                    if out_ap_fn is None:
                        dst_ap = dst[c * 128:(c + 1) * 128, qq * 2048:qq * 2048 + w]
                    else:
                        dst_ap = out_ap_fn(c, qq, w)
                    src_ap = src[c * 128:(c + 1) * 128, qq * 2048:qq * 2048 + w]
                    prep_jobs.append((src_ap, dst_ap, w))

        def pump_prep(n):
            for _ in range(n):
                if pump_prep.i >= len(prep_jobs):
                    return
                src_ap, dst_ap, w = prep_jobs[pump_prep.i]
                pump_prep.i += 1
                par = pump_prep.i % 2
                sin = pst_in[par]
                sout = pst_out[par]
                prog.wait_all("gpsimd", tk(f"pstin{par}").war())
                tin = prog.op(
                    "gpsimd",
                    lambda e, src_ap=src_ap, w=w, sin=sin: e.dma_start(
                        out=sin[:, 0:w], in_=src_ap),
                    inc16=f"g_in{par}")
                tk(f"pstin{par}").wrote(tin)
                prog.wait("gpsimd", tin)
                prog.wait_all("gpsimd", tk(f"pstout{par}").war())
                tcv = prog.op(
                    "gpsimd",
                    lambda e, w=w, sin=sin, sout=sout: e.tensor_copy(
                        sout[:, 0:w], sin[:, 0:w]),
                    inc="gp")
                tk(f"pstin{par}").read(tcv)
                tk(f"pstout{par}").wrote(tcv)
                prog.wait("sync", tcv)
                tout = prog.op(
                    "sync",
                    lambda e, dst_ap=dst_ap, w=w, sout=sout: e.dma_start(
                        out=dst_ap, in_=sout[:, 0:w]),
                    inc16=f"g_out{par}")
                tk(f"pstout{par}").read(tout)
        pump_prep.i = 0

        def packed_out(dstp):
            def fn(c, qq, w):
                ni = w // 128
                t = dstp[qq * 16:qq * 16 + ni, :, c * 128:(c + 1) * 128]
                return t.rearrange("i f d -> f i d")
            return fn

        prep_matrix(wq, None, D, 2048, out_ap_fn=packed_out(wq_b))
        t_prep_wq = (("g_out0", 8 * 16), ("g_out1", 8 * 16))
        prog.sem("g_out0")
        prog.sem("g_out1")
        prep_matrix(wo, wo_b, 2048, D)
        prep_matrix(wg, None, D, INTER, out_ap_fn=packed_out(wg_p))
        prep_matrix(wu, None, D, INTER, out_ap_fn=packed_out(wu_p))
        prep_matrix(rg, None, D, RINTER, out_ap_fn=packed_out(rg_p))
        prep_matrix(ru, None, D, RINTER, out_ap_fn=packed_out(ru_p))
        prep_matrix(wd, wd_b, INTER, D)
        prep_matrix(rd, rd_b, RINTER, D)

        if True:
            A.reset(m_prep)
            wkf = [A.alloc(128, 512, F32) for i in range(2)]
            wkb = A.alloc(128, NCH * 512, BF16)
            wvb = A.alloc(128, NCH * 512, BF16)
            LNP = 3
            xbuf = [A.alloc(128, D, F32) for i in range(LNP)]
            xnt = [A.alloc(128, D, BF16) for i in range(LNP)]
            xnfm = [A.alloc(128, NCH * 512, BF16) for i in range(2)]
            cosb = [A.alloc(64, 512, F32) for i in range(2)]
            sinb = [A.alloc(64, 512, F32) for i in range(2)]
            tmpA = A.alloc(128, 512, F32)
            tmpB = A.alloc(128, 512, F32)
            krop = [A.alloc(128, 512, BF16) for i in range(2)]
            vtmp = A.alloc(128, 512, BF16)
            vhat = [A.alloc(128, 516, BF16) for i in range(2)]
            stt = [A.alloc(128, 24, F32) for i in range(LNP)]
            mvb = [A.alloc(128, 2, F32) for i in range(LNP)]
            rstd = [A.alloc(128, 1, F32) for i in range(LNP)]
            negm = [A.alloc(128, 1, F32) for i in range(LNP)]
            nbias = [A.alloc(128, 1, F32) for i in range(LNP)]
            ptx = pab[0]
            pk = [paf[1][:, 0:512], paf[1][:, 512:1024]]
            pv = paf[2][:, 0:512]
            ptv = pab[2][:, 1024:2048]
            prope = [paf[3][:, 0:512], paf[3][:, 512:1024]]

            for mat, dstb in ((wk, wkb), (wv, wvb)):
                for c in range(NCH):
                    par = c % 2
                    prog.wait_all("sync", tk(f"wkf{par}").war())
                    tin = prog.op("sync", lambda e, mat=mat, c=c, par=par: e.dma_start(
                        out=wkf[par][:, :], in_=mat[c * 128:(c + 1) * 128, :]),
                        inc16=f"d_wk{par}")
                    tk(f"wkf{par}").wrote(tin)
                    prog.wait("vector", tin)
                    tcv = prog.op("vector", lambda e, dstb=dstb, c=c, par=par:
                                  e.tensor_copy(dstb[:, c * 512:(c + 1) * 512],
                                                wkf[par][:, :]), inc="dv")
                    tk(f"wkf{par}").read(tcv)
            t_wkv = prog.now("dv")

            for i in range(2):
                vh = vhat[i]
                ap1 = strided(vh, 128, [(129, 4)])
                prog.op("vector", lambda e, ap1=ap1: e.memset(ap1, 1.0), inc="dv")

            def ln_subtile(src_rows, n, dst_xnfm, s4):
                par = n % LNP
                xB, xT = xbuf[par], xnt[par]
                sT, mV, rS, nM, nB = stt[par], mvb[par], rstd[par], negm[par], nbias[par]
                prog.wait_all("sync", tk(f"xbuf{par}").war())
                tin = prog.op("sync", lambda e, src_rows=src_rows, xB=xB: e.dma_start(
                    out=xB[:, :], in_=src_rows), inc16=f"d_x{par}")
                tk(f"xbuf{par}").wrote(tin)
                prog.wait("vector", tin)
                prog.wait_all("vector", tk(f"lnchain{par}").war())
                for i in range(4):
                    lst = prog.op("vector", lambda e, i=i, xB=xB, sT=sT: e.bn_stats(
                        sT[:, 6 * i:6 * i + 6], xB[:, 512 * i:512 * i + 512]),
                        inc="dv" if i == 3 else None)
                tk(f"xbuf{par}").read(lst)
                prog.wait("vector", lst)
                tag = prog.op("vector", lambda e, sT=sT, mV=mV: e.bn_aggr(
                    mV[:, :], sT[:, :]), inc="dv")
                prog.wait("vector", tag)
                prog.op("vector", lambda e, mV=mV, nM=nM: e.tensor_scalar(
                    nM[:, :], mV[:, 0:1], -1.0, None, AluOpType.mult))
                tve = prog.op("vector", lambda e, mV=mV: e.tensor_scalar(
                    mV[:, 1:2], mV[:, 1:2], 1e-5, None, AluOpType.add), inc="dv")
                prog.wait("scalar", tve)
                tsq = prog.op("scalar", lambda e, mV=mV, rS=rS: e.activation(
                    rS[:, :], mV[:, 1:2], AF.Sqrt), inc="ac")
                prog.wait("vector", tsq)
                trc = prog.op("vector", lambda e, rS=rS: e.reciprocal(
                    rS[:, :], rS[:, :]), inc="dv")
                prog.wait("vector", trc)
                tnb = prog.op("vector", lambda e, nM=nM, rS=rS, nB=nB: e.tensor_tensor(
                    nB[:, :], nM[:, :], rS[:, :], AluOpType.mult), inc="dv")
                prog.wait("scalar", tnb)
                prog.wait_all("scalar", tk(f"xnt{par}").war())
                taf = prog.op("scalar", lambda e, xB=xB, xT=xT, rS=rS, nB=nB:
                              e.activation(
                    xT[:, :], xB[:, :], AF.Identity, bias=nB[:, :],
                    scale=rS[:, :]), inc="ac")
                tk(f"xbuf{par}").read(taf)
                tk(f"xnt{par}").wrote(taf)
                tk(f"lnchain{par}").wrote(taf)
                prog.wait("tensor", taf)
                prog.wait("tensor", t_id)
                prog.wait_all("tensor", tk("ptx").war())
                for c in range(NCH):
                    ttr = prog.op("tensor", lambda e, c=c, xT=xT: e.transpose(
                        ptx[:, 128 * c:128 * c + 128], xT[:, 128 * c:128 * c + 128],
                        idb[:, :]), inc="pe" if c == NCH - 1 else None)
                tk(f"xnt{par}").read(ttr)
                tk("ptx").wrote(ttr)
                prog.wait("vector", ttr)
                out_ap = strided(dst_xnfm, 128 * s4, [(512, NCH), (1, 128)])
                in_ap = strided(ptx, 0, [(128, NCH), (1, 128)])
                tev = prog.op("vector", lambda e, out_ap=out_ap, in_ap=in_ap:
                              e.tensor_copy(out_ap, in_ap), inc="dv")
                tk("ptx").read(tev)
                return tev

            nsub = 0
            for g in range(NG + NOG):
                pump_prep(2)
                own = g >= NG
                gl = g - NG if own else g
                gpar = g % 2
                fmB = xnfm[gpar]
                prog.wait_all("vector", tk(f"xnfm{gpar}").war())
                evtoks = []
                for s4 in range(4):
                    r0 = gl * 512 + s4 * 128
                    rows = (xo if own else xb)[r0:r0 + 128, :]
                    evtoks.append(ln_subtile(rows, nsub, fmB, s4))
                    nsub += 1
                tk(f"xnfm{gpar}").wrote(evtoks[-1])

                if own:
                    prog.wait_all("scalar", tk(f"xnfm{gpar}").raw())
                    for c in range(NCH):
                        txn = prog.op("scalar", lambda e, c=c, gl=gl, fmB=fmB:
                                      e.dma_start(
                            out=xn_d[c][:, gl * 512:(gl + 1) * 512],
                            in_=fmB[:, 512 * c:512 * c + 512]), inc16=f"d_no{gpar}")
                        tk(f"xnfm{gpar}").read(txn)
                    continue

                cpar = g % 2
                prog.wait_all("sync", tk(f"cosb{cpar}").war())
                prog.op("sync", lambda e, g=g, cpar=cpar: e.dma_start(
                    out=cosb[cpar][:, :], in_=cosf[:, g * 512:(g + 1) * 512]),
                    inc16=f"d_c{cpar}")
                tc2 = prog.op("sync", lambda e, g=g, cpar=cpar: e.dma_start(
                    out=sinb[cpar][:, :], in_=sinf[:, g * 512:(g + 1) * 512]),
                    inc16=f"d_c{cpar}")
                tk(f"cosb{cpar}").wrote(tc2)

                for kvh in range(4):
                    if cfg.get("skip_kv"):
                        break
                    kp = (g * 4 + kvh) % 2
                    prog.wait_all("tensor", tk(f"xnfm{gpar}").raw())
                    prog.wait_all("tensor", tk(f"pk{kp}").war())
                    prog.wait("tensor", t_wkv)
                    for c in range(NCH):
                        tmm = prog.op("tensor", lambda e, c=c, kvh=kvh, kp=kp, fmB=fmB:
                                      e.matmul(
                            pk[kp][:, :],
                            wkb[:, 512 * c + 128 * kvh:512 * c + 128 * kvh + 128],
                            fmB[:, 512 * c:512 * c + 512],
                            start=(c == 0), stop=(c == NCH - 1)),
                            inc="pe" if c == NCH - 1 else None)
                    tk(f"xnfm{gpar}").read(tmm)
                    tk(f"pk{kp}").wrote(tmm)
                    prog.wait("vector", tmm)
                    prog.wait_all("vector", tk(f"cosb{cpar}").raw())
                    prog.wait_all("vector", tk(f"krop{kp}").war())
                    prog.wait_all("vector", tk("ropetmp").war())
                    cb, sn, kr = cosb[cpar], sinb[cpar], krop[kp]
                    pkk = pk[kp]
                    prp = prope[kp]
                    prog.wait_all("vector", tk(f"prope{kp}").war())
                    prog.op("vector", lambda e, pkk=pkk, cb=cb: e.tensor_tensor(
                        tmpA[0:64, :], pkk[0:64, :], cb[:, :], AluOpType.mult))
                    t2m = prog.op("vector", lambda e, pkk=pkk, sn=sn, prp=prp:
                                  e.tensor_tensor(
                        prp[64:128, :], pkk[64:128, :], sn[:, :], AluOpType.mult),
                        inc="dv")
                    prog.wait("vector", t2m)
                    prog.op("vector", lambda e, kr=kr, prp=prp: e.tensor_tensor(
                        kr[0:64, :], tmpA[0:64, :], prp[64:128, :],
                        AluOpType.subtract))
                    prog.op("vector", lambda e, pkk=pkk, cb=cb: e.tensor_tensor(
                        tmpA[64:128, :], pkk[64:128, :], cb[:, :], AluOpType.mult))
                    t4m = prog.op("vector", lambda e, pkk=pkk, sn=sn, prp=prp:
                                  e.tensor_tensor(
                        prp[0:64, :], pkk[0:64, :], sn[:, :], AluOpType.mult),
                        inc="dv")
                    prog.wait("vector", t4m)
                    trp = prog.op("vector", lambda e, kr=kr, prp=prp: e.tensor_tensor(
                        kr[64:128, :], tmpA[64:128, :], prp[0:64, :], AluOpType.add),
                        inc="dv")
                    tk("ropetmp").wrote(trp)
                    tk(f"prope{kp}").wrote(trp)
                    tk(f"pk{kp}").read(trp)
                    tk(f"cosb{cpar}").read(trp)
                    tk(f"krop{kp}").wrote(trp)
                    prog.wait("scalar", trp)
                    tko = prog.op("scalar", lambda e, kvh=kvh, g=g, kr=kr: e.dma_start(
                        out=k_d[kvh][:, g * 512:(g + 1) * 512], in_=kr[:, :]),
                        inc16=f"d_ko{kp}")
                    tk(f"krop{kp}").read(tko)

                    if cfg.get("skip_v"):
                        continue
                    prog.wait_all("tensor", tk("pv").war())
                    for c in range(NCH):
                        tvm = prog.op("tensor", lambda e, c=c, kvh=kvh, fmB=fmB:
                                      e.matmul(
                            pv[:, :],
                            wvb[:, 512 * c + 128 * kvh:512 * c + 128 * kvh + 128],
                            fmB[:, 512 * c:512 * c + 512],
                            start=(c == 0), stop=(c == NCH - 1)),
                            inc="pe" if c == NCH - 1 else None)
                    tk(f"xnfm{gpar}").read(tvm)
                    tk("pv").wrote(tvm)
                    prog.wait("vector", tvm)
                    prog.wait_all("vector", tk("vtmp").war())
                    tve = prog.op("vector", lambda e: e.tensor_copy(
                        vtmp[:, :], pv[:, :]), inc="dv")
                    tk("pv").read(tve)
                    tk("vtmp").wrote(tve)
                    prog.wait("tensor", tve)
                    prog.wait_all("tensor", tk("ptv").war())
                    for t in range(4):
                        ttv = prog.op("tensor", lambda e, t=t: e.transpose(
                            ptv[:, 128 * t:128 * t + 128], vtmp[:, 128 * t:128 * t + 128],
                            idb[:, :]), inc="pe" if t == 3 else None)
                    tk("vtmp").read(ttv)
                    tk("ptv").wrote(ttv)
                    vp = (g * 4 + kvh) % 2
                    vh = vhat[vp]
                    prog.wait("vector", ttv)
                    prog.wait_all("vector", tk(f"vhat{vp}").war())
                    out_ap = strided(vh, 0, [(129, 4), (1, 128)])
                    in_ap = strided(ptv, 0, [(128, 4), (1, 128)])
                    tvh = prog.op("vector", lambda e, out_ap=out_ap, in_ap=in_ap:
                                  e.tensor_copy(out_ap, in_ap), inc="dv")
                    tk("ptv").read(tvh)
                    tk(f"vhat{vp}").wrote(tvh)
                    prog.wait("scalar", tvh)
                    tvo = prog.op("scalar", lambda e, kvh=kvh, g=g, vh=vh: e.dma_start(
                        out=v_d[kvh][:, g * 516:(g + 1) * 516], in_=vh[:, :]),
                        inc16=f"d_vo{vp}")
                    tk(f"vhat{vp}").read(tvo)

        prog.barrier("b_p1a", dma_sems=[
            "d_x0", "d_x1", "d_c0", "d_c1", "d_wk0", "d_wk1",
            "d_ko0", "d_ko1", "d_vo0", "d_vo1", "d_no0", "d_no1"])

        if True:
            A.reset(m_prep)
            xno = A.alloc(128, NCH * NO, BF16)
            wqt = [A.alloc(128, NCH * 128, BF16) for i in range(2)]
            qrop = [A.alloc(128, 512, BF16) for i in range(2)]
            cosoB = A.alloc(64, NO, F32)
            sinoB = A.alloc(64, NO, F32)
            tmpA2 = A.alloc(128, 512, F32)
            tmpB2 = A.alloc(128, 512, F32)
            pq = [paf[0][:, 0:512], paf[0][:, 512:1024]]
            prope2 = paf[1][:, 0:512]

            prog.op("sync", lambda e: e.dma_start(out=cosoB[:, :], in_=coso),
                    inc16="d_qc")
            prog.op("sync", lambda e: e.dma_start(out=sinoB[:, :], in_=sino),
                    inc16="d_qc")
            for c in range(NCH):
                prog.op("sync", lambda e, c=c: e.dma_start(
                    out=xno[:, c * NO:(c + 1) * NO], in_=xn_d[c][:, :]), inc16="d_qx")
            t_qx = prog.now("d_qx")

            pump_prep(max(0, 16 - pump_prep.i))
            prog.wait("sync", t_prep_wq[0])
            prog.wait("sync", t_prep_wq[1])
            for h in range(16):
                pump_prep(3)
                hp = h % 2
                prog.wait_all("sync", tk(f"wqt{hp}").war())
                twq = prog.op("sync", lambda e, h=h, hp=hp: e.dma_start(
                    out=wqt[hp][:, :], in_=wq_b[h]), inc16=f"d_wq{hp}")
                tk(f"wqt{hp}").wrote(twq)
                for tg in range(NOG):
                    qp = (h * NOG + tg) % 2
                    prog.wait("tensor", twq)
                    prog.wait("tensor", t_qx)
                    prog.wait_all("tensor", tk(f"pq{qp}").war())
                    for c in range(NCH):
                        tqm = prog.op("tensor", lambda e, c=c, tg=tg, qp=qp, hp=hp:
                                      e.matmul(
                            pq[qp][:, :], wqt[hp][:, c * 128:(c + 1) * 128],
                            xno[:, c * NO + tg * 512: c * NO + tg * 512 + 512],
                            start=(c == 0), stop=(c == NCH - 1)),
                            inc="pe" if c == NCH - 1 else None)
                    tk(f"wqt{hp}").read(tqm)
                    tk(f"pq{qp}").wrote(tqm)
                    prog.wait("vector", tqm)
                    prog.wait(eng="vector", tok=("d_qc", 32))
                    prog.wait_all("vector", tk(f"qrop{qp}").war())
                    prog.wait_all("vector", tk("ropetmp2").war())
                    qr = qrop[qp]
                    pqq = pq[qp]
                    cslc = cosoB[:, tg * 512:(tg + 1) * 512]
                    sslc = sinoB[:, tg * 512:(tg + 1) * 512]
                    prog.op("vector", lambda e, pqq=pqq, cslc=cslc: e.tensor_tensor(
                        tmpA2[0:64, :], pqq[0:64, :], cslc, AluOpType.mult))
                    t2m = prog.op("vector", lambda e, pqq=pqq, sslc=sslc:
                                  e.tensor_tensor(
                        prope2[64:128, :], pqq[64:128, :], sslc, AluOpType.mult),
                        inc="dv")
                    prog.wait("vector", t2m)
                    prog.op("vector", lambda e, qr=qr: e.tensor_tensor(
                        qr[0:64, :], tmpA2[0:64, :], prope2[64:128, :],
                        AluOpType.subtract))
                    prog.op("vector", lambda e, pqq=pqq, cslc=cslc: e.tensor_tensor(
                        tmpA2[64:128, :], pqq[64:128, :], cslc, AluOpType.mult))
                    t4m = prog.op("vector", lambda e, pqq=pqq, sslc=sslc:
                                  e.tensor_tensor(
                        prope2[0:64, :], pqq[0:64, :], sslc, AluOpType.mult), inc="dv")
                    prog.wait("vector", t4m)
                    trp = prog.op("vector", lambda e, qr=qr: e.tensor_tensor(
                        qr[64:128, :], tmpA2[64:128, :], prope2[0:64, :],
                        AluOpType.add), inc="dv")
                    tk("ropetmp2").wrote(trp)
                    tk(f"pq{qp}").read(trp)
                    tk(f"qrop{qp}").wrote(trp)
                    prog.wait("scalar", trp)
                    tqo = prog.op("scalar", lambda e, h=h, tg=tg, qr=qr: e.dma_start(
                        out=q_d[h][:, tg * 512:(tg + 1) * 512], in_=qr[:, :]),
                        inc16=f"d_qo{qp}")
                    tk(f"qrop{qp}").read(tqo)

        prog.barrier("b_p1c", dma_sems=["d_qc", "d_qx", "d_wq0", "d_wq1",
                                        "d_qo0", "d_qo1"])
        if debug:
            for src, name in [(k_d, "d_k"), (v_d, "d_v"), (q_d, "d_q"),
                              (xn_d, "d_xn")]:
                prog.op("sync", lambda e, src=src, name=name: e.dma_start(
                    out=dbg[name], in_=src), inc16="d_dbg")

        if True:
            A.reset(m_prep)
            ksb = [A.alloc(128, 2 * S, BF16) for i in range(2)]
            vsb = [A.alloc(128, 2 * VW, BF16) for i in range(2)]
            msb = A.alloc(128, 16 * 512, BF16)
            qt = [A.alloc(128, 512, BF16) for i in range(3)]
            esb = [A.alloc(128, 1024, BF16) for i in range(2)]
            atn = [A.alloc(128, 128, BF16) for i in range(4)]
            atn4 = [A.alloc(128, 512, BF16) for i in range(2)]
            rcp = A.alloc(128, 4, F32)
            psS = [paf[0], paf[1]]
            psAV = [paf[2][:, 0:512], paf[2][:, 512:1024],
                    paf[3][:, 0:512], paf[3][:, 512:1024]]
            psTV = [pab[2][:, 0:1024], pab[2][:, 1024:2048],
                    pab[3][:, 0:1024], pab[3][:, 1024:2048]]

            for z in range(16):
                t_m = prog.op("sync", lambda e, z=z: e.dma_start(
                    out=msb[:, z * 512:(z + 1) * 512], in_=maskt[z]), inc16="d_m")
            t_mask = t_m

            t_kv_tok = [None, None]
            for pp in range(2):
                for j in range(2):
                    prog.op("sync", lambda e, pp=pp, j=j: e.dma_start(
                        out=ksb[pp][:, j * S:(j + 1) * S], in_=k_d[2 * pp + j][:, :]),
                        inc16=f"d_kv{pp}")
                    prog.op("sync", lambda e, pp=pp, j=j: e.dma_start(
                        out=vsb[pp][:, j * VW:(j + 1) * VW], in_=v_d[2 * pp + j][:, :]),
                        inc16=f"d_kv{pp}")
                t_kv_tok[pp] = prog.now(f"d_kv{pp}")
            for p in range(2):
                t_kv = t_kv_tok[p]
                ksb_p = ksb[p]
                vsb_p = vsb[p]

                units = [(g, h) for g in range(NOG) for h in range(8)]
                if p == 0:
                    prog.wait_all("sync", tk("qt0").war())
                    tq0 = prog.op("sync", lambda e, p=p: e.dma_start(
                        out=qt[0][:, :], in_=q_d[8 * p][:, 0:512]), inc16="d_q0")
                    tk("qt0").wrote(tq0)
                    prog.q_tok = tq0
                    prog.q_next = 1
                for ui, (g, h) in enumerate(units):
                    u = p * NOG * 8 + g * 8 + h
                    up = u % 3
                    gh = 8 * p + h
                    kvloc = h // 4
                    NKB = 16 * g + 16
                    pump_prep(3)
                    tql = prog.q_tok
                    if ui + 1 < len(units):
                        g2, h2 = units[ui + 1]
                        gh2, up2 = 8 * p + h2, (u + 1) % 3
                    elif p == 0:
                        g2, h2 = units[0]
                        gh2, up2 = 8 + h2, (u + 1) % 3
                    else:
                        g2 = None
                    if g2 is not None:
                        prog.wait_all("sync", tk(f"qt{up2}").war())
                        tq2 = prog.op("sync", lambda e, gh2=gh2, g2=g2, up2=up2:
                                      e.dma_start(
                            out=qt[up2][:, :],
                            in_=q_d[gh2][:, g2 * 512:(g2 + 1) * 512]),
                            inc16=f"d_q{up2}")
                        tk(f"qt{up2}").wrote(tq2)
                        prog.q_tok = tq2
                    prog.wait("tensor", tql)
                    prog.wait("tensor", t_kv)
                    avlast = None
                    NKB2 = NKB // 2

                    def col0_of(kb):
                        z = kb - 16 * g
                        return 0 if z < 4 else (z // 4) * 128

                    def emit_S(kb2):
                        sp = kb2 % 2
                        prog.wait_all("tensor", tk(f"psS{sp}").war())
                        for j in range(2):
                            kb = 2 * kb2 + j
                            c0 = col0_of(kb)
                            tsm = prog.op("tensor", lambda e, sp=sp, j=j, kb=kb, c0=c0,
                                          kvloc=kvloc, up=up, ksb_p=ksb_p: e.matmul(
                                psS[sp][:, j * 512 + c0:(j + 1) * 512],
                                ksb_p[:, kvloc * S + kb * 128:
                                      kvloc * S + kb * 128 + 128],
                                qt[up][:, c0:512], start=True, stop=True),
                                inc="pe" if j == 1 else None)
                        tk(f"qt{up}").read(tsm)
                        tk(f"psS{sp}").wrote(tsm)
                        return tsm

                    s_tok = {0: emit_S(0)}
                    for kb2 in range(NKB2):
                        sp = kb2 % 2
                        tsm = s_tok.pop(kb2)
                        prog.wait("scalar", tsm)
                        prog.wait_all("scalar", tk(f"esb{sp}").war())
                        prog.wait("scalar", t_cst)
                        mtoks = [None, None]
                        c00 = col0_of(2 * kb2)
                        tex = prog.op("scalar", lambda e, sp=sp, c00=c00:
                                      e.activation(
                            esb[sp][:, c00:1024],
                            psS[sp][:, c00:1024], AF.Exp,
                            bias=c_shift[:, :], scale=c_scale[:, :]), inc="ac")
                        extoks = [tex, tex]
                        for j in range(2):
                            kb = 2 * kb2 + j
                            c0 = col0_of(kb)
                            z = kb - 16 * g
                            if z >= 0:
                                prog.wait("vector", tex)
                                prog.wait("vector", t_mask)
                                tmk = prog.op("vector", lambda e, sp=sp, j=j, z=z,
                                              c0=c0: e.tensor_tensor(
                                    esb[sp][:, j * 512 + c0:(j + 1) * 512],
                                    esb[sp][:, j * 512 + c0:(j + 1) * 512],
                                    msb[:, z * 512 + c0:(z + 1) * 512],
                                    AluOpType.mult), inc="dv")
                                mtoks[j] = tmk
                        tk(f"psS{sp}").read(extoks[1])
                        tk(f"esb{sp}").wrote(extoks[1])
                        for j in range(2):
                            if mtoks[j] is not None:
                                tk(f"esb{sp}").w.append(mtoks[j])
                        if kb2 + 1 < NKB2:
                            s_tok[kb2 + 1] = emit_S(kb2 + 1)
                        for j in range(2):
                            kb = 2 * kb2 + j
                            z = kb - 16 * g
                            qqmin = 0 if z < 4 else z // 4
                            prog.wait("tensor", extoks[j])
                            if mtoks[j] is not None:
                                prog.wait("tensor", mtoks[j])
                            for qq in range(qqmin, 4):
                                if kb == 0:
                                    prog.wait_all("tensor", tk(f"psAV{qq}").war())
                                last_kb = 16 * g + 4 * qq + 3
                                tav = prog.op("tensor", lambda e, sp=sp, j=j, qq=qq,
                                              kvloc=kvloc, kb=kb, last_kb=last_kb,
                                              vsb_p=vsb_p: e.matmul(
                                    psAV[qq][:, 0:129],
                                    esb[sp][:, j * 512 + qq * 128:
                                            j * 512 + qq * 128 + 128],
                                    vsb_p[:, kvloc * VW + kb * 129:
                                          kvloc * VW + kb * 129 + 129],
                                    start=(kb == 0), stop=(kb == last_kb)),
                                    inc="pe")
                                tk(f"psAV{qq}").wrote(tav)
                                tk(f"esb{sp}").read(tav)
                                avlast = tav
                    prog.wait("vector", avlast)
                    prog.wait_all("vector", tk("rcp").war())
                    for qq in range(4):
                        trr = prog.op("vector", lambda e, qq=qq: e.reciprocal(
                            rcp[:, qq:qq + 1], psAV[qq][:, 128:129]),
                            inc="dv" if qq == 3 else None)
                    tk("rcp").wrote(trr)
                    prog.wait("vector", trr)
                    a4 = atn4[u % 2]
                    prog.wait_all("vector", tk(f"atn4_{u % 2}").war())
                    ntoks = []
                    for qq in range(4):
                        prog.wait_all("vector", tk(f"atn{qq}").war())
                        tns = prog.op("vector", lambda e, qq=qq: e.tensor_scalar(
                            atn[qq][:, :], psAV[qq][:, 0:128], rcp[:, qq:qq + 1],
                            None, AluOpType.mult), inc="dv")
                        tk(f"psAV{qq}").read(tns)
                        tk("rcp").read(tns)
                        tk(f"atn{qq}").wrote(tns)
                        ntoks.append(tns)
                    for qq in range(4):
                        prog.wait("tensor", ntoks[qq])
                        prog.wait("tensor", t_id)
                        ttr = prog.op("tensor", lambda e, qq=qq: e.transpose(
                            psTV[qq][:, 0:128], atn[qq][:, :], idb[:, :]),
                            inc="pe")
                        tk(f"psAV{qq}").wrote(ttr)
                        tk(f"atn{qq}").read(ttr)
                        prog.wait("vector", ttr)
                        tev = prog.op("vector", lambda e, qq=qq, a4=a4:
                                      e.tensor_copy(
                            a4[:, qq * 128:(qq + 1) * 128], psTV[qq][:, 0:128]),
                            inc="dv")
                        tk(f"psAV{qq}").read(tev)
                        tk(f"atn4_{u % 2}").wrote(tev)
                    prog.wait("sync", tev)
                    tad = prog.op("sync", lambda e, g=g, gh=gh, a4=a4:
                                  e.dma_start(
                        out=attn_d[gh * 128:(gh + 1) * 128,
                                   g * 512:(g + 1) * 512],
                        in_=a4[:, :]), inc16=f"d_at{u % 2}")
                    tk(f"atn4_{u % 2}").read(tad)


        prog.barrier("b_p2", dma_sems=["d_q0", "d_q1", "d_kv0", "d_kv1", "d_m",
                                       "d_at0", "d_at1", "d_at2", "d_at3"])
        if debug:
            prog.op("sync", lambda e: e.dma_start(out=dbg["d_attn"], in_=attn_d),
                    inc16="d_dbg")

        pump_prep(len(prep_jobs))

        if True:
            A.reset(m_persist)
            xnoB = A.alloc(128, NCH * 512, BF16)
            act = A.alloc(128, NIC * 512, BF16)
            ract = A.alloc(128, NRIC * 512, BF16)
            acc = A.alloc(128, 4 * D, F32)
            WGN, WDN, WON = 3, 5, 5
            wgt = [A.alloc(128, NCH * 128, BF16) for i in range(WGN)]
            wut = [A.alloc(128, NCH * 128, BF16) for i in range(WGN)]
            wdt = [A.alloc(128, 512, BF16) for i in range(WDN)]
            wot = [A.alloc(128, 512, BF16) for i in range(WON)]
            afm = A.alloc(128, NCH * 512, BF16)
            xob = A.alloc(128, D, F32)
            outb = [A.alloc(128, 512, F32) for i in range(2)]
            sgb = A.alloc(128, 512, F32)
            sgb2 = A.alloc(128, 512, F32)
            rwb = A.alloc(128, NCH * 8, BF16)
            rwf = A.alloc(128, 8, F32)
            lgs = A.alloc(128, 8, F32)
            m1 = A.alloc(128, 1, F32)
            m2 = A.alloc(128, 1, F32)
            eqv = A.alloc(128, 8, F32)
            evv = A.alloc(128, 1, F32)
            svv = A.alloc(128, 1, F32)
            comb = A.alloc(128, 4, F32)
            psG = paf[0][:, 0:512]
            psU = paf[0][:, 512:1024]
            psD = [paf[1][:, 0:512], paf[1][:, 512:1024],
                   paf[2][:, 0:512], paf[2][:, 512:1024]]
            psR = paf[3][:, 512:1024]

            for _e in ("sync", "tensor", "vector", "scalar"):
                prog.wait(_e, prog.now("gp"))
                prog.wait(_e, prog.now("g_out0"))
                prog.wait(_e, prog.now("g_out1"))
                prog.wait(_e, prog.now("g_in0"))
                prog.wait(_e, prog.now("g_in1"))

            for c in range(NCH):
                prog.wait_all("sync", tk("rwf").war())
                tin = prog.op("sync", lambda e, c=c: e.dma_start(
                    out=rwf[:, :], in_=rw[c * 128:(c + 1) * 128, :]), inc16="d_rw")
                tk("rwf").wrote(tin)
                prog.wait("vector", tin)
                tcv = prog.op("vector", lambda e, c=c: e.tensor_copy(
                    rwb[:, c * 8:(c + 1) * 8], rwf[:, :]), inc="dv")
                tk("rwf").read(tcv)
            t_rwb = prog.now("dv")

            for blk in range(NOG):
                prog.wait_all("sync", tk("xnoB").war())
                for c in range(NCH):
                    txn = prog.op("sync", lambda e, c=c, blk=blk: e.dma_start(
                        out=xnoB[:, c * 512:(c + 1) * 512],
                        in_=xn_d[c][:, blk * 512:(blk + 1) * 512]), inc16="d_xb")
                tk("xnoB").wrote(txn)
                t_xn = prog.now("d_xb")

                for t4 in range(4):
                    prog.wait("tensor", t_xn)
                    prog.wait("tensor", t_rwb)
                    prog.wait_all("tensor", tk("psR").war())
                    for c in range(NCH):
                        trm = prog.op("tensor", lambda e, c=c, t4=t4: e.matmul(
                            psR[:, 0:8],
                            xnoB[:, c * 512 + t4 * 128: c * 512 + t4 * 128 + 128],
                            rwb[:, c * 8:(c + 1) * 8],
                            start=(c == 0), stop=(c == NCH - 1)),
                            inc="pe" if c == NCH - 1 else None)
                    tk("psR").wrote(trm)
                    tk("xnoB").read(trm)
                    prog.wait("vector", trm)
                    prog.wait_all("vector", tk("router").war())
                    tcp = prog.op("vector", lambda e: e.tensor_copy(
                        lgs[:, :], psR[:, 0:8]), inc="dv")
                    tk("psR").read(tcp)
                    prog.wait("vector", tcp)
                    tm1 = prog.op("vector", lambda e: e.reduce_max(
                        m1[:, :], lgs[:, :], axis=AX), inc="dv")
                    prog.wait("vector", tm1)
                    teq = prog.op("vector", lambda e: e.tensor_scalar(
                        eqv[:, :], lgs[:, :], m1[:, :], None, AluOpType.is_equal),
                        inc="dv")
                    prog.wait("vector", teq)
                    tms = prog.op("vector", lambda e: e.scalar_tensor_tensor(
                        eqv[:, :], eqv[:, :], -1e30, lgs[:, :],
                        AluOpType.mult, AluOpType.add), inc="dv")
                    prog.wait("vector", tms)
                    tm2 = prog.op("vector", lambda e: e.reduce_max(
                        m2[:, :], eqv[:, :], axis=AX), inc="dv")
                    prog.wait("vector", tm2)
                    tdd = prog.op("vector", lambda e: e.tensor_tensor(
                        m2[:, :], m2[:, :], m1[:, :], AluOpType.subtract), inc="dv")
                    prog.wait("scalar", tdd)
                    tev = prog.op("scalar", lambda e: e.activation(
                        evv[:, :], m2[:, :], AF.Exp), inc="ac")
                    prog.wait("vector", tev)
                    tsv = prog.op("vector", lambda e: e.tensor_scalar(
                        svv[:, :], evv[:, :], 1.0, None, AluOpType.add), inc="dv")
                    prog.wait("vector", tsv)
                    trc = prog.op("vector", lambda e: e.reciprocal(
                        svv[:, :], svv[:, :]), inc="dv")
                    prog.wait("vector", trc)
                    tcb = prog.op("vector", lambda e, t4=t4: e.scalar_tensor_tensor(
                        comb[:, t4:t4 + 1], evv[:, :], 1.0, svv[:, :],
                        AluOpType.add, AluOpType.mult), inc="dv")
                    tk("router").wrote(tcb)
                    tk(f"comb{t4}").wrote(tcb)

                def gateup(nic, gp_src, up_src, act_dst, actname):
                    prog.wait_all("vector", tk(actname).war())
                    for ic in range(nic):
                        wp = ic % WGN
                        wgt_b, wut_b = wgt[wp], wut[wp]
                        prog.wait_all("sync", tk(f"wgt{wp}").war())
                        prog.op("sync", lambda e, ic=ic, gp_src=gp_src, wgt_b=wgt_b:
                                e.dma_start(out=wgt_b, in_=gp_src[ic]),
                            inc16=f"d_g{wp}")
                        prog.op("sync", lambda e, ic=ic, up_src=up_src, wut_b=wut_b:
                                e.dma_start(out=wut_b, in_=up_src[ic]),
                            inc16=f"d_g{wp}")
                        tld = prog.now(f"d_g{wp}")
                        tk(f"wgt{wp}").wrote(tld)
                        prog.wait("tensor", tld)
                        prog.wait("tensor", t_xn)
                        prog.wait_all("tensor", tk("psG").war())
                        for c in range(NCH):
                            tgm = prog.op("tensor", lambda e, c=c, wgt_b=wgt_b:
                                          e.matmul(
                                psG[:, :], wgt_b[:, c * 128:(c + 1) * 128],
                                xnoB[:, c * 512:(c + 1) * 512],
                                start=(c == 0), stop=(c == NCH - 1)),
                                inc="pe" if c == NCH - 1 else None)
                        tk("psG").wrote(tgm)
                        prog.wait_all("tensor", tk("psU").war())
                        for c in range(NCH):
                            tum = prog.op("tensor", lambda e, c=c, wut_b=wut_b:
                                          e.matmul(
                                psU[:, :], wut_b[:, c * 128:(c + 1) * 128],
                                xnoB[:, c * 512:(c + 1) * 512],
                                start=(c == 0), stop=(c == NCH - 1)),
                                inc="pe" if c == NCH - 1 else None)
                        tk("psU").wrote(tum)
                        tk(f"wgt{wp}").read(tum)
                        prog.wait("scalar", tgm)
                        prog.wait_all("scalar", tk("sgb").war())
                        tsl = prog.op("scalar", lambda e: e.activation(
                            sgb[:, :], psG[:, :], AF.Silu), inc="ac")
                        tk("psG").read(tsl)
                        tk("sgb").wrote(tsl)
                        prog.wait("vector", tsl)
                        prog.wait("vector", tum)
                        tac = prog.op("vector", lambda e, ic=ic, act_dst=act_dst:
                                      e.tensor_tensor(
                            act_dst[:, ic * 512:(ic + 1) * 512], sgb[:, :], psU[:, :],
                            AluOpType.mult), inc="dv")
                        tk("psU").read(tac)
                        tk("sgb").read(tac)
                        tk(actname).wrote(tac)
                    return tac

                t_act = gateup(NIC, wg_p, wu_p, act, "act")
                t_ract = gateup(NRIC, rg_p, ru_p, ract, "ract")

                for od in range(4):
                    for ic in range(NIC):
                        wp = ic % WDN
                        wdt_b = wdt[wp]
                        prog.wait_all("scalar", tk(f"wdt{wp}").war())
                        tdl = prog.op("scalar", lambda e, ic=ic, od=od, wdt_b=wdt_b:
                                      e.dma_start(
                            out=wdt_b[:, :],
                            in_=wd_b[ic * 128:(ic + 1) * 128, od * 512:(od + 1) * 512]),
                            inc16=f"d_d{wp}")
                        tk(f"wdt{wp}").wrote(tdl)
                        prog.wait("tensor", tdl)
                        if ic == 0:
                            prog.wait("tensor", t_act)
                            for t4 in range(4):
                                prog.wait_all("tensor", tk(f"psD{t4}").war())
                        for t4 in range(4):
                            tdm = prog.op("tensor", lambda e, ic=ic, t4=t4, wdt_b=wdt_b:
                                          e.matmul(
                                psD[t4][:, :],
                                act[:, ic * 512 + t4 * 128: ic * 512 + t4 * 128 + 128],
                                wdt_b[:, :],
                                start=(ic == 0), stop=(ic == NIC - 1)),
                                inc="pe" if t4 == 3 else None)
                            tk(f"psD{t4}").wrote(tdm)
                            tk("act").read(tdm)
                        tk(f"wdt{wp}").read(tdm)
                    prog.wait("vector", tdm)
                    prog.wait_all("vector", tk("acc").war())
                    for t4 in range(4):
                        tevc = prog.op("vector", lambda e, t4=t4, od=od: e.tensor_copy(
                            acc[:, t4 * D + od * 512: t4 * D + od * 512 + 512],
                            psD[t4][:, :]), inc="dv")
                        tk(f"psD{t4}").read(tevc)
                    tk("acc").wrote(tevc)

                for od in range(4):
                    for ic in range(NRIC):
                        wp = ic % WDN
                        wdt_b = wdt[wp]
                        prog.wait_all("scalar", tk(f"wdt{wp}").war())
                        tdl = prog.op("scalar", lambda e, ic=ic, od=od, wdt_b=wdt_b:
                                      e.dma_start(
                            out=wdt_b[:, :],
                            in_=rd_b[ic * 128:(ic + 1) * 128, od * 512:(od + 1) * 512]),
                            inc16=f"d_d{wp}")
                        tk(f"wdt{wp}").wrote(tdl)
                        prog.wait("tensor", tdl)
                        if ic == 0:
                            prog.wait("tensor", t_ract)
                            for t4 in range(4):
                                prog.wait_all("tensor", tk(f"psD{t4}").war())
                        for t4 in range(4):
                            tdm = prog.op("tensor", lambda e, ic=ic, t4=t4, wdt_b=wdt_b:
                                          e.matmul(
                                psD[t4][:, :],
                                ract[:, ic * 512 + t4 * 128: ic * 512 + t4 * 128 + 128],
                                wdt_b[:, :],
                                start=(ic == 0), stop=(ic == NRIC - 1)),
                                inc="pe" if t4 == 3 else None)
                            tk(f"psD{t4}").wrote(tdm)
                            tk("ract").read(tdm)
                        tk(f"wdt{wp}").read(tdm)
                    prog.wait("vector", tdm)
                    for t4 in range(4):
                        prog.wait_all("vector", tk(f"comb{t4}").raw())
                        prog.wait_all("vector", tk("sgb2").war())
                        tsc = prog.op("vector", lambda e, t4=t4: e.tensor_scalar(
                            sgb2[:, :], psD[t4][:, :], comb[:, t4:t4 + 1], None,
                            AluOpType.mult), inc="dv")
                        tk(f"psD{t4}").read(tsc)
                        tk("sgb2").wrote(tsc)
                        prog.wait("vector", tsc)
                        prog.wait_all("vector", tk("acc").raw())
                        tadd = prog.op("vector", lambda e, t4=t4, od=od:
                                       e.tensor_tensor(
                            acc[:, t4 * D + od * 512: t4 * D + od * 512 + 512],
                            acc[:, t4 * D + od * 512: t4 * D + od * 512 + 512],
                            sgb2[:, :], AluOpType.add), inc="dv")
                        tk("sgb2").read(tadd)
                        tk("acc").wrote(tadd)

                prog.wait_all("sync", tk("afm").war())
                for c in range(NCH):
                    tal = prog.op("sync", lambda e, c=c, blk=blk: e.dma_start(
                        out=afm[:, c * 512:(c + 1) * 512],
                        in_=attn_d[c * 128:(c + 1) * 128,
                                   blk * 512:(blk + 1) * 512]), inc16="d_am")
                tk("afm").wrote(tal)
                t_afm = prog.now("d_am")

                for od in range(4):
                    for c in range(NCH):
                        wp = c % WON
                        wot_b = wot[wp]
                        prog.wait_all("scalar", tk(f"wot{wp}").war())
                        twl = prog.op("scalar", lambda e, c=c, od=od, wot_b=wot_b:
                                      e.dma_start(
                            out=wot_b[:, :],
                            in_=wo_b[c * 128:(c + 1) * 128, od * 512:(od + 1) * 512]),
                            inc16=f"d_o{wp}")
                        tk(f"wot{wp}").wrote(twl)
                        prog.wait("tensor", twl)
                        if c == 0:
                            prog.wait("tensor", t_afm)
                            for t4 in range(4):
                                prog.wait_all("tensor", tk(f"psD{t4}").war())
                        for t4 in range(4):
                            twm = prog.op("tensor", lambda e, c=c, t4=t4, wot_b=wot_b:
                                          e.matmul(
                                psD[t4][:, :],
                                afm[:, c * 512 + t4 * 128:
                                    c * 512 + t4 * 128 + 128],
                                wot_b[:, :],
                                start=(c == 0), stop=(c == NCH - 1)),
                                inc="pe" if t4 == 3 else None)
                            tk(f"psD{t4}").wrote(twm)
                            tk("afm").read(twm)
                        tk(f"wot{wp}").read(twm)
                    prog.wait("vector", twm)
                    prog.wait_all("vector", tk("acc").raw())
                    for t4 in range(4):
                        tadd = prog.op("vector", lambda e, t4=t4, od=od:
                                       e.tensor_tensor(
                            acc[:, t4 * D + od * 512: t4 * D + od * 512 + 512],
                            acc[:, t4 * D + od * 512: t4 * D + od * 512 + 512],
                            psD[t4][:, :], AluOpType.add), inc="dv")
                        tk(f"psD{t4}").read(tadd)
                    tk("acc").wrote(tadd)

                for t4 in range(4):
                    prog.wait_all("sync", tk("xob").war())
                    txl = prog.op("sync", lambda e, t4=t4, blk=blk: e.dma_start(
                        out=xob[:, :],
                        in_=xo[blk * 512 + t4 * 128: blk * 512 + t4 * 128 + 128, :]),
                        inc16="d_xo")
                    tk("xob").wrote(txl)
                    prog.wait("vector", txl)
                    prog.wait_all("vector", tk("acc").raw())
                    for od in range(4):
                        op_ = od % 2
                        prog.wait_all("vector", tk(f"outb{op_}").war())
                        tfin = prog.op("vector", lambda e, t4=t4, od=od, op_=op_:
                                       e.tensor_tensor(
                            outb[op_][:, :],
                            acc[:, t4 * D + od * 512: t4 * D + od * 512 + 512],
                            xob[:, od * 512:(od + 1) * 512],
                            AluOpType.add), inc="dv")
                        tk("xob").read(tfin)
                        tk("acc").read(tfin)
                        tk(f"outb{op_}").wrote(tfin)
                        prog.wait("scalar", tfin)
                        tyo = prog.op("scalar", lambda e, t4=t4, blk=blk, od=od,
                                      op_=op_: e.dma_start(
                            out=y[blk * 512 + t4 * 128: blk * 512 + t4 * 128 + 128,
                                  od * 512:(od + 1) * 512],
                            in_=outb[op_][:, :]), inc16=f"d_y{op_}")
                        tk(f"outb{op_}").read(tyo)

        prog.wait("sync", prog.now("d_y0"))
        prog.wait("sync", prog.now("d_y1"))
        if debug:
            prog.wait("sync", prog.now("d_dbg"))

        block = octx.enter_context(nc.Block())
        prog.emit(block)

    return nc



def _host_tables(cfg):
    S = cfg["S"]
    t = np.arange(S, dtype=np.float32)
    j = np.arange(0, 128, 2, dtype=np.float32)
    inv = (1.0 / (10000.0 ** (j / 128.0))).astype(np.float32)
    fr = t[:, None] * inv[None, :]
    cos = np.cos(fr).astype(np.float32)
    sin = np.sin(fr).astype(np.float32)
    masks = np.zeros((4, 16, 128, 512), dtype=ml_dtypes.bfloat16)
    for r in range(4):
        z = np.arange(16)[:, None, None]
        s = np.arange(128)[None, :, None]
        col = np.arange(512)[None, None, :]
        masks[r] = (128 * z + s <= 4 * col + r).astype(ml_dtypes.bfloat16)
    return cos, sin, masks


def make_in_maps(cfg, x, Wq, Wk, Wv, Wo, router_w, sh_gate, sh_up, sh_down,
                 r_gate, r_up, r_down):
    S = cfg["S"]
    cos, sin, masks = _host_tables(cfg)
    cosT = np.ascontiguousarray(cos.T)
    sinT = np.ascontiguousarray(sin.T)
    ident = np.eye(128, dtype=ml_dtypes.bfloat16)
    com = {
        "wq": np.ascontiguousarray(np.asarray(Wq, np.float32)),
        "wk": np.ascontiguousarray(np.asarray(Wk, np.float32)),
        "wv": np.ascontiguousarray(np.asarray(Wv, np.float32)),
        "wo": np.ascontiguousarray(np.asarray(Wo, np.float32)),
        "wg": np.ascontiguousarray(np.asarray(sh_gate, np.float32).T),
        "wu": np.ascontiguousarray(np.asarray(sh_up, np.float32).T),
        "wd": np.ascontiguousarray(np.asarray(sh_down, np.float32).T),
        "rg": np.ascontiguousarray(np.asarray(r_gate, np.float32).T),
        "ru": np.ascontiguousarray(np.asarray(r_up, np.float32).T),
        "rd": np.ascontiguousarray(np.asarray(r_down, np.float32).T),
        "rw": np.ascontiguousarray(np.asarray(router_w, np.float32).T),
        "cosf": cosT, "sinf": sinT, "ident": ident,
    }
    in_maps = []
    for c in range(8):
        b, r = c // 4, c % 4
        m = dict(com)
        m["xb"] = np.ascontiguousarray(x[b])
        m["xo"] = np.ascontiguousarray(x[b][r::4])
        m["coso"] = np.ascontiguousarray(cosT[:, r::4])
        m["sino"] = np.ascontiguousarray(sinT[:, r::4])
        m["maskt"] = np.ascontiguousarray(masks[r])
        in_maps.append(m)
    return in_maps


def kernel(L0_attn_x, L0_attn_Wq, L0_attn_Wk, L0_attn_Wv, L0_attn_Wo,
           L0_ffn_router_weight, L0_ffn_sh_gate, L0_ffn_sh_up, L0_ffn_sh_down,
           L0_ffn_r_gate, L0_ffn_r_up, L0_ffn_r_down):
    cfg = default_cfg()
    S, D = cfg["S"], cfg["D"]
    x = np.ascontiguousarray(np.asarray(L0_attn_x, dtype=np.float32))
    in_maps = make_in_maps(cfg, x, L0_attn_Wq, L0_attn_Wk, L0_attn_Wv, L0_attn_Wo,
                           L0_ffn_router_weight, L0_ffn_sh_gate, L0_ffn_sh_up,
                           L0_ffn_sh_down, L0_ffn_r_gate, L0_ffn_r_up, L0_ffn_r_down)
    nc = build(cfg)
    res = run_bass_kernel_spmd(nc, in_maps, core_ids=list(range(8)))
    out = np.empty((2, S, D), dtype=np.float32)
    for c in range(8):
        b, r = c // 4, c % 4
        out[b, r::4] = res.results[c]["y"]
    return out


# version 15
# speedup vs baseline: 1.1099x; 1.0141x over previous
import math
import numpy as np
import ml_dtypes
import concourse.bass as bass
import concourse.mybir as mybir
from concourse.bass_utils import run_bass_kernel_spmd
from concourse.alu_op_type import AluOpType
from contextlib import ExitStack

F32 = mybir.dt.float32
BF16 = mybir.dt.bfloat16
AF = mybir.ActivationFunctionType
AX = mybir.AxisListType.X

ENGINES = ("tensor", "vector", "scalar", "gpsimd", "sync")


class Prog:

    def __init__(self, nc, ctx):
        self.nc = nc
        self.ctx = ctx
        self.items = {e: [] for e in ENGINES}
        self.sems = {}
        self.cnt = {}

    def sem(self, name):
        if name not in self.sems:
            self.sems[name] = self.ctx.enter_context(self.nc.semaphore(name))
            self.cnt[name] = 0
        return name

    def op(self, eng, fn, inc=None, inc16=None):
        tok = None
        incs = []
        if inc is not None:
            self.sem(inc)
            self.cnt[inc] += 1
            incs.append((inc, 1))
            tok = (inc, self.cnt[inc])
        if inc16 is not None:
            self.sem(inc16)
            self.cnt[inc16] += 16
            incs.append((inc16, 16))
            tok = (inc16, self.cnt[inc16])
        self.items[eng].append(("op", fn, incs))
        return tok

    def wait(self, eng, tok):
        if tok is None:
            return
        sem, val = tok
        self.sem(sem)
        if val > 0:
            self.items[eng].append(("wait", sem, val))

    def wait_all(self, eng, toks):
        best = {}
        for t in toks:
            if t is None:
                continue
            s, v = t
            best[s] = max(best.get(s, 0), v)
        for s, v in sorted(best.items()):
            self.wait(eng, (s, v))

    def now(self, sem):
        self.sem(sem)
        return (sem, self.cnt[sem])

    def barrier(self, name, dma_sems=(), engines=("tensor", "vector", "scalar")):
        toks = [self.now(s) for s in dma_sems if s in self.cnt]
        for e in engines:
            toks.append(self.op(e, lambda eg: eg.drain(), inc=self.sem(name)))
        for e in ENGINES:
            self.wait_all(e, toks)

    def emit(self, block):
        for eng in ENGINES:
            dec = getattr(block, eng)
            items = self.items[eng]

            def make(items):
                def _(e):
                    for it in items:
                        if it[0] == "wait":
                            e.wait_ge(self.sems[it[1]], it[2])
                        else:
                            inst = it[1](e)
                            for s, k in it[2]:
                                inst.then_inc(self.sems[s], k)
                return _
            dec(make(items))


class Tok:
    def __init__(self):
        self.w = []
        self.r = []

    def wrote(self, tok):
        self.w = [tok] if tok is not None else []
        self.r = []

    def read(self, tok):
        if tok is not None:
            self.r.append(tok)

    def raw(self):
        return list(self.w)

    def war(self):
        return list(self.w) + list(self.r)




def _align_up(x, a):
    return (x + a - 1) & ~(a - 1)


class SbArena:

    def __init__(self, apf, cap_bytes):
        self.apf = apf
        self.apb = apf.bitcast(BF16)
        self.cap = cap_bytes
        self.off = 0
        self.peak = 0

    def alloc(self, parts, cols, dtype):
        sz = cols * (4 if dtype is F32 else 2)
        off = _align_up(self.off, 64)
        self.off = off + sz
        self.peak = max(self.peak, self.off)
        assert self.off <= self.cap, (self.off, self.cap)
        if dtype is F32:
            return self.apf[0:parts, off // 4: off // 4 + cols]
        return self.apb[0:parts, off // 2: off // 2 + cols]

    def mark(self):
        return self.off

    def reset(self, m):
        self.off = m


def strided(slice_ap, extra, dims):
    part = list(slice_ap.ap[0])
    return bass.AP(slice_ap.tensor, slice_ap.offset + extra,
                   [part] + [list(d) for d in dims])


def default_cfg():
    return dict(S=8192, D=2048, INTER=8192, RINTER=1024)


def build(cfg, debug=False):
    S = cfg["S"]
    D = cfg["D"]
    INTER = cfg["INTER"]
    RINTER = cfg["RINTER"]
    NCH = D // 128
    NG = S // 512
    NO = S // 4
    NOG = NO // 512
    NIC = INTER // 128
    NRIC = RINTER // 128
    SCALE = 1.0 / math.sqrt(128.0)
    ESHIFT = -20.0
    VW = (S // 128) * 129

    nc = bass.Bass("TRN2", target_bir_lowering=False)

    xb = nc.dram_tensor("xb", [S, D], F32, kind="ExternalInput").ap()
    xo = nc.dram_tensor("xo", [NO, D], F32, kind="ExternalInput").ap()
    wq = nc.dram_tensor("wq", [D, 2048], F32, kind="ExternalInput").ap()
    wk = nc.dram_tensor("wk", [D, 512], F32, kind="ExternalInput").ap()
    wv = nc.dram_tensor("wv", [D, 512], F32, kind="ExternalInput").ap()
    wo = nc.dram_tensor("wo", [2048, D], F32, kind="ExternalInput").ap()
    wg = nc.dram_tensor("wg", [D, INTER], F32, kind="ExternalInput").ap()
    wu = nc.dram_tensor("wu", [D, INTER], F32, kind="ExternalInput").ap()
    wd = nc.dram_tensor("wd", [INTER, D], F32, kind="ExternalInput").ap()
    rg = nc.dram_tensor("rg", [D, RINTER], F32, kind="ExternalInput").ap()
    ru = nc.dram_tensor("ru", [D, RINTER], F32, kind="ExternalInput").ap()
    rd = nc.dram_tensor("rd", [RINTER, D], F32, kind="ExternalInput").ap()
    rw = nc.dram_tensor("rw", [D, 8], F32, kind="ExternalInput").ap()
    cosf = nc.dram_tensor("cosf", [64, S], F32, kind="ExternalInput").ap()
    sinf = nc.dram_tensor("sinf", [64, S], F32, kind="ExternalInput").ap()
    coso = nc.dram_tensor("coso", [64, NO], F32, kind="ExternalInput").ap()
    sino = nc.dram_tensor("sino", [64, NO], F32, kind="ExternalInput").ap()
    maskt = nc.dram_tensor("maskt", [16, 128, 512], BF16, kind="ExternalInput").ap()
    ident = nc.dram_tensor("ident", [128, 128], BF16, kind="ExternalInput").ap()
    y = nc.dram_tensor("y", [NO, D], F32, kind="ExternalOutput").ap()

    wq_b = nc.dram_tensor("wq_b", [16, 128, D], BF16, kind="Internal").ap()
    wo_b = nc.dram_tensor("wo_b", [2048, D], BF16, kind="Internal").ap()
    wg_p = nc.dram_tensor("wg_p", [NIC, 128, NCH * 128], BF16, kind="Internal").ap()
    wu_p = nc.dram_tensor("wu_p", [NIC, 128, NCH * 128], BF16, kind="Internal").ap()
    rg_p = nc.dram_tensor("rg_p", [NRIC, 128, NCH * 128], BF16, kind="Internal").ap()
    ru_p = nc.dram_tensor("ru_p", [NRIC, 128, NCH * 128], BF16, kind="Internal").ap()
    wd_b = nc.dram_tensor("wd_b", [INTER, D], BF16, kind="Internal").ap()
    rd_b = nc.dram_tensor("rd_b", [RINTER, D], BF16, kind="Internal").ap()
    k_d = nc.dram_tensor("k_d", [4, 128, S], BF16, kind="Internal").ap()
    v_d = nc.dram_tensor("v_d", [4, 128, VW], BF16, kind="Internal").ap()
    q_d = nc.dram_tensor("q_d", [16, 128, NO], BF16, kind="Internal").ap()
    xn_d = nc.dram_tensor("xn_d", [NCH, 128, NO], BF16, kind="Internal").ap()
    attn_d = nc.dram_tensor("attn_d", [2048, NO], BF16, kind="Internal").ap()

    dbg = {}
    if debug:
        for name, t in [("d_k", k_d), ("d_v", v_d), ("d_q", q_d),
                        ("d_xn", xn_d), ("d_attn", attn_d)]:
            dbg[name] = nc.dram_tensor(name, list(t.shape), BF16,
                                       kind="ExternalOutput").ap()

    with ExitStack() as octx:
        prog = Prog(nc, octx)
        ARENA_COLS = 49152
        arena_t = octx.enter_context(nc.sbuf_tensor("arena", [128, ARENA_COLS], F32))
        A = SbArena(arena_t.ap(), ARENA_COLS * 4)
        pa = [octx.enter_context(nc.psum_tensor(f"pa{i}", [128, 1024], F32))
              for i in range(4)]
        paf = [p.ap() for p in pa]
        pab = [p.ap().bitcast(BF16) for p in pa]

        idb = A.alloc(128, 128, BF16)
        c_scale = A.alloc(128, 1, F32)
        c_shift = A.alloc(128, 1, F32)
        m_persist = A.mark()
        pst_in = [A.alloc(128, 2048, F32) for i in range(2)]
        pst_out = [A.alloc(128, 2048, BF16) for i in range(2)]
        m_prep = A.mark()

        T = {}
        def tk(name):
            if name not in T:
                T[name] = Tok()
            return T[name]

        t_id = prog.op("sync", lambda e: e.dma_start(out=idb[:, :], in_=ident),
                       inc16="d_cst")
        prog.op("vector", lambda e: e.memset(c_scale[:, :], SCALE))
        t_cst = prog.op("vector", lambda e: e.memset(c_shift[:, :], ESHIFT), inc="dv")

        prep_jobs = []

        def prep_matrix(src, dst, nrows, ncols, out_ap_fn=None):
            for c in range(nrows // 128):
                nq = (ncols + 2047) // 2048
                for qq in range(nq):
                    w = min(2048, ncols - qq * 2048)
                    if out_ap_fn is None:
                        dst_ap = dst[c * 128:(c + 1) * 128, qq * 2048:qq * 2048 + w]
                    else:
                        dst_ap = out_ap_fn(c, qq, w)
                    src_ap = src[c * 128:(c + 1) * 128, qq * 2048:qq * 2048 + w]
                    prep_jobs.append((src_ap, dst_ap, w))

        def pump_prep(n):
            for _ in range(n):
                if pump_prep.i >= len(prep_jobs):
                    return
                src_ap, dst_ap, w = prep_jobs[pump_prep.i]
                pump_prep.i += 1
                par = pump_prep.i % 2
                sin = pst_in[par]
                sout = pst_out[par]
                prog.wait_all("gpsimd", tk(f"pstin{par}").war())
                tin = prog.op(
                    "gpsimd",
                    lambda e, src_ap=src_ap, w=w, sin=sin: e.dma_start(
                        out=sin[:, 0:w], in_=src_ap),
                    inc16=f"g_in{par}")
                tk(f"pstin{par}").wrote(tin)
                prog.wait("gpsimd", tin)
                prog.wait_all("gpsimd", tk(f"pstout{par}").war())
                tcv = prog.op(
                    "gpsimd",
                    lambda e, w=w, sin=sin, sout=sout: e.tensor_copy(
                        sout[:, 0:w], sin[:, 0:w]),
                    inc="gp")
                tk(f"pstin{par}").read(tcv)
                tk(f"pstout{par}").wrote(tcv)
                prog.wait("sync", tcv)
                tout = prog.op(
                    "sync",
                    lambda e, dst_ap=dst_ap, w=w, sout=sout: e.dma_start(
                        out=dst_ap, in_=sout[:, 0:w]),
                    inc16=f"g_out{par}")
                tk(f"pstout{par}").read(tout)
        pump_prep.i = 0

        def packed_out(dstp):
            def fn(c, qq, w):
                ni = w // 128
                t = dstp[qq * 16:qq * 16 + ni, :, c * 128:(c + 1) * 128]
                return t.rearrange("i f d -> f i d")
            return fn

        prep_matrix(wq, None, D, 2048, out_ap_fn=packed_out(wq_b))
        t_prep_wq = (("g_out0", 8 * 16), ("g_out1", 8 * 16))
        prog.sem("g_out0")
        prog.sem("g_out1")
        prep_matrix(wo, wo_b, 2048, D)
        prep_matrix(wg, None, D, INTER, out_ap_fn=packed_out(wg_p))
        prep_matrix(wu, None, D, INTER, out_ap_fn=packed_out(wu_p))
        prep_matrix(rg, None, D, RINTER, out_ap_fn=packed_out(rg_p))
        prep_matrix(ru, None, D, RINTER, out_ap_fn=packed_out(ru_p))
        prep_matrix(wd, wd_b, INTER, D)
        prep_matrix(rd, rd_b, RINTER, D)

        if True:
            A.reset(m_prep)
            wkf = [A.alloc(128, 512, F32) for i in range(2)]
            wkb = A.alloc(128, NCH * 512, BF16)
            wvb = A.alloc(128, NCH * 512, BF16)
            LNP = 4
            xbuf = [A.alloc(128, D, F32) for i in range(LNP)]
            xnt = [A.alloc(128, D, BF16) for i in range(LNP)]
            xnfm = [A.alloc(128, NCH * 512, BF16) for i in range(2)]
            cosb = [A.alloc(64, 512, F32) for i in range(2)]
            sinb = [A.alloc(64, 512, F32) for i in range(2)]
            tmpA = A.alloc(128, 512, F32)
            tmpB = A.alloc(128, 512, F32)
            krop = [A.alloc(128, 512, BF16) for i in range(2)]
            vtmp = A.alloc(128, 512, BF16)
            vhat = [A.alloc(128, 516, BF16) for i in range(2)]
            stt = [A.alloc(128, 24, F32) for i in range(LNP)]
            mvb = [A.alloc(128, 2, F32) for i in range(LNP)]
            rstd = [A.alloc(128, 1, F32) for i in range(LNP)]
            negm = [A.alloc(128, 1, F32) for i in range(LNP)]
            nbias = [A.alloc(128, 1, F32) for i in range(LNP)]
            ptx = pab[0]
            pk = [paf[1][:, 0:512], paf[1][:, 512:1024]]
            pv = paf[2][:, 0:512]
            ptv = pab[2][:, 1024:2048]
            prope = [paf[3][:, 0:512], paf[3][:, 512:1024]]

            for mat, dstb in ((wk, wkb), (wv, wvb)):
                for c in range(NCH):
                    par = c % 2
                    prog.wait_all("sync", tk(f"wkf{par}").war())
                    tin = prog.op("sync", lambda e, mat=mat, c=c, par=par: e.dma_start(
                        out=wkf[par][:, :], in_=mat[c * 128:(c + 1) * 128, :]),
                        inc16=f"d_wk{par}")
                    tk(f"wkf{par}").wrote(tin)
                    prog.wait("vector", tin)
                    tcv = prog.op("vector", lambda e, dstb=dstb, c=c, par=par:
                                  e.tensor_copy(dstb[:, c * 512:(c + 1) * 512],
                                                wkf[par][:, :]), inc="dv")
                    tk(f"wkf{par}").read(tcv)
            t_wkv = prog.now("dv")

            for i in range(2):
                vh = vhat[i]
                ap1 = strided(vh, 128, [(129, 4)])
                prog.op("vector", lambda e, ap1=ap1: e.memset(ap1, 1.0), inc="dv")

            def ln_part1(src_rows, n):
                par = n % LNP
                xB, xT = xbuf[par], xnt[par]
                sT, mV, rS, nM, nB = stt[par], mvb[par], rstd[par], negm[par], nbias[par]
                prog.wait_all("sync", tk(f"xbuf{par}").war())
                tin = prog.op("sync", lambda e, src_rows=src_rows, xB=xB: e.dma_start(
                    out=xB[:, :], in_=src_rows), inc16=f"d_x{par}")
                tk(f"xbuf{par}").wrote(tin)
                prog.wait("vector", tin)
                prog.wait_all("vector", tk(f"lnchain{par}").war())
                for i in range(4):
                    lst = prog.op("vector", lambda e, i=i, xB=xB, sT=sT: e.bn_stats(
                        sT[:, 6 * i:6 * i + 6], xB[:, 512 * i:512 * i + 512]),
                        inc="dv" if i == 3 else None)
                tk(f"xbuf{par}").read(lst)
                prog.wait("vector", lst)
                tag = prog.op("vector", lambda e, sT=sT, mV=mV: e.bn_aggr(
                    mV[:, :], sT[:, :]), inc="dv")
                prog.wait("vector", tag)
                prog.op("vector", lambda e, mV=mV, nM=nM: e.tensor_scalar(
                    nM[:, :], mV[:, 0:1], -1.0, None, AluOpType.mult))
                tve = prog.op("vector", lambda e, mV=mV: e.tensor_scalar(
                    mV[:, 1:2], mV[:, 1:2], 1e-5, None, AluOpType.add), inc="dv")
                prog.wait("scalar", tve)
                tsq = prog.op("scalar", lambda e, mV=mV, rS=rS: e.activation(
                    rS[:, :], mV[:, 1:2], AF.Sqrt), inc="ac")
                prog.wait("vector", tsq)
                trc = prog.op("vector", lambda e, rS=rS: e.reciprocal(
                    rS[:, :], rS[:, :]), inc="dv")
                prog.wait("vector", trc)
                tnb = prog.op("vector", lambda e, nM=nM, rS=rS, nB=nB: e.tensor_tensor(
                    nB[:, :], nM[:, :], rS[:, :], AluOpType.mult), inc="dv")
                prog.wait("scalar", tnb)
                prog.wait_all("scalar", tk(f"xnt{par}").war())
                taf = prog.op("scalar", lambda e, xB=xB, xT=xT, rS=rS, nB=nB:
                              e.activation(
                    xT[:, :], xB[:, :], AF.Identity, bias=nB[:, :],
                    scale=rS[:, :]), inc="ac")
                tk(f"xbuf{par}").read(taf)
                tk(f"xnt{par}").wrote(taf)
                tk(f"lnchain{par}").wrote(taf)
                return par, taf

            def ln_part2(par, taf, dst_xnfm, s4):
                xT = xnt[par]
                prog.wait("tensor", taf)
                prog.wait("tensor", t_id)
                prog.wait_all("tensor", tk("ptx").war())
                for c in range(NCH):
                    ttr = prog.op("tensor", lambda e, c=c, xT=xT: e.transpose(
                        ptx[:, 128 * c:128 * c + 128], xT[:, 128 * c:128 * c + 128],
                        idb[:, :]), inc="pe" if c == NCH - 1 else None)
                tk(f"xnt{par}").read(ttr)
                tk("ptx").wrote(ttr)
                prog.wait("scalar", ttr)
                out_ap = strided(dst_xnfm, 128 * s4, [(512, NCH), (1, 128)])
                in_ap = strided(ptx, 0, [(128, NCH), (1, 128)])
                tev = prog.op("scalar", lambda e, out_ap=out_ap, in_ap=in_ap:
                              e.copy(out_ap, in_ap), inc="ac")
                tk("ptx").read(tev)
                return tev

            nsub = 0
            for g in range(NG + NOG):
                pump_prep(2)
                own = g >= NG
                gl = g - NG if own else g
                gpar = g % 2
                fmB = xnfm[gpar]
                prog.wait_all("scalar", tk(f"xnfm{gpar}").war())
                p1s = []
                for s4 in range(4):
                    r0 = gl * 512 + s4 * 128
                    rows = (xo if own else xb)[r0:r0 + 128, :]
                    p1s.append(ln_part1(rows, nsub))
                    nsub += 1
                evtoks = [ln_part2(par, taf, fmB, s4)
                          for s4, (par, taf) in enumerate(p1s)]
                tk(f"xnfm{gpar}").wrote(evtoks[-1])

                if own:
                    prog.wait_all("scalar", tk(f"xnfm{gpar}").raw())
                    for c in range(NCH):
                        txn = prog.op("scalar", lambda e, c=c, gl=gl, fmB=fmB:
                                      e.dma_start(
                            out=xn_d[c][:, gl * 512:(gl + 1) * 512],
                            in_=fmB[:, 512 * c:512 * c + 512]), inc16=f"d_no{gpar}")
                        tk(f"xnfm{gpar}").read(txn)
                    continue

                cpar = g % 2
                prog.wait_all("sync", tk(f"cosb{cpar}").war())
                prog.op("sync", lambda e, g=g, cpar=cpar: e.dma_start(
                    out=cosb[cpar][:, :], in_=cosf[:, g * 512:(g + 1) * 512]),
                    inc16=f"d_c{cpar}")
                tc2 = prog.op("sync", lambda e, g=g, cpar=cpar: e.dma_start(
                    out=sinb[cpar][:, :], in_=sinf[:, g * 512:(g + 1) * 512]),
                    inc16=f"d_c{cpar}")
                tk(f"cosb{cpar}").wrote(tc2)

                for kvh in range(4):
                    if cfg.get("skip_kv"):
                        break
                    kp = (g * 4 + kvh) % 2
                    prog.wait_all("tensor", tk(f"xnfm{gpar}").raw())
                    prog.wait_all("tensor", tk(f"pk{kp}").war())
                    prog.wait("tensor", t_wkv)
                    for c in range(NCH):
                        tmm = prog.op("tensor", lambda e, c=c, kvh=kvh, kp=kp, fmB=fmB:
                                      e.matmul(
                            pk[kp][:, :],
                            wkb[:, 512 * c + 128 * kvh:512 * c + 128 * kvh + 128],
                            fmB[:, 512 * c:512 * c + 512],
                            start=(c == 0), stop=(c == NCH - 1)),
                            inc="pe" if c == NCH - 1 else None)
                    tk(f"xnfm{gpar}").read(tmm)
                    tk(f"pk{kp}").wrote(tmm)
                    prog.wait("vector", tmm)
                    prog.wait_all("vector", tk(f"cosb{cpar}").raw())
                    prog.wait_all("vector", tk(f"krop{kp}").war())
                    prog.wait_all("vector", tk("ropetmp").war())
                    cb, sn, kr = cosb[cpar], sinb[cpar], krop[kp]
                    pkk = pk[kp]
                    prp = prope[kp]
                    prog.wait_all("vector", tk(f"prope{kp}").war())
                    prog.op("vector", lambda e, pkk=pkk, cb=cb: e.tensor_tensor(
                        tmpA[0:64, :], pkk[0:64, :], cb[:, :], AluOpType.mult))
                    t2m = prog.op("vector", lambda e, pkk=pkk, sn=sn, prp=prp:
                                  e.tensor_tensor(
                        prp[64:128, :], pkk[64:128, :], sn[:, :], AluOpType.mult),
                        inc="dv")
                    prog.wait("vector", t2m)
                    prog.op("vector", lambda e, kr=kr, prp=prp: e.tensor_tensor(
                        kr[0:64, :], tmpA[0:64, :], prp[64:128, :],
                        AluOpType.subtract))
                    prog.op("vector", lambda e, pkk=pkk, cb=cb: e.tensor_tensor(
                        tmpA[64:128, :], pkk[64:128, :], cb[:, :], AluOpType.mult))
                    t4m = prog.op("vector", lambda e, pkk=pkk, sn=sn, prp=prp:
                                  e.tensor_tensor(
                        prp[0:64, :], pkk[0:64, :], sn[:, :], AluOpType.mult),
                        inc="dv")
                    prog.wait("vector", t4m)
                    trp = prog.op("vector", lambda e, kr=kr, prp=prp: e.tensor_tensor(
                        kr[64:128, :], tmpA[64:128, :], prp[0:64, :], AluOpType.add),
                        inc="dv")
                    tk("ropetmp").wrote(trp)
                    tk(f"prope{kp}").wrote(trp)
                    tk(f"pk{kp}").read(trp)
                    tk(f"cosb{cpar}").read(trp)
                    tk(f"krop{kp}").wrote(trp)
                    prog.wait("scalar", trp)
                    tko = prog.op("scalar", lambda e, kvh=kvh, g=g, kr=kr: e.dma_start(
                        out=k_d[kvh][:, g * 512:(g + 1) * 512], in_=kr[:, :]),
                        inc16=f"d_ko{kp}")
                    tk(f"krop{kp}").read(tko)

                    if cfg.get("skip_v"):
                        continue
                    prog.wait_all("tensor", tk("pv").war())
                    for c in range(NCH):
                        tvm = prog.op("tensor", lambda e, c=c, kvh=kvh, fmB=fmB:
                                      e.matmul(
                            pv[:, :],
                            wvb[:, 512 * c + 128 * kvh:512 * c + 128 * kvh + 128],
                            fmB[:, 512 * c:512 * c + 512],
                            start=(c == 0), stop=(c == NCH - 1)),
                            inc="pe" if c == NCH - 1 else None)
                    tk(f"xnfm{gpar}").read(tvm)
                    tk("pv").wrote(tvm)
                    prog.wait("scalar", tvm)
                    prog.wait_all("scalar", tk("vtmp").war())
                    tve = prog.op("scalar", lambda e: e.copy(
                        vtmp[:, :], pv[:, :]), inc="ac")
                    tk("pv").read(tve)
                    tk("vtmp").wrote(tve)
                    prog.wait("tensor", tve)
                    prog.wait_all("tensor", tk("ptv").war())
                    for t in range(4):
                        ttv = prog.op("tensor", lambda e, t=t: e.transpose(
                            ptv[:, 128 * t:128 * t + 128], vtmp[:, 128 * t:128 * t + 128],
                            idb[:, :]), inc="pe" if t == 3 else None)
                    tk("vtmp").read(ttv)
                    tk("ptv").wrote(ttv)
                    vp = (g * 4 + kvh) % 2
                    vh = vhat[vp]
                    prog.wait("scalar", ttv)
                    prog.wait_all("scalar", tk(f"vhat{vp}").war())
                    out_ap = strided(vh, 0, [(129, 4), (1, 128)])
                    in_ap = strided(ptv, 0, [(128, 4), (1, 128)])
                    tvh = prog.op("scalar", lambda e, out_ap=out_ap, in_ap=in_ap:
                                  e.copy(out_ap, in_ap), inc="ac")
                    tk("ptv").read(tvh)
                    tk(f"vhat{vp}").wrote(tvh)
                    prog.wait("scalar", tvh)
                    tvo = prog.op("scalar", lambda e, kvh=kvh, g=g, vh=vh: e.dma_start(
                        out=v_d[kvh][:, g * 516:(g + 1) * 516], in_=vh[:, :]),
                        inc16=f"d_vo{vp}")
                    tk(f"vhat{vp}").read(tvo)

        prog.barrier("b_p1a", dma_sems=[
            "d_x0", "d_x1", "d_c0", "d_c1", "d_wk0", "d_wk1",
            "d_ko0", "d_ko1", "d_vo0", "d_vo1", "d_no0", "d_no1"])

        if True:
            A.reset(m_prep)
            xno = A.alloc(128, NCH * NO, BF16)
            wqt = [A.alloc(128, NCH * 128, BF16) for i in range(2)]
            qrop = [A.alloc(128, 512, BF16) for i in range(2)]
            cosoB = A.alloc(64, NO, F32)
            sinoB = A.alloc(64, NO, F32)
            tmpA2 = A.alloc(128, 512, F32)
            tmpB2 = A.alloc(128, 512, F32)
            pq = [paf[0][:, 0:512], paf[0][:, 512:1024]]
            prope2 = paf[1][:, 0:512]

            prog.op("sync", lambda e: e.dma_start(out=cosoB[:, :], in_=coso),
                    inc16="d_qc")
            prog.op("sync", lambda e: e.dma_start(out=sinoB[:, :], in_=sino),
                    inc16="d_qc")
            for c in range(NCH):
                prog.op("sync", lambda e, c=c: e.dma_start(
                    out=xno[:, c * NO:(c + 1) * NO], in_=xn_d[c][:, :]), inc16="d_qx")
            t_qx = prog.now("d_qx")

            pump_prep(max(0, 16 - pump_prep.i))
            prog.wait("sync", t_prep_wq[0])
            prog.wait("sync", t_prep_wq[1])
            for h in range(16):
                pump_prep(3)
                hp = h % 2
                prog.wait_all("sync", tk(f"wqt{hp}").war())
                twq = prog.op("sync", lambda e, h=h, hp=hp: e.dma_start(
                    out=wqt[hp][:, :], in_=wq_b[h]), inc16=f"d_wq{hp}")
                tk(f"wqt{hp}").wrote(twq)
                for tg in range(NOG):
                    qp = (h * NOG + tg) % 2
                    prog.wait("tensor", twq)
                    prog.wait("tensor", t_qx)
                    prog.wait_all("tensor", tk(f"pq{qp}").war())
                    for c in range(NCH):
                        tqm = prog.op("tensor", lambda e, c=c, tg=tg, qp=qp, hp=hp:
                                      e.matmul(
                            pq[qp][:, :], wqt[hp][:, c * 128:(c + 1) * 128],
                            xno[:, c * NO + tg * 512: c * NO + tg * 512 + 512],
                            start=(c == 0), stop=(c == NCH - 1)),
                            inc="pe" if c == NCH - 1 else None)
                    tk(f"wqt{hp}").read(tqm)
                    tk(f"pq{qp}").wrote(tqm)
                    prog.wait("vector", tqm)
                    prog.wait(eng="vector", tok=("d_qc", 32))
                    prog.wait_all("vector", tk(f"qrop{qp}").war())
                    prog.wait_all("vector", tk("ropetmp2").war())
                    qr = qrop[qp]
                    pqq = pq[qp]
                    cslc = cosoB[:, tg * 512:(tg + 1) * 512]
                    sslc = sinoB[:, tg * 512:(tg + 1) * 512]
                    prog.op("vector", lambda e, pqq=pqq, cslc=cslc: e.tensor_tensor(
                        tmpA2[0:64, :], pqq[0:64, :], cslc, AluOpType.mult))
                    t2m = prog.op("vector", lambda e, pqq=pqq, sslc=sslc:
                                  e.tensor_tensor(
                        prope2[64:128, :], pqq[64:128, :], sslc, AluOpType.mult),
                        inc="dv")
                    prog.wait("vector", t2m)
                    prog.op("vector", lambda e, qr=qr: e.tensor_tensor(
                        qr[0:64, :], tmpA2[0:64, :], prope2[64:128, :],
                        AluOpType.subtract))
                    prog.op("vector", lambda e, pqq=pqq, cslc=cslc: e.tensor_tensor(
                        tmpA2[64:128, :], pqq[64:128, :], cslc, AluOpType.mult))
                    t4m = prog.op("vector", lambda e, pqq=pqq, sslc=sslc:
                                  e.tensor_tensor(
                        prope2[0:64, :], pqq[0:64, :], sslc, AluOpType.mult), inc="dv")
                    prog.wait("vector", t4m)
                    trp = prog.op("vector", lambda e, qr=qr: e.tensor_tensor(
                        qr[64:128, :], tmpA2[64:128, :], prope2[0:64, :],
                        AluOpType.add), inc="dv")
                    tk("ropetmp2").wrote(trp)
                    tk(f"pq{qp}").read(trp)
                    tk(f"qrop{qp}").wrote(trp)
                    prog.wait("scalar", trp)
                    tqo = prog.op("scalar", lambda e, h=h, tg=tg, qr=qr: e.dma_start(
                        out=q_d[h][:, tg * 512:(tg + 1) * 512], in_=qr[:, :]),
                        inc16=f"d_qo{qp}")
                    tk(f"qrop{qp}").read(tqo)

        prog.barrier("b_p1c", dma_sems=["d_qc", "d_qx", "d_wq0", "d_wq1",
                                        "d_qo0", "d_qo1"])
        if debug:
            for src, name in [(k_d, "d_k"), (v_d, "d_v"), (q_d, "d_q"),
                              (xn_d, "d_xn")]:
                prog.op("sync", lambda e, src=src, name=name: e.dma_start(
                    out=dbg[name], in_=src), inc16="d_dbg")

        if True:
            A.reset(m_prep)
            ksb = [A.alloc(128, 2 * S, BF16) for i in range(2)]
            vsb = [A.alloc(128, 2 * VW, BF16) for i in range(2)]
            msb = A.alloc(128, 16 * 512, BF16)
            qt = [A.alloc(128, 512, BF16) for i in range(3)]
            esb = [A.alloc(128, 1024, BF16) for i in range(2)]
            atn = [A.alloc(128, 128, BF16) for i in range(4)]
            atn4 = [A.alloc(128, 512, BF16) for i in range(2)]
            rcp = A.alloc(128, 4, F32)
            psS = [paf[0], paf[1]]
            psAV = [paf[2][:, 0:512], paf[2][:, 512:1024],
                    paf[3][:, 0:512], paf[3][:, 512:1024]]
            psTV = [pab[2][:, 0:1024], pab[2][:, 1024:2048],
                    pab[3][:, 0:1024], pab[3][:, 1024:2048]]

            for z in range(16):
                t_m = prog.op("sync", lambda e, z=z: e.dma_start(
                    out=msb[:, z * 512:(z + 1) * 512], in_=maskt[z]), inc16="d_m")
            t_mask = t_m

            t_kv_tok = [None, None]
            for pp in range(2):
                for j in range(2):
                    prog.op("sync", lambda e, pp=pp, j=j: e.dma_start(
                        out=ksb[pp][:, j * S:(j + 1) * S], in_=k_d[2 * pp + j][:, :]),
                        inc16=f"d_kv{pp}")
                    prog.op("sync", lambda e, pp=pp, j=j: e.dma_start(
                        out=vsb[pp][:, j * VW:(j + 1) * VW], in_=v_d[2 * pp + j][:, :]),
                        inc16=f"d_kv{pp}")
                t_kv_tok[pp] = prog.now(f"d_kv{pp}")
            for p in range(2):
                t_kv = t_kv_tok[p]
                ksb_p = ksb[p]
                vsb_p = vsb[p]

                units = [(g, h) for g in range(NOG) for h in range(8)]
                if p == 0:
                    prog.wait_all("sync", tk("qt0").war())
                    tq0 = prog.op("sync", lambda e, p=p: e.dma_start(
                        out=qt[0][:, :], in_=q_d[8 * p][:, 0:512]), inc16="d_q0")
                    tk("qt0").wrote(tq0)
                    prog.q_tok = tq0
                    prog.q_next = 1
                for ui, (g, h) in enumerate(units):
                    u = p * NOG * 8 + g * 8 + h
                    up = u % 3
                    gh = 8 * p + h
                    kvloc = h // 4
                    NKB = 16 * g + 16
                    pump_prep(3)
                    tql = prog.q_tok
                    if ui + 1 < len(units):
                        g2, h2 = units[ui + 1]
                        gh2, up2 = 8 * p + h2, (u + 1) % 3
                    elif p == 0:
                        g2, h2 = units[0]
                        gh2, up2 = 8 + h2, (u + 1) % 3
                    else:
                        g2 = None
                    if g2 is not None:
                        prog.wait_all("sync", tk(f"qt{up2}").war())
                        tq2 = prog.op("sync", lambda e, gh2=gh2, g2=g2, up2=up2:
                                      e.dma_start(
                            out=qt[up2][:, :],
                            in_=q_d[gh2][:, g2 * 512:(g2 + 1) * 512]),
                            inc16=f"d_q{up2}")
                        tk(f"qt{up2}").wrote(tq2)
                        prog.q_tok = tq2
                    prog.wait("tensor", tql)
                    prog.wait("tensor", t_kv)
                    avlast = None
                    NKB2 = NKB // 2

                    def col0_of(kb):
                        z = kb - 16 * g
                        return 0 if z < 4 else (z // 4) * 128

                    def emit_S(kb2):
                        sp = kb2 % 2
                        prog.wait_all("tensor", tk(f"psS{sp}").war())
                        for j in range(2):
                            kb = 2 * kb2 + j
                            c0 = col0_of(kb)
                            tsm = prog.op("tensor", lambda e, sp=sp, j=j, kb=kb, c0=c0,
                                          kvloc=kvloc, up=up, ksb_p=ksb_p: e.matmul(
                                psS[sp][:, j * 512 + c0:(j + 1) * 512],
                                ksb_p[:, kvloc * S + kb * 128:
                                      kvloc * S + kb * 128 + 128],
                                qt[up][:, c0:512], start=True, stop=True),
                                inc="pe" if j == 1 else None)
                        tk(f"qt{up}").read(tsm)
                        tk(f"psS{sp}").wrote(tsm)
                        return tsm

                    s_tok = {0: emit_S(0)}
                    for kb2 in range(NKB2):
                        sp = kb2 % 2
                        tsm = s_tok.pop(kb2)
                        prog.wait("scalar", tsm)
                        prog.wait_all("scalar", tk(f"esb{sp}").war())
                        prog.wait("scalar", t_cst)
                        mtoks = [None, None]
                        c00 = col0_of(2 * kb2)
                        tex = prog.op("scalar", lambda e, sp=sp, c00=c00:
                                      e.activation(
                            esb[sp][:, c00:1024],
                            psS[sp][:, c00:1024], AF.Exp,
                            bias=c_shift[:, :], scale=c_scale[:, :]), inc="ac")
                        extoks = [tex, tex]
                        for j in range(2):
                            kb = 2 * kb2 + j
                            c0 = col0_of(kb)
                            z = kb - 16 * g
                            if z >= 0:
                                prog.wait("vector", tex)
                                prog.wait("vector", t_mask)
                                tmk = prog.op("vector", lambda e, sp=sp, j=j, z=z,
                                              c0=c0: e.tensor_tensor(
                                    esb[sp][:, j * 512 + c0:(j + 1) * 512],
                                    esb[sp][:, j * 512 + c0:(j + 1) * 512],
                                    msb[:, z * 512 + c0:(z + 1) * 512],
                                    AluOpType.mult), inc="dv")
                                mtoks[j] = tmk
                        tk(f"psS{sp}").read(extoks[1])
                        tk(f"esb{sp}").wrote(extoks[1])
                        for j in range(2):
                            if mtoks[j] is not None:
                                tk(f"esb{sp}").w.append(mtoks[j])
                        if kb2 + 1 < NKB2:
                            s_tok[kb2 + 1] = emit_S(kb2 + 1)
                        for j in range(2):
                            kb = 2 * kb2 + j
                            z = kb - 16 * g
                            qqmin = 0 if z < 4 else z // 4
                            prog.wait("tensor", extoks[j])
                            if mtoks[j] is not None:
                                prog.wait("tensor", mtoks[j])
                            for qq in range(qqmin, 4):
                                if kb == 0:
                                    prog.wait_all("tensor", tk(f"psAV{qq}").war())
                                last_kb = 16 * g + 4 * qq + 3
                                tav = prog.op("tensor", lambda e, sp=sp, j=j, qq=qq,
                                              kvloc=kvloc, kb=kb, last_kb=last_kb,
                                              vsb_p=vsb_p: e.matmul(
                                    psAV[qq][:, 0:129],
                                    esb[sp][:, j * 512 + qq * 128:
                                            j * 512 + qq * 128 + 128],
                                    vsb_p[:, kvloc * VW + kb * 129:
                                          kvloc * VW + kb * 129 + 129],
                                    start=(kb == 0), stop=(kb == last_kb)),
                                    inc="pe")
                                tk(f"psAV{qq}").wrote(tav)
                                tk(f"esb{sp}").read(tav)
                                avlast = tav
                    prog.wait("vector", avlast)
                    prog.wait_all("vector", tk("rcp").war())
                    for qq in range(4):
                        trr = prog.op("vector", lambda e, qq=qq: e.reciprocal(
                            rcp[:, qq:qq + 1], psAV[qq][:, 128:129]),
                            inc="dv" if qq == 3 else None)
                    tk("rcp").wrote(trr)
                    prog.wait("vector", trr)
                    a4 = atn4[u % 2]
                    prog.wait_all("vector", tk(f"atn4_{u % 2}").war())
                    ntoks = []
                    for qq in range(4):
                        prog.wait_all("vector", tk(f"atn{qq}").war())
                        tns = prog.op("vector", lambda e, qq=qq: e.tensor_scalar(
                            atn[qq][:, :], psAV[qq][:, 0:128], rcp[:, qq:qq + 1],
                            None, AluOpType.mult), inc="dv")
                        tk(f"psAV{qq}").read(tns)
                        tk("rcp").read(tns)
                        tk(f"atn{qq}").wrote(tns)
                        ntoks.append(tns)
                    for qq in range(4):
                        prog.wait("tensor", ntoks[qq])
                        prog.wait("tensor", t_id)
                        ttr = prog.op("tensor", lambda e, qq=qq: e.transpose(
                            psTV[qq][:, 0:128], atn[qq][:, :], idb[:, :]),
                            inc="pe")
                        tk(f"psAV{qq}").wrote(ttr)
                        tk(f"atn{qq}").read(ttr)
                        prog.wait("vector", ttr)
                        tev = prog.op("vector", lambda e, qq=qq, a4=a4:
                                      e.tensor_copy(
                            a4[:, qq * 128:(qq + 1) * 128], psTV[qq][:, 0:128]),
                            inc="dv")
                        tk(f"psAV{qq}").read(tev)
                        tk(f"atn4_{u % 2}").wrote(tev)
                    prog.wait("sync", tev)
                    tad = prog.op("sync", lambda e, g=g, gh=gh, a4=a4:
                                  e.dma_start(
                        out=attn_d[gh * 128:(gh + 1) * 128,
                                   g * 512:(g + 1) * 512],
                        in_=a4[:, :]), inc16=f"d_at{u % 2}")
                    tk(f"atn4_{u % 2}").read(tad)


        prog.barrier("b_p2", dma_sems=["d_q0", "d_q1", "d_kv0", "d_kv1", "d_m",
                                       "d_at0", "d_at1", "d_at2", "d_at3"])
        if debug:
            prog.op("sync", lambda e: e.dma_start(out=dbg["d_attn"], in_=attn_d),
                    inc16="d_dbg")

        pump_prep(len(prep_jobs))

        if True:
            A.reset(m_persist)
            xnoB = A.alloc(128, NCH * 512, BF16)
            act = A.alloc(128, NIC * 512, BF16)
            ract = A.alloc(128, NRIC * 512, BF16)
            acc = A.alloc(128, 4 * D, F32)
            WGN, WDN, WON = 3, 5, 5
            wgt = [A.alloc(128, NCH * 128, BF16) for i in range(WGN)]
            wut = [A.alloc(128, NCH * 128, BF16) for i in range(WGN)]
            wdt = [A.alloc(128, 512, BF16) for i in range(WDN)]
            wot = [A.alloc(128, 512, BF16) for i in range(WON)]
            afm = A.alloc(128, NCH * 512, BF16)
            xob = A.alloc(128, D, F32)
            outb = [A.alloc(128, 512, F32) for i in range(2)]
            sgb = A.alloc(128, 512, F32)
            sgb2 = A.alloc(128, 512, F32)
            rwb = A.alloc(128, NCH * 8, BF16)
            rwf = A.alloc(128, 8, F32)
            lgs = A.alloc(128, 8, F32)
            m1 = A.alloc(128, 1, F32)
            m2 = A.alloc(128, 1, F32)
            eqv = A.alloc(128, 8, F32)
            evv = A.alloc(128, 1, F32)
            svv = A.alloc(128, 1, F32)
            comb = A.alloc(128, 4, F32)
            psG = paf[0][:, 0:512]
            psU = paf[0][:, 512:1024]
            psD = [paf[1][:, 0:512], paf[1][:, 512:1024],
                   paf[2][:, 0:512], paf[2][:, 512:1024]]
            psR = paf[3][:, 512:1024]

            for _e in ("sync", "tensor", "vector", "scalar"):
                prog.wait(_e, prog.now("gp"))
                prog.wait(_e, prog.now("g_out0"))
                prog.wait(_e, prog.now("g_out1"))
                prog.wait(_e, prog.now("g_in0"))
                prog.wait(_e, prog.now("g_in1"))

            for c in range(NCH):
                prog.wait_all("sync", tk("rwf").war())
                tin = prog.op("sync", lambda e, c=c: e.dma_start(
                    out=rwf[:, :], in_=rw[c * 128:(c + 1) * 128, :]), inc16="d_rw")
                tk("rwf").wrote(tin)
                prog.wait("vector", tin)
                tcv = prog.op("vector", lambda e, c=c: e.tensor_copy(
                    rwb[:, c * 8:(c + 1) * 8], rwf[:, :]), inc="dv")
                tk("rwf").read(tcv)
            t_rwb = prog.now("dv")

            for blk in range(NOG):
                prog.wait_all("sync", tk("xnoB").war())
                for c in range(NCH):
                    txn = prog.op("sync", lambda e, c=c, blk=blk: e.dma_start(
                        out=xnoB[:, c * 512:(c + 1) * 512],
                        in_=xn_d[c][:, blk * 512:(blk + 1) * 512]), inc16="d_xb")
                tk("xnoB").wrote(txn)
                t_xn = prog.now("d_xb")

                for t4 in range(4):
                    prog.wait("tensor", t_xn)
                    prog.wait("tensor", t_rwb)
                    prog.wait_all("tensor", tk("psR").war())
                    for c in range(NCH):
                        trm = prog.op("tensor", lambda e, c=c, t4=t4: e.matmul(
                            psR[:, 0:8],
                            xnoB[:, c * 512 + t4 * 128: c * 512 + t4 * 128 + 128],
                            rwb[:, c * 8:(c + 1) * 8],
                            start=(c == 0), stop=(c == NCH - 1)),
                            inc="pe" if c == NCH - 1 else None)
                    tk("psR").wrote(trm)
                    tk("xnoB").read(trm)
                    prog.wait("vector", trm)
                    prog.wait_all("vector", tk("router").war())
                    tcp = prog.op("vector", lambda e: e.tensor_copy(
                        lgs[:, :], psR[:, 0:8]), inc="dv")
                    tk("psR").read(tcp)
                    prog.wait("vector", tcp)
                    tm1 = prog.op("vector", lambda e: e.reduce_max(
                        m1[:, :], lgs[:, :], axis=AX), inc="dv")
                    prog.wait("vector", tm1)
                    teq = prog.op("vector", lambda e: e.tensor_scalar(
                        eqv[:, :], lgs[:, :], m1[:, :], None, AluOpType.is_equal),
                        inc="dv")
                    prog.wait("vector", teq)
                    tms = prog.op("vector", lambda e: e.scalar_tensor_tensor(
                        eqv[:, :], eqv[:, :], -1e30, lgs[:, :],
                        AluOpType.mult, AluOpType.add), inc="dv")
                    prog.wait("vector", tms)
                    tm2 = prog.op("vector", lambda e: e.reduce_max(
                        m2[:, :], eqv[:, :], axis=AX), inc="dv")
                    prog.wait("vector", tm2)
                    tdd = prog.op("vector", lambda e: e.tensor_tensor(
                        m2[:, :], m2[:, :], m1[:, :], AluOpType.subtract), inc="dv")
                    prog.wait("scalar", tdd)
                    tev = prog.op("scalar", lambda e: e.activation(
                        evv[:, :], m2[:, :], AF.Exp), inc="ac")
                    prog.wait("vector", tev)
                    tsv = prog.op("vector", lambda e: e.tensor_scalar(
                        svv[:, :], evv[:, :], 1.0, None, AluOpType.add), inc="dv")
                    prog.wait("vector", tsv)
                    trc = prog.op("vector", lambda e: e.reciprocal(
                        svv[:, :], svv[:, :]), inc="dv")
                    prog.wait("vector", trc)
                    tcb = prog.op("vector", lambda e, t4=t4: e.scalar_tensor_tensor(
                        comb[:, t4:t4 + 1], evv[:, :], 1.0, svv[:, :],
                        AluOpType.add, AluOpType.mult), inc="dv")
                    tk("router").wrote(tcb)
                    tk(f"comb{t4}").wrote(tcb)

                def gateup(nic, gp_src, up_src, act_dst, actname):
                    prog.wait_all("vector", tk(actname).war())
                    for ic in range(nic):
                        wp = ic % WGN
                        wgt_b, wut_b = wgt[wp], wut[wp]
                        prog.wait_all("sync", tk(f"wgt{wp}").war())
                        prog.op("sync", lambda e, ic=ic, gp_src=gp_src, wgt_b=wgt_b:
                                e.dma_start(out=wgt_b, in_=gp_src[ic]),
                            inc16=f"d_g{wp}")
                        prog.op("sync", lambda e, ic=ic, up_src=up_src, wut_b=wut_b:
                                e.dma_start(out=wut_b, in_=up_src[ic]),
                            inc16=f"d_g{wp}")
                        tld = prog.now(f"d_g{wp}")
                        tk(f"wgt{wp}").wrote(tld)
                        prog.wait("tensor", tld)
                        prog.wait("tensor", t_xn)
                        prog.wait_all("tensor", tk("psG").war())
                        for c in range(NCH):
                            tgm = prog.op("tensor", lambda e, c=c, wgt_b=wgt_b:
                                          e.matmul(
                                psG[:, :], wgt_b[:, c * 128:(c + 1) * 128],
                                xnoB[:, c * 512:(c + 1) * 512],
                                start=(c == 0), stop=(c == NCH - 1)),
                                inc="pe" if c == NCH - 1 else None)
                        tk("psG").wrote(tgm)
                        prog.wait_all("tensor", tk("psU").war())
                        for c in range(NCH):
                            tum = prog.op("tensor", lambda e, c=c, wut_b=wut_b:
                                          e.matmul(
                                psU[:, :], wut_b[:, c * 128:(c + 1) * 128],
                                xnoB[:, c * 512:(c + 1) * 512],
                                start=(c == 0), stop=(c == NCH - 1)),
                                inc="pe" if c == NCH - 1 else None)
                        tk("psU").wrote(tum)
                        tk(f"wgt{wp}").read(tum)
                        prog.wait("scalar", tgm)
                        prog.wait_all("scalar", tk("sgb").war())
                        tsl = prog.op("scalar", lambda e: e.activation(
                            sgb[:, :], psG[:, :], AF.Silu), inc="ac")
                        tk("psG").read(tsl)
                        tk("sgb").wrote(tsl)
                        prog.wait("vector", tsl)
                        prog.wait("vector", tum)
                        tac = prog.op("vector", lambda e, ic=ic, act_dst=act_dst:
                                      e.tensor_tensor(
                            act_dst[:, ic * 512:(ic + 1) * 512], sgb[:, :], psU[:, :],
                            AluOpType.mult), inc="dv")
                        tk("psU").read(tac)
                        tk("sgb").read(tac)
                        tk(actname).wrote(tac)
                    return tac

                t_act = gateup(NIC, wg_p, wu_p, act, "act")
                t_ract = gateup(NRIC, rg_p, ru_p, ract, "ract")

                for od in range(4):
                    for ic in range(NIC):
                        wp = ic % WDN
                        wdt_b = wdt[wp]
                        prog.wait_all("scalar", tk(f"wdt{wp}").war())
                        tdl = prog.op("scalar", lambda e, ic=ic, od=od, wdt_b=wdt_b:
                                      e.dma_start(
                            out=wdt_b[:, :],
                            in_=wd_b[ic * 128:(ic + 1) * 128, od * 512:(od + 1) * 512]),
                            inc16=f"d_d{wp}")
                        tk(f"wdt{wp}").wrote(tdl)
                        prog.wait("tensor", tdl)
                        if ic == 0:
                            prog.wait("tensor", t_act)
                            for t4 in range(4):
                                prog.wait_all("tensor", tk(f"psD{t4}").war())
                        for t4 in range(4):
                            tdm = prog.op("tensor", lambda e, ic=ic, t4=t4, wdt_b=wdt_b:
                                          e.matmul(
                                psD[t4][:, :],
                                act[:, ic * 512 + t4 * 128: ic * 512 + t4 * 128 + 128],
                                wdt_b[:, :],
                                start=(ic == 0), stop=(ic == NIC - 1)),
                                inc="pe" if t4 == 3 else None)
                            tk(f"psD{t4}").wrote(tdm)
                            tk("act").read(tdm)
                        tk(f"wdt{wp}").read(tdm)
                    prog.wait("vector", tdm)
                    prog.wait_all("vector", tk("acc").war())
                    for t4 in range(4):
                        tevc = prog.op("vector", lambda e, t4=t4, od=od: e.tensor_copy(
                            acc[:, t4 * D + od * 512: t4 * D + od * 512 + 512],
                            psD[t4][:, :]), inc="dv")
                        tk(f"psD{t4}").read(tevc)
                    tk("acc").wrote(tevc)

                for od in range(4):
                    for ic in range(NRIC):
                        wp = ic % WDN
                        wdt_b = wdt[wp]
                        prog.wait_all("scalar", tk(f"wdt{wp}").war())
                        tdl = prog.op("scalar", lambda e, ic=ic, od=od, wdt_b=wdt_b:
                                      e.dma_start(
                            out=wdt_b[:, :],
                            in_=rd_b[ic * 128:(ic + 1) * 128, od * 512:(od + 1) * 512]),
                            inc16=f"d_d{wp}")
                        tk(f"wdt{wp}").wrote(tdl)
                        prog.wait("tensor", tdl)
                        if ic == 0:
                            prog.wait("tensor", t_ract)
                            for t4 in range(4):
                                prog.wait_all("tensor", tk(f"psD{t4}").war())
                        for t4 in range(4):
                            tdm = prog.op("tensor", lambda e, ic=ic, t4=t4, wdt_b=wdt_b:
                                          e.matmul(
                                psD[t4][:, :],
                                ract[:, ic * 512 + t4 * 128: ic * 512 + t4 * 128 + 128],
                                wdt_b[:, :],
                                start=(ic == 0), stop=(ic == NRIC - 1)),
                                inc="pe" if t4 == 3 else None)
                            tk(f"psD{t4}").wrote(tdm)
                            tk("ract").read(tdm)
                        tk(f"wdt{wp}").read(tdm)
                    prog.wait("vector", tdm)
                    for t4 in range(4):
                        prog.wait_all("vector", tk(f"comb{t4}").raw())
                        prog.wait_all("vector", tk("sgb2").war())
                        tsc = prog.op("vector", lambda e, t4=t4: e.tensor_scalar(
                            sgb2[:, :], psD[t4][:, :], comb[:, t4:t4 + 1], None,
                            AluOpType.mult), inc="dv")
                        tk(f"psD{t4}").read(tsc)
                        tk("sgb2").wrote(tsc)
                        prog.wait("vector", tsc)
                        prog.wait_all("vector", tk("acc").raw())
                        tadd = prog.op("vector", lambda e, t4=t4, od=od:
                                       e.tensor_tensor(
                            acc[:, t4 * D + od * 512: t4 * D + od * 512 + 512],
                            acc[:, t4 * D + od * 512: t4 * D + od * 512 + 512],
                            sgb2[:, :], AluOpType.add), inc="dv")
                        tk("sgb2").read(tadd)
                        tk("acc").wrote(tadd)

                prog.wait_all("sync", tk("afm").war())
                for c in range(NCH):
                    tal = prog.op("sync", lambda e, c=c, blk=blk: e.dma_start(
                        out=afm[:, c * 512:(c + 1) * 512],
                        in_=attn_d[c * 128:(c + 1) * 128,
                                   blk * 512:(blk + 1) * 512]), inc16="d_am")
                tk("afm").wrote(tal)
                t_afm = prog.now("d_am")

                for od in range(4):
                    for c in range(NCH):
                        wp = c % WON
                        wot_b = wot[wp]
                        prog.wait_all("scalar", tk(f"wot{wp}").war())
                        twl = prog.op("scalar", lambda e, c=c, od=od, wot_b=wot_b:
                                      e.dma_start(
                            out=wot_b[:, :],
                            in_=wo_b[c * 128:(c + 1) * 128, od * 512:(od + 1) * 512]),
                            inc16=f"d_o{wp}")
                        tk(f"wot{wp}").wrote(twl)
                        prog.wait("tensor", twl)
                        if c == 0:
                            prog.wait("tensor", t_afm)
                            for t4 in range(4):
                                prog.wait_all("tensor", tk(f"psD{t4}").war())
                        for t4 in range(4):
                            twm = prog.op("tensor", lambda e, c=c, t4=t4, wot_b=wot_b:
                                          e.matmul(
                                psD[t4][:, :],
                                afm[:, c * 512 + t4 * 128:
                                    c * 512 + t4 * 128 + 128],
                                wot_b[:, :],
                                start=(c == 0), stop=(c == NCH - 1)),
                                inc="pe" if t4 == 3 else None)
                            tk(f"psD{t4}").wrote(twm)
                            tk("afm").read(twm)
                        tk(f"wot{wp}").read(twm)
                    prog.wait("vector", twm)
                    prog.wait_all("vector", tk("acc").raw())
                    for t4 in range(4):
                        tadd = prog.op("vector", lambda e, t4=t4, od=od:
                                       e.tensor_tensor(
                            acc[:, t4 * D + od * 512: t4 * D + od * 512 + 512],
                            acc[:, t4 * D + od * 512: t4 * D + od * 512 + 512],
                            psD[t4][:, :], AluOpType.add), inc="dv")
                        tk(f"psD{t4}").read(tadd)
                    tk("acc").wrote(tadd)

                for t4 in range(4):
                    prog.wait_all("sync", tk("xob").war())
                    txl = prog.op("sync", lambda e, t4=t4, blk=blk: e.dma_start(
                        out=xob[:, :],
                        in_=xo[blk * 512 + t4 * 128: blk * 512 + t4 * 128 + 128, :]),
                        inc16="d_xo")
                    tk("xob").wrote(txl)
                    prog.wait("vector", txl)
                    prog.wait_all("vector", tk("acc").raw())
                    for od in range(4):
                        op_ = od % 2
                        prog.wait_all("vector", tk(f"outb{op_}").war())
                        tfin = prog.op("vector", lambda e, t4=t4, od=od, op_=op_:
                                       e.tensor_tensor(
                            outb[op_][:, :],
                            acc[:, t4 * D + od * 512: t4 * D + od * 512 + 512],
                            xob[:, od * 512:(od + 1) * 512],
                            AluOpType.add), inc="dv")
                        tk("xob").read(tfin)
                        tk("acc").read(tfin)
                        tk(f"outb{op_}").wrote(tfin)
                        prog.wait("scalar", tfin)
                        tyo = prog.op("scalar", lambda e, t4=t4, blk=blk, od=od,
                                      op_=op_: e.dma_start(
                            out=y[blk * 512 + t4 * 128: blk * 512 + t4 * 128 + 128,
                                  od * 512:(od + 1) * 512],
                            in_=outb[op_][:, :]), inc16=f"d_y{op_}")
                        tk(f"outb{op_}").read(tyo)

        prog.wait("sync", prog.now("d_y0"))
        prog.wait("sync", prog.now("d_y1"))
        if debug:
            prog.wait("sync", prog.now("d_dbg"))

        block = octx.enter_context(nc.Block())
        prog.emit(block)

    return nc



def _host_tables(cfg):
    S = cfg["S"]
    t = np.arange(S, dtype=np.float32)
    j = np.arange(0, 128, 2, dtype=np.float32)
    inv = (1.0 / (10000.0 ** (j / 128.0))).astype(np.float32)
    fr = t[:, None] * inv[None, :]
    cos = np.cos(fr).astype(np.float32)
    sin = np.sin(fr).astype(np.float32)
    masks = np.zeros((4, 16, 128, 512), dtype=ml_dtypes.bfloat16)
    for r in range(4):
        z = np.arange(16)[:, None, None]
        s = np.arange(128)[None, :, None]
        col = np.arange(512)[None, None, :]
        masks[r] = (128 * z + s <= 4 * col + r).astype(ml_dtypes.bfloat16)
    return cos, sin, masks


def make_in_maps(cfg, x, Wq, Wk, Wv, Wo, router_w, sh_gate, sh_up, sh_down,
                 r_gate, r_up, r_down):
    S = cfg["S"]
    cos, sin, masks = _host_tables(cfg)
    cosT = np.ascontiguousarray(cos.T)
    sinT = np.ascontiguousarray(sin.T)
    ident = np.eye(128, dtype=ml_dtypes.bfloat16)
    com = {
        "wq": np.ascontiguousarray(np.asarray(Wq, np.float32)),
        "wk": np.ascontiguousarray(np.asarray(Wk, np.float32)),
        "wv": np.ascontiguousarray(np.asarray(Wv, np.float32)),
        "wo": np.ascontiguousarray(np.asarray(Wo, np.float32)),
        "wg": np.ascontiguousarray(np.asarray(sh_gate, np.float32).T),
        "wu": np.ascontiguousarray(np.asarray(sh_up, np.float32).T),
        "wd": np.ascontiguousarray(np.asarray(sh_down, np.float32).T),
        "rg": np.ascontiguousarray(np.asarray(r_gate, np.float32).T),
        "ru": np.ascontiguousarray(np.asarray(r_up, np.float32).T),
        "rd": np.ascontiguousarray(np.asarray(r_down, np.float32).T),
        "rw": np.ascontiguousarray(np.asarray(router_w, np.float32).T),
        "cosf": cosT, "sinf": sinT, "ident": ident,
    }
    in_maps = []
    for c in range(8):
        b, r = c // 4, c % 4
        m = dict(com)
        m["xb"] = np.ascontiguousarray(x[b])
        m["xo"] = np.ascontiguousarray(x[b][r::4])
        m["coso"] = np.ascontiguousarray(cosT[:, r::4])
        m["sino"] = np.ascontiguousarray(sinT[:, r::4])
        m["maskt"] = np.ascontiguousarray(masks[r])
        in_maps.append(m)
    return in_maps


def kernel(L0_attn_x, L0_attn_Wq, L0_attn_Wk, L0_attn_Wv, L0_attn_Wo,
           L0_ffn_router_weight, L0_ffn_sh_gate, L0_ffn_sh_up, L0_ffn_sh_down,
           L0_ffn_r_gate, L0_ffn_r_up, L0_ffn_r_down):
    cfg = default_cfg()
    S, D = cfg["S"], cfg["D"]
    x = np.ascontiguousarray(np.asarray(L0_attn_x, dtype=np.float32))
    in_maps = make_in_maps(cfg, x, L0_attn_Wq, L0_attn_Wk, L0_attn_Wv, L0_attn_Wo,
                           L0_ffn_router_weight, L0_ffn_sh_gate, L0_ffn_sh_up,
                           L0_ffn_sh_down, L0_ffn_r_gate, L0_ffn_r_up, L0_ffn_r_down)
    nc = build(cfg)
    res = run_bass_kernel_spmd(nc, in_maps, core_ids=list(range(8)))
    out = np.empty((2, S, D), dtype=np.float32)
    for c in range(8):
        b, r = c // 4, c % 4
        out[b, r::4] = res.results[c]["y"]
    return out


# version 16 (final)
# speedup vs baseline: 1.1900x; 1.0722x over previous
import math
import numpy as np
import ml_dtypes
import concourse.bass as bass
import concourse.mybir as mybir
from concourse.bass_utils import run_bass_kernel_spmd
from concourse.alu_op_type import AluOpType
from contextlib import ExitStack

F32 = mybir.dt.float32
BF16 = mybir.dt.bfloat16
AF = mybir.ActivationFunctionType
AX = mybir.AxisListType.X

ENGINES = ("tensor", "vector", "scalar", "gpsimd", "sync")


class Prog:

    def __init__(self, nc, ctx):
        self.nc = nc
        self.ctx = ctx
        self.items = {e: [] for e in ENGINES}
        self.sems = {}
        self.cnt = {}

    def sem(self, name):
        if name not in self.sems:
            self.sems[name] = self.ctx.enter_context(self.nc.semaphore(name))
            self.cnt[name] = 0
        return name

    def op(self, eng, fn, inc=None, inc16=None):
        tok = None
        incs = []
        if inc is not None:
            self.sem(inc)
            self.cnt[inc] += 1
            incs.append((inc, 1))
            tok = (inc, self.cnt[inc])
        if inc16 is not None:
            self.sem(inc16)
            self.cnt[inc16] += 16
            incs.append((inc16, 16))
            tok = (inc16, self.cnt[inc16])
        self.items[eng].append(("op", fn, incs))
        return tok

    def wait(self, eng, tok):
        if tok is None:
            return
        sem, val = tok
        self.sem(sem)
        if val > 0:
            self.items[eng].append(("wait", sem, val))

    def wait_all(self, eng, toks):
        best = {}
        for t in toks:
            if t is None:
                continue
            s, v = t
            best[s] = max(best.get(s, 0), v)
        for s, v in sorted(best.items()):
            self.wait(eng, (s, v))

    def now(self, sem):
        self.sem(sem)
        return (sem, self.cnt[sem])

    def barrier(self, name, dma_sems=(), engines=("tensor", "vector", "scalar")):
        toks = [self.now(s) for s in dma_sems if s in self.cnt]
        for e in engines:
            toks.append(self.op(e, lambda eg: eg.drain(), inc=self.sem(name)))
        for e in ENGINES:
            self.wait_all(e, toks)

    def emit(self, block):
        for eng in ENGINES:
            dec = getattr(block, eng)
            items = self.items[eng]

            def make(items):
                def _(e):
                    for it in items:
                        if it[0] == "wait":
                            e.wait_ge(self.sems[it[1]], it[2])
                        else:
                            inst = it[1](e)
                            for s, k in it[2]:
                                inst.then_inc(self.sems[s], k)
                return _
            dec(make(items))


class Tok:
    def __init__(self):
        self.w = []
        self.r = []

    def wrote(self, tok):
        self.w = [tok] if tok is not None else []
        self.r = []

    def read(self, tok):
        if tok is not None:
            self.r.append(tok)

    def raw(self):
        return list(self.w)

    def war(self):
        return list(self.w) + list(self.r)




def _align_up(x, a):
    return (x + a - 1) & ~(a - 1)


class SbArena:

    def __init__(self, apf, cap_bytes):
        self.apf = apf
        self.apb = apf.bitcast(BF16)
        self.cap = cap_bytes
        self.off = 0
        self.peak = 0

    def alloc(self, parts, cols, dtype):
        sz = cols * (4 if dtype is F32 else 2)
        off = _align_up(self.off, 64)
        self.off = off + sz
        self.peak = max(self.peak, self.off)
        assert self.off <= self.cap, (self.off, self.cap)
        if dtype is F32:
            return self.apf[0:parts, off // 4: off // 4 + cols]
        return self.apb[0:parts, off // 2: off // 2 + cols]

    def mark(self):
        return self.off

    def reset(self, m):
        self.off = m


def strided(slice_ap, extra, dims):
    part = list(slice_ap.ap[0])
    return bass.AP(slice_ap.tensor, slice_ap.offset + extra,
                   [part] + [list(d) for d in dims])


def default_cfg():
    return dict(S=8192, D=2048, INTER=8192, RINTER=1024)


def build(cfg, debug=False):
    S = cfg["S"]
    D = cfg["D"]
    INTER = cfg["INTER"]
    RINTER = cfg["RINTER"]
    NCH = D // 128
    NG = S // 512
    NO = S // 4
    NOG = NO // 512
    NIC = INTER // 128
    NRIC = RINTER // 128
    SCALE = 1.0 / math.sqrt(128.0)
    ESHIFT = -20.0
    VW = (S // 128) * 129

    nc = bass.Bass("TRN2", target_bir_lowering=False)

    xb = nc.dram_tensor("xb", [S, D], F32, kind="ExternalInput").ap()
    xo = nc.dram_tensor("xo", [NO, D], F32, kind="ExternalInput").ap()
    wq = nc.dram_tensor("wq", [D, 2048], F32, kind="ExternalInput").ap()
    wk = nc.dram_tensor("wk", [D, 512], F32, kind="ExternalInput").ap()
    wv = nc.dram_tensor("wv", [D, 512], F32, kind="ExternalInput").ap()
    wo = nc.dram_tensor("wo", [2048, D], F32, kind="ExternalInput").ap()
    wg = nc.dram_tensor("wg", [D, INTER], F32, kind="ExternalInput").ap()
    wu = nc.dram_tensor("wu", [D, INTER], F32, kind="ExternalInput").ap()
    wd = nc.dram_tensor("wd", [INTER, D], F32, kind="ExternalInput").ap()
    rg = nc.dram_tensor("rg", [D, RINTER], F32, kind="ExternalInput").ap()
    ru = nc.dram_tensor("ru", [D, RINTER], F32, kind="ExternalInput").ap()
    rd = nc.dram_tensor("rd", [RINTER, D], F32, kind="ExternalInput").ap()
    rw = nc.dram_tensor("rw", [D, 8], F32, kind="ExternalInput").ap()
    cosf = nc.dram_tensor("cosf", [64, S], F32, kind="ExternalInput").ap()
    sinf = nc.dram_tensor("sinf", [64, S], F32, kind="ExternalInput").ap()
    coso = nc.dram_tensor("coso", [64, NO], F32, kind="ExternalInput").ap()
    sino = nc.dram_tensor("sino", [64, NO], F32, kind="ExternalInput").ap()
    maskt = nc.dram_tensor("maskt", [16, 128, 512], BF16, kind="ExternalInput").ap()
    ident = nc.dram_tensor("ident", [128, 128], BF16, kind="ExternalInput").ap()
    y = nc.dram_tensor("y", [NO, D], F32, kind="ExternalOutput").ap()

    wq_b = nc.dram_tensor("wq_b", [16, 128, D], BF16, kind="Internal").ap()
    wo_b = nc.dram_tensor("wo_b", [2048, D], BF16, kind="Internal").ap()
    wg_p = nc.dram_tensor("wg_p", [NIC, 128, NCH * 128], BF16, kind="Internal").ap()
    wu_p = nc.dram_tensor("wu_p", [NIC, 128, NCH * 128], BF16, kind="Internal").ap()
    rg_p = nc.dram_tensor("rg_p", [NRIC, 128, NCH * 128], BF16, kind="Internal").ap()
    ru_p = nc.dram_tensor("ru_p", [NRIC, 128, NCH * 128], BF16, kind="Internal").ap()
    wd_b = nc.dram_tensor("wd_b", [INTER, D], BF16, kind="Internal").ap()
    rd_b = nc.dram_tensor("rd_b", [RINTER, D], BF16, kind="Internal").ap()
    k_d = nc.dram_tensor("k_d", [4, 128, S], BF16, kind="Internal").ap()
    v_d = nc.dram_tensor("v_d", [4, 128, VW], BF16, kind="Internal").ap()
    q_d = nc.dram_tensor("q_d", [16, 128, NO], BF16, kind="Internal").ap()
    xn_d = nc.dram_tensor("xn_d", [NCH, 128, NO], BF16, kind="Internal").ap()
    attn_d = nc.dram_tensor("attn_d", [2048, NO], BF16, kind="Internal").ap()

    dbg = {}
    if debug:
        for name, t in [("d_k", k_d), ("d_v", v_d), ("d_q", q_d),
                        ("d_xn", xn_d), ("d_attn", attn_d)]:
            dbg[name] = nc.dram_tensor(name, list(t.shape), BF16,
                                       kind="ExternalOutput").ap()

    with ExitStack() as octx:
        prog = Prog(nc, octx)
        ARENA_COLS = 49152
        arena_t = octx.enter_context(nc.sbuf_tensor("arena", [128, ARENA_COLS], F32))
        A = SbArena(arena_t.ap(), ARENA_COLS * 4)
        pa = [octx.enter_context(nc.psum_tensor(f"pa{i}", [128, 1024], F32))
              for i in range(4)]
        paf = [p.ap() for p in pa]
        pab = [p.ap().bitcast(BF16) for p in pa]

        idb = A.alloc(128, 128, BF16)
        c_scale = A.alloc(128, 1, F32)
        c_shift = A.alloc(128, 1, F32)
        m_persist = A.mark()
        pst_in = [A.alloc(128, 2048, F32) for i in range(2)]
        pst_out = [A.alloc(128, 2048, BF16) for i in range(2)]
        m_prep = A.mark()

        T = {}
        def tk(name):
            if name not in T:
                T[name] = Tok()
            return T[name]

        t_id = prog.op("sync", lambda e: e.dma_start(out=idb[:, :], in_=ident),
                       inc16="d_cst")
        prog.op("vector", lambda e: e.memset(c_scale[:, :], SCALE))
        t_cst = prog.op("vector", lambda e: e.memset(c_shift[:, :], ESHIFT), inc="dv")

        prep_jobs = []

        def prep_matrix(src, dst, nrows, ncols, out_ap_fn=None):
            for c in range(nrows // 128):
                nq = (ncols + 2047) // 2048
                for qq in range(nq):
                    w = min(2048, ncols - qq * 2048)
                    if out_ap_fn is None:
                        dst_ap = dst[c * 128:(c + 1) * 128, qq * 2048:qq * 2048 + w]
                    else:
                        dst_ap = out_ap_fn(c, qq, w)
                    src_ap = src[c * 128:(c + 1) * 128, qq * 2048:qq * 2048 + w]
                    prep_jobs.append((src_ap, dst_ap, w))

        def pump_prep(n):
            for _ in range(n):
                if pump_prep.i >= len(prep_jobs):
                    return
                src_ap, dst_ap, w = prep_jobs[pump_prep.i]
                pump_prep.i += 1
                par = pump_prep.i % 2
                sin = pst_in[par]
                sout = pst_out[par]
                prog.wait_all("gpsimd", tk(f"pstin{par}").war())
                tin = prog.op(
                    "gpsimd",
                    lambda e, src_ap=src_ap, w=w, sin=sin: e.dma_start(
                        out=sin[:, 0:w], in_=src_ap),
                    inc16=f"g_in{par}")
                tk(f"pstin{par}").wrote(tin)
                prog.wait("gpsimd", tin)
                prog.wait_all("gpsimd", tk(f"pstout{par}").war())
                tcv = prog.op(
                    "gpsimd",
                    lambda e, w=w, sin=sin, sout=sout: e.tensor_copy(
                        sout[:, 0:w], sin[:, 0:w]),
                    inc="gp")
                tk(f"pstin{par}").read(tcv)
                tk(f"pstout{par}").wrote(tcv)
                prog.wait("sync", tcv)
                tout = prog.op(
                    "sync",
                    lambda e, dst_ap=dst_ap, w=w, sout=sout: e.dma_start(
                        out=dst_ap, in_=sout[:, 0:w]),
                    inc16=f"g_out{par}")
                tk(f"pstout{par}").read(tout)
        pump_prep.i = 0

        def packed_out(dstp):
            def fn(c, qq, w):
                ni = w // 128
                t = dstp[qq * 16:qq * 16 + ni, :, c * 128:(c + 1) * 128]
                return t.rearrange("i f d -> f i d")
            return fn

        prep_matrix(wq, None, D, 2048, out_ap_fn=packed_out(wq_b))
        t_prep_wq = (("g_out0", 8 * 16), ("g_out1", 8 * 16))
        prog.sem("g_out0")
        prog.sem("g_out1")
        prep_matrix(wo, wo_b, 2048, D)
        prep_matrix(wg, None, D, INTER, out_ap_fn=packed_out(wg_p))
        prep_matrix(wu, None, D, INTER, out_ap_fn=packed_out(wu_p))
        prep_matrix(rg, None, D, RINTER, out_ap_fn=packed_out(rg_p))
        prep_matrix(ru, None, D, RINTER, out_ap_fn=packed_out(ru_p))
        prep_matrix(wd, wd_b, INTER, D)
        prep_matrix(rd, rd_b, RINTER, D)

        if True:
            A.reset(m_prep)
            wkf = [A.alloc(128, 512, F32) for i in range(2)]
            wkb = A.alloc(128, NCH * 512, BF16)
            wvb = A.alloc(128, NCH * 512, BF16)
            LNP = 4
            xbuf = [A.alloc(128, D, F32) for i in range(LNP)]
            xnt = [A.alloc(128, D, BF16) for i in range(LNP)]
            xnfm = [A.alloc(128, NCH * 512, BF16) for i in range(2)]
            cosb = [A.alloc(64, 512, F32) for i in range(2)]
            sinb = [A.alloc(64, 512, F32) for i in range(2)]
            tmpA = A.alloc(128, 512, F32)
            tmpB = A.alloc(128, 512, F32)
            krop = [A.alloc(128, 512, BF16) for i in range(2)]
            vtmp = A.alloc(128, 512, BF16)
            vhat = [A.alloc(128, 516, BF16) for i in range(2)]
            stt = [A.alloc(128, 24, F32) for i in range(LNP)]
            mvb = [A.alloc(128, 2, F32) for i in range(LNP)]
            rstd = [A.alloc(128, 1, F32) for i in range(LNP)]
            negm = [A.alloc(128, 1, F32) for i in range(LNP)]
            nbias = [A.alloc(128, 1, F32) for i in range(LNP)]
            ptx = pab[0]
            pk = [paf[1][:, 0:512], paf[1][:, 512:1024]]
            pv = paf[2][:, 0:512]
            ptv = pab[2][:, 1024:2048]
            prope = [paf[3][:, 0:512], paf[3][:, 512:1024]]

            for mat, dstb in ((wk, wkb), (wv, wvb)):
                for c in range(NCH):
                    par = c % 2
                    prog.wait_all("sync", tk(f"wkf{par}").war())
                    tin = prog.op("sync", lambda e, mat=mat, c=c, par=par: e.dma_start(
                        out=wkf[par][:, :], in_=mat[c * 128:(c + 1) * 128, :]),
                        inc16=f"d_wk{par}")
                    tk(f"wkf{par}").wrote(tin)
                    prog.wait("vector", tin)
                    tcv = prog.op("vector", lambda e, dstb=dstb, c=c, par=par:
                                  e.tensor_copy(dstb[:, c * 512:(c + 1) * 512],
                                                wkf[par][:, :]), inc="dv")
                    tk(f"wkf{par}").read(tcv)
            t_wkv = prog.now("dv")

            for i in range(2):
                vh = vhat[i]
                ap1 = strided(vh, 128, [(129, 4)])
                prog.op("vector", lambda e, ap1=ap1: e.memset(ap1, 1.0), inc="dv")

            def ln_part1(src_rows, n):
                par = n % LNP
                xB, xT = xbuf[par], xnt[par]
                sT, mV, rS, nM, nB = stt[par], mvb[par], rstd[par], negm[par], nbias[par]
                prog.wait_all("sync", tk(f"xbuf{par}").war())
                tin = prog.op("sync", lambda e, src_rows=src_rows, xB=xB: e.dma_start(
                    out=xB[:, :], in_=src_rows), inc16=f"d_x{par}")
                tk(f"xbuf{par}").wrote(tin)
                prog.wait("vector", tin)
                prog.wait_all("vector", tk(f"lnchain{par}").war())
                for i in range(4):
                    lst = prog.op("vector", lambda e, i=i, xB=xB, sT=sT: e.bn_stats(
                        sT[:, 6 * i:6 * i + 6], xB[:, 512 * i:512 * i + 512]),
                        inc="dv" if i == 3 else None)
                tk(f"xbuf{par}").read(lst)
                prog.wait("vector", lst)
                tag = prog.op("vector", lambda e, sT=sT, mV=mV: e.bn_aggr(
                    mV[:, :], sT[:, :]), inc="dv")
                prog.wait("vector", tag)
                prog.op("vector", lambda e, mV=mV, nM=nM: e.tensor_scalar(
                    nM[:, :], mV[:, 0:1], -1.0, None, AluOpType.mult))
                tve = prog.op("vector", lambda e, mV=mV: e.tensor_scalar(
                    mV[:, 1:2], mV[:, 1:2], 1e-5, None, AluOpType.add), inc="dv")
                prog.wait("scalar", tve)
                tsq = prog.op("scalar", lambda e, mV=mV, rS=rS: e.activation(
                    rS[:, :], mV[:, 1:2], AF.Sqrt), inc="ac")
                prog.wait("vector", tsq)
                trc = prog.op("vector", lambda e, rS=rS: e.reciprocal(
                    rS[:, :], rS[:, :]), inc="dv")
                prog.wait("vector", trc)
                tnb = prog.op("vector", lambda e, nM=nM, rS=rS, nB=nB: e.tensor_tensor(
                    nB[:, :], nM[:, :], rS[:, :], AluOpType.mult), inc="dv")
                prog.wait("scalar", tnb)
                prog.wait_all("scalar", tk(f"xnt{par}").war())
                taf = prog.op("scalar", lambda e, xB=xB, xT=xT, rS=rS, nB=nB:
                              e.activation(
                    xT[:, :], xB[:, :], AF.Identity, bias=nB[:, :],
                    scale=rS[:, :]), inc="ac")
                tk(f"xbuf{par}").read(taf)
                tk(f"xnt{par}").wrote(taf)
                tk(f"lnchain{par}").wrote(taf)
                return par, taf

            def ln_part2(par, taf, dst_xnfm, s4):
                xT = xnt[par]
                prog.wait("tensor", taf)
                prog.wait("tensor", t_id)
                prog.wait_all("tensor", tk("ptx").war())
                for c in range(NCH):
                    ttr = prog.op("tensor", lambda e, c=c, xT=xT: e.transpose(
                        ptx[:, 128 * c:128 * c + 128], xT[:, 128 * c:128 * c + 128],
                        idb[:, :]), inc="pe" if c == NCH - 1 else None)
                tk(f"xnt{par}").read(ttr)
                tk("ptx").wrote(ttr)
                prog.wait("scalar", ttr)
                out_ap = strided(dst_xnfm, 128 * s4, [(512, NCH), (1, 128)])
                in_ap = strided(ptx, 0, [(128, NCH), (1, 128)])
                tev = prog.op("scalar", lambda e, out_ap=out_ap, in_ap=in_ap:
                              e.copy(out_ap, in_ap), inc="ac")
                tk("ptx").read(tev)
                return tev

            nsub = 0

            def do_p1s(g):
                own = g >= NG
                gl = g - NG if own else g
                gpar = g % 2
                fmB = xnfm[gpar]
                nonlocal_state = {}
                p1s = []
                for s4 in range(4):
                    r0 = gl * 512 + s4 * 128
                    rows = (xo if own else xb)[r0:r0 + 128, :]
                    p1s.append(ln_part1(rows, do_p1s.nsub))
                    do_p1s.nsub += 1
                return dict(g=g, gl=gl, own=own, gpar=gpar, fmB=fmB, p1s=p1s)
            do_p1s.nsub = 0

            def do_p2s(st):
                gpar, fmB = st["gpar"], st["fmB"]
                prog.wait_all("scalar", tk(f"xnfm{gpar}").war())
                evtoks = [ln_part2(par, taf, fmB, s4)
                          for s4, (par, taf) in enumerate(st["p1s"])]
                tk(f"xnfm{gpar}").wrote(evtoks[-1])

            def do_tail(st):
                g, gl, own, gpar, fmB = (st["g"], st["gl"], st["own"],
                                         st["gpar"], st["fmB"])
                if own:
                    prog.wait_all("scalar", tk(f"xnfm{gpar}").raw())
                    for c in range(NCH):
                        txn = prog.op("scalar", lambda e, c=c, gl=gl, fmB=fmB:
                                      e.dma_start(
                            out=xn_d[c][:, gl * 512:(gl + 1) * 512],
                            in_=fmB[:, 512 * c:512 * c + 512]), inc16=f"d_no{gpar}")
                        tk(f"xnfm{gpar}").read(txn)
                    return

                cpar = g % 2
                prog.wait_all("sync", tk(f"cosb{cpar}").war())
                prog.op("sync", lambda e, g=g, cpar=cpar: e.dma_start(
                    out=cosb[cpar][:, :], in_=cosf[:, g * 512:(g + 1) * 512]),
                    inc16=f"d_c{cpar}")
                tc2 = prog.op("sync", lambda e, g=g, cpar=cpar: e.dma_start(
                    out=sinb[cpar][:, :], in_=sinf[:, g * 512:(g + 1) * 512]),
                    inc16=f"d_c{cpar}")
                tk(f"cosb{cpar}").wrote(tc2)

                for kvh in range(4):
                    if cfg.get("skip_kv"):
                        break
                    kp = (g * 4 + kvh) % 2
                    prog.wait_all("tensor", tk(f"xnfm{gpar}").raw())
                    prog.wait_all("tensor", tk(f"pk{kp}").war())
                    prog.wait("tensor", t_wkv)
                    for c in range(NCH):
                        tmm = prog.op("tensor", lambda e, c=c, kvh=kvh, kp=kp, fmB=fmB:
                                      e.matmul(
                            pk[kp][:, :],
                            wkb[:, 512 * c + 128 * kvh:512 * c + 128 * kvh + 128],
                            fmB[:, 512 * c:512 * c + 512],
                            start=(c == 0), stop=(c == NCH - 1)),
                            inc="pe" if c == NCH - 1 else None)
                    tk(f"xnfm{gpar}").read(tmm)
                    tk(f"pk{kp}").wrote(tmm)
                    prog.wait("vector", tmm)
                    prog.wait_all("vector", tk(f"cosb{cpar}").raw())
                    prog.wait_all("vector", tk(f"krop{kp}").war())
                    prog.wait_all("vector", tk("ropetmp").war())
                    cb, sn, kr = cosb[cpar], sinb[cpar], krop[kp]
                    pkk = pk[kp]
                    prp = prope[kp]
                    prog.wait_all("vector", tk(f"prope{kp}").war())
                    prog.op("vector", lambda e, pkk=pkk, cb=cb: e.tensor_tensor(
                        tmpA[0:64, :], pkk[0:64, :], cb[:, :], AluOpType.mult))
                    t2m = prog.op("vector", lambda e, pkk=pkk, sn=sn, prp=prp:
                                  e.tensor_tensor(
                        prp[64:128, :], pkk[64:128, :], sn[:, :], AluOpType.mult),
                        inc="dv")
                    prog.wait("vector", t2m)
                    prog.op("vector", lambda e, kr=kr, prp=prp: e.tensor_tensor(
                        kr[0:64, :], tmpA[0:64, :], prp[64:128, :],
                        AluOpType.subtract))
                    prog.op("vector", lambda e, pkk=pkk, cb=cb: e.tensor_tensor(
                        tmpA[64:128, :], pkk[64:128, :], cb[:, :], AluOpType.mult))
                    t4m = prog.op("vector", lambda e, pkk=pkk, sn=sn, prp=prp:
                                  e.tensor_tensor(
                        prp[0:64, :], pkk[0:64, :], sn[:, :], AluOpType.mult),
                        inc="dv")
                    prog.wait("vector", t4m)
                    trp = prog.op("vector", lambda e, kr=kr, prp=prp: e.tensor_tensor(
                        kr[64:128, :], tmpA[64:128, :], prp[0:64, :], AluOpType.add),
                        inc="dv")
                    tk("ropetmp").wrote(trp)
                    tk(f"prope{kp}").wrote(trp)
                    tk(f"pk{kp}").read(trp)
                    tk(f"cosb{cpar}").read(trp)
                    tk(f"krop{kp}").wrote(trp)
                    prog.wait("scalar", trp)
                    tko = prog.op("scalar", lambda e, kvh=kvh, g=g, kr=kr: e.dma_start(
                        out=k_d[kvh][:, g * 512:(g + 1) * 512], in_=kr[:, :]),
                        inc16=f"d_ko{kp}")
                    tk(f"krop{kp}").read(tko)

                    if cfg.get("skip_v"):
                        continue
                    prog.wait_all("tensor", tk("pv").war())
                    for c in range(NCH):
                        tvm = prog.op("tensor", lambda e, c=c, kvh=kvh, fmB=fmB:
                                      e.matmul(
                            pv[:, :],
                            wvb[:, 512 * c + 128 * kvh:512 * c + 128 * kvh + 128],
                            fmB[:, 512 * c:512 * c + 512],
                            start=(c == 0), stop=(c == NCH - 1)),
                            inc="pe" if c == NCH - 1 else None)
                    tk(f"xnfm{gpar}").read(tvm)
                    tk("pv").wrote(tvm)
                    prog.wait("scalar", tvm)
                    prog.wait_all("scalar", tk("vtmp").war())
                    tve = prog.op("scalar", lambda e: e.copy(
                        vtmp[:, :], pv[:, :]), inc="ac")
                    tk("pv").read(tve)
                    tk("vtmp").wrote(tve)
                    prog.wait("tensor", tve)
                    prog.wait_all("tensor", tk("ptv").war())
                    for t in range(4):
                        ttv = prog.op("tensor", lambda e, t=t: e.transpose(
                            ptv[:, 128 * t:128 * t + 128], vtmp[:, 128 * t:128 * t + 128],
                            idb[:, :]), inc="pe" if t == 3 else None)
                    tk("vtmp").read(ttv)
                    tk("ptv").wrote(ttv)
                    vp = (g * 4 + kvh) % 2
                    vh = vhat[vp]
                    prog.wait("scalar", ttv)
                    prog.wait_all("scalar", tk(f"vhat{vp}").war())
                    out_ap = strided(vh, 0, [(129, 4), (1, 128)])
                    in_ap = strided(ptv, 0, [(128, 4), (1, 128)])
                    tvh = prog.op("scalar", lambda e, out_ap=out_ap, in_ap=in_ap:
                                  e.copy(out_ap, in_ap), inc="ac")
                    tk("ptv").read(tvh)
                    tk(f"vhat{vp}").wrote(tvh)
                    prog.wait("scalar", tvh)
                    tvo = prog.op("scalar", lambda e, kvh=kvh, g=g, vh=vh: e.dma_start(
                        out=v_d[kvh][:, g * 516:(g + 1) * 516], in_=vh[:, :]),
                        inc16=f"d_vo{vp}")
                    tk(f"vhat{vp}").read(tvo)


            prev = None
            for g in range(NG + NOG):
                pump_prep(2)
                st = do_p1s(g)
                if prev is not None:
                    do_tail(prev)
                do_p2s(st)
                prev = st
            do_tail(prev)
        prog.barrier("b_p1a", dma_sems=[
            "d_x0", "d_x1", "d_c0", "d_c1", "d_wk0", "d_wk1",
            "d_ko0", "d_ko1", "d_vo0", "d_vo1", "d_no0", "d_no1"])

        if True:
            A.reset(m_prep)
            xno = A.alloc(128, NCH * NO, BF16)
            wqt = [A.alloc(128, NCH * 128, BF16) for i in range(2)]
            qrop = [A.alloc(128, 512, BF16) for i in range(2)]
            cosoB = A.alloc(64, NO, F32)
            sinoB = A.alloc(64, NO, F32)
            tmpA2 = A.alloc(128, 512, F32)
            tmpB2 = A.alloc(128, 512, F32)
            pq = [paf[0][:, 0:512], paf[0][:, 512:1024]]
            prope2 = paf[1][:, 0:512]

            prog.op("sync", lambda e: e.dma_start(out=cosoB[:, :], in_=coso),
                    inc16="d_qc")
            prog.op("sync", lambda e: e.dma_start(out=sinoB[:, :], in_=sino),
                    inc16="d_qc")
            for c in range(NCH):
                prog.op("sync", lambda e, c=c: e.dma_start(
                    out=xno[:, c * NO:(c + 1) * NO], in_=xn_d[c][:, :]), inc16="d_qx")
            t_qx = prog.now("d_qx")

            pump_prep(max(0, 16 - pump_prep.i))
            prog.wait("sync", t_prep_wq[0])
            prog.wait("sync", t_prep_wq[1])
            for h in range(16):
                pump_prep(3)
                hp = h % 2
                prog.wait_all("sync", tk(f"wqt{hp}").war())
                twq = prog.op("sync", lambda e, h=h, hp=hp: e.dma_start(
                    out=wqt[hp][:, :], in_=wq_b[h]), inc16=f"d_wq{hp}")
                tk(f"wqt{hp}").wrote(twq)
                for tg in range(NOG):
                    qp = (h * NOG + tg) % 2
                    prog.wait("tensor", twq)
                    prog.wait("tensor", t_qx)
                    prog.wait_all("tensor", tk(f"pq{qp}").war())
                    for c in range(NCH):
                        tqm = prog.op("tensor", lambda e, c=c, tg=tg, qp=qp, hp=hp:
                                      e.matmul(
                            pq[qp][:, :], wqt[hp][:, c * 128:(c + 1) * 128],
                            xno[:, c * NO + tg * 512: c * NO + tg * 512 + 512],
                            start=(c == 0), stop=(c == NCH - 1)),
                            inc="pe" if c == NCH - 1 else None)
                    tk(f"wqt{hp}").read(tqm)
                    tk(f"pq{qp}").wrote(tqm)
                    prog.wait("vector", tqm)
                    prog.wait(eng="vector", tok=("d_qc", 32))
                    prog.wait_all("vector", tk(f"qrop{qp}").war())
                    prog.wait_all("vector", tk("ropetmp2").war())
                    qr = qrop[qp]
                    pqq = pq[qp]
                    cslc = cosoB[:, tg * 512:(tg + 1) * 512]
                    sslc = sinoB[:, tg * 512:(tg + 1) * 512]
                    prog.op("vector", lambda e, pqq=pqq, cslc=cslc: e.tensor_tensor(
                        tmpA2[0:64, :], pqq[0:64, :], cslc, AluOpType.mult))
                    t2m = prog.op("vector", lambda e, pqq=pqq, sslc=sslc:
                                  e.tensor_tensor(
                        prope2[64:128, :], pqq[64:128, :], sslc, AluOpType.mult),
                        inc="dv")
                    prog.wait("vector", t2m)
                    prog.op("vector", lambda e, qr=qr: e.tensor_tensor(
                        qr[0:64, :], tmpA2[0:64, :], prope2[64:128, :],
                        AluOpType.subtract))
                    prog.op("vector", lambda e, pqq=pqq, cslc=cslc: e.tensor_tensor(
                        tmpA2[64:128, :], pqq[64:128, :], cslc, AluOpType.mult))
                    t4m = prog.op("vector", lambda e, pqq=pqq, sslc=sslc:
                                  e.tensor_tensor(
                        prope2[0:64, :], pqq[0:64, :], sslc, AluOpType.mult), inc="dv")
                    prog.wait("vector", t4m)
                    trp = prog.op("vector", lambda e, qr=qr: e.tensor_tensor(
                        qr[64:128, :], tmpA2[64:128, :], prope2[0:64, :],
                        AluOpType.add), inc="dv")
                    tk("ropetmp2").wrote(trp)
                    tk(f"pq{qp}").read(trp)
                    tk(f"qrop{qp}").wrote(trp)
                    prog.wait("scalar", trp)
                    tqo = prog.op("scalar", lambda e, h=h, tg=tg, qr=qr: e.dma_start(
                        out=q_d[h][:, tg * 512:(tg + 1) * 512], in_=qr[:, :]),
                        inc16=f"d_qo{qp}")
                    tk(f"qrop{qp}").read(tqo)

        prog.barrier("b_p1c", dma_sems=["d_qc", "d_qx", "d_wq0", "d_wq1",
                                        "d_qo0", "d_qo1"])
        if debug:
            for src, name in [(k_d, "d_k"), (v_d, "d_v"), (q_d, "d_q"),
                              (xn_d, "d_xn")]:
                prog.op("sync", lambda e, src=src, name=name: e.dma_start(
                    out=dbg[name], in_=src), inc16="d_dbg")

        if True:
            A.reset(m_prep)
            ksb = [A.alloc(128, 2 * S, BF16) for i in range(2)]
            vsb = [A.alloc(128, 2 * VW, BF16) for i in range(2)]
            msb = A.alloc(128, 16 * 512, BF16)
            qt = [A.alloc(128, 512, BF16) for i in range(3)]
            esb = [A.alloc(128, 1024, BF16) for i in range(2)]
            atn = [A.alloc(128, 128, BF16) for i in range(4)]
            atn4 = [A.alloc(128, 512, BF16) for i in range(2)]
            rcp = A.alloc(128, 4, F32)
            psS = [paf[0], paf[1]]
            psAV = [paf[2][:, 0:512], paf[2][:, 512:1024],
                    paf[3][:, 0:512], paf[3][:, 512:1024]]
            psTV = [pab[2][:, 0:1024], pab[2][:, 1024:2048],
                    pab[3][:, 0:1024], pab[3][:, 1024:2048]]

            for z in range(16):
                t_m = prog.op("sync", lambda e, z=z: e.dma_start(
                    out=msb[:, z * 512:(z + 1) * 512], in_=maskt[z]), inc16="d_m")
            t_mask = t_m

            t_kv_tok = [None, None]
            for pp in range(2):
                for j in range(2):
                    prog.op("sync", lambda e, pp=pp, j=j: e.dma_start(
                        out=ksb[pp][:, j * S:(j + 1) * S], in_=k_d[2 * pp + j][:, :]),
                        inc16=f"d_kv{pp}")
                    prog.op("sync", lambda e, pp=pp, j=j: e.dma_start(
                        out=vsb[pp][:, j * VW:(j + 1) * VW], in_=v_d[2 * pp + j][:, :]),
                        inc16=f"d_kv{pp}")
                t_kv_tok[pp] = prog.now(f"d_kv{pp}")
            for p in range(2):
                t_kv = t_kv_tok[p]
                ksb_p = ksb[p]
                vsb_p = vsb[p]

                units = [(g, h) for g in range(NOG) for h in range(8)]
                if p == 0:
                    prog.wait_all("sync", tk("qt0").war())
                    tq0 = prog.op("sync", lambda e, p=p: e.dma_start(
                        out=qt[0][:, :], in_=q_d[8 * p][:, 0:512]), inc16="d_q0")
                    tk("qt0").wrote(tq0)
                    prog.q_tok = tq0
                    prog.q_next = 1
                for ui, (g, h) in enumerate(units):
                    u = p * NOG * 8 + g * 8 + h
                    up = u % 3
                    gh = 8 * p + h
                    kvloc = h // 4
                    NKB = 16 * g + 16
                    pump_prep(3)
                    tql = prog.q_tok
                    if ui + 1 < len(units):
                        g2, h2 = units[ui + 1]
                        gh2, up2 = 8 * p + h2, (u + 1) % 3
                    elif p == 0:
                        g2, h2 = units[0]
                        gh2, up2 = 8 + h2, (u + 1) % 3
                    else:
                        g2 = None
                    if g2 is not None:
                        prog.wait_all("sync", tk(f"qt{up2}").war())
                        tq2 = prog.op("sync", lambda e, gh2=gh2, g2=g2, up2=up2:
                                      e.dma_start(
                            out=qt[up2][:, :],
                            in_=q_d[gh2][:, g2 * 512:(g2 + 1) * 512]),
                            inc16=f"d_q{up2}")
                        tk(f"qt{up2}").wrote(tq2)
                        prog.q_tok = tq2
                    prog.wait("tensor", tql)
                    prog.wait("tensor", t_kv)
                    avlast = None
                    NKB2 = NKB // 2

                    def col0_of(kb):
                        z = kb - 16 * g
                        return 0 if z < 4 else (z // 4) * 128

                    def emit_S(kb2):
                        sp = kb2 % 2
                        prog.wait_all("tensor", tk(f"psS{sp}").war())
                        for j in range(2):
                            kb = 2 * kb2 + j
                            c0 = col0_of(kb)
                            tsm = prog.op("tensor", lambda e, sp=sp, j=j, kb=kb, c0=c0,
                                          kvloc=kvloc, up=up, ksb_p=ksb_p: e.matmul(
                                psS[sp][:, j * 512 + c0:(j + 1) * 512],
                                ksb_p[:, kvloc * S + kb * 128:
                                      kvloc * S + kb * 128 + 128],
                                qt[up][:, c0:512], start=True, stop=True),
                                inc="pe" if j == 1 else None)
                        tk(f"qt{up}").read(tsm)
                        tk(f"psS{sp}").wrote(tsm)
                        return tsm

                    s_tok = {0: emit_S(0)}
                    for kb2 in range(NKB2):
                        sp = kb2 % 2
                        tsm = s_tok.pop(kb2)
                        prog.wait("scalar", tsm)
                        prog.wait_all("scalar", tk(f"esb{sp}").war())
                        prog.wait("scalar", t_cst)
                        mtoks = [None, None]
                        c00 = col0_of(2 * kb2)
                        tex = prog.op("scalar", lambda e, sp=sp, c00=c00:
                                      e.activation(
                            esb[sp][:, c00:1024],
                            psS[sp][:, c00:1024], AF.Exp,
                            bias=c_shift[:, :], scale=c_scale[:, :]), inc="ac")
                        extoks = [tex, tex]
                        for j in range(2):
                            kb = 2 * kb2 + j
                            c0 = col0_of(kb)
                            z = kb - 16 * g
                            if z >= 0:
                                prog.wait("vector", tex)
                                prog.wait("vector", t_mask)
                                tmk = prog.op("vector", lambda e, sp=sp, j=j, z=z,
                                              c0=c0: e.tensor_tensor(
                                    esb[sp][:, j * 512 + c0:(j + 1) * 512],
                                    esb[sp][:, j * 512 + c0:(j + 1) * 512],
                                    msb[:, z * 512 + c0:(z + 1) * 512],
                                    AluOpType.mult), inc="dv")
                                mtoks[j] = tmk
                        tk(f"psS{sp}").read(extoks[1])
                        tk(f"esb{sp}").wrote(extoks[1])
                        for j in range(2):
                            if mtoks[j] is not None:
                                tk(f"esb{sp}").w.append(mtoks[j])
                        if kb2 + 1 < NKB2:
                            s_tok[kb2 + 1] = emit_S(kb2 + 1)
                        for j in range(2):
                            kb = 2 * kb2 + j
                            z = kb - 16 * g
                            qqmin = 0 if z < 4 else z // 4
                            prog.wait("tensor", extoks[j])
                            if mtoks[j] is not None:
                                prog.wait("tensor", mtoks[j])
                            for qq in range(qqmin, 4):
                                if kb == 0:
                                    prog.wait_all("tensor", tk(f"psAV{qq}").war())
                                last_kb = 16 * g + 4 * qq + 3
                                tav = prog.op("tensor", lambda e, sp=sp, j=j, qq=qq,
                                              kvloc=kvloc, kb=kb, last_kb=last_kb,
                                              vsb_p=vsb_p: e.matmul(
                                    psAV[qq][:, 0:129],
                                    esb[sp][:, j * 512 + qq * 128:
                                            j * 512 + qq * 128 + 128],
                                    vsb_p[:, kvloc * VW + kb * 129:
                                          kvloc * VW + kb * 129 + 129],
                                    start=(kb == 0), stop=(kb == last_kb)),
                                    inc="pe")
                                tk(f"psAV{qq}").wrote(tav)
                                tk(f"esb{sp}").read(tav)
                                avlast = tav
                    prog.wait("vector", avlast)
                    prog.wait_all("vector", tk("rcp").war())
                    for qq in range(4):
                        trr = prog.op("vector", lambda e, qq=qq: e.reciprocal(
                            rcp[:, qq:qq + 1], psAV[qq][:, 128:129]),
                            inc="dv" if qq == 3 else None)
                    tk("rcp").wrote(trr)
                    prog.wait("vector", trr)
                    a4 = atn4[u % 2]
                    prog.wait_all("vector", tk(f"atn4_{u % 2}").war())
                    ntoks = []
                    for qq in range(4):
                        prog.wait_all("vector", tk(f"atn{qq}").war())
                        tns = prog.op("vector", lambda e, qq=qq: e.tensor_scalar(
                            atn[qq][:, :], psAV[qq][:, 0:128], rcp[:, qq:qq + 1],
                            None, AluOpType.mult), inc="dv")
                        tk(f"psAV{qq}").read(tns)
                        tk("rcp").read(tns)
                        tk(f"atn{qq}").wrote(tns)
                        ntoks.append(tns)
                    for qq in range(4):
                        prog.wait("tensor", ntoks[qq])
                        prog.wait("tensor", t_id)
                        ttr = prog.op("tensor", lambda e, qq=qq: e.transpose(
                            psTV[qq][:, 0:128], atn[qq][:, :], idb[:, :]),
                            inc="pe")
                        tk(f"psAV{qq}").wrote(ttr)
                        tk(f"atn{qq}").read(ttr)
                        prog.wait("vector", ttr)
                        tev = prog.op("vector", lambda e, qq=qq, a4=a4:
                                      e.tensor_copy(
                            a4[:, qq * 128:(qq + 1) * 128], psTV[qq][:, 0:128]),
                            inc="dv")
                        tk(f"psAV{qq}").read(tev)
                        tk(f"atn4_{u % 2}").wrote(tev)
                    prog.wait("sync", tev)
                    tad = prog.op("sync", lambda e, g=g, gh=gh, a4=a4:
                                  e.dma_start(
                        out=attn_d[gh * 128:(gh + 1) * 128,
                                   g * 512:(g + 1) * 512],
                        in_=a4[:, :]), inc16=f"d_at{u % 2}")
                    tk(f"atn4_{u % 2}").read(tad)


        prog.barrier("b_p2", dma_sems=["d_q0", "d_q1", "d_kv0", "d_kv1", "d_m",
                                       "d_at0", "d_at1", "d_at2", "d_at3"])
        if debug:
            prog.op("sync", lambda e: e.dma_start(out=dbg["d_attn"], in_=attn_d),
                    inc16="d_dbg")

        pump_prep(len(prep_jobs))

        if True:
            A.reset(m_persist)
            xnoB = A.alloc(128, NCH * 512, BF16)
            act = A.alloc(128, NIC * 512, BF16)
            ract = A.alloc(128, NRIC * 512, BF16)
            acc = A.alloc(128, 4 * D, F32)
            WGN, WDN, WON = 3, 5, 5
            wgt = [A.alloc(128, NCH * 128, BF16) for i in range(WGN)]
            wut = [A.alloc(128, NCH * 128, BF16) for i in range(WGN)]
            wdt = [A.alloc(128, 512, BF16) for i in range(WDN)]
            wot = [A.alloc(128, 512, BF16) for i in range(WON)]
            afm = A.alloc(128, NCH * 512, BF16)
            xob = A.alloc(128, D, F32)
            outb = [A.alloc(128, 512, F32) for i in range(2)]
            sgb = A.alloc(128, 512, F32)
            sgb2 = A.alloc(128, 512, F32)
            rwb = A.alloc(128, NCH * 8, BF16)
            rwf = A.alloc(128, 8, F32)
            lgs = A.alloc(128, 8, F32)
            m1 = A.alloc(128, 1, F32)
            m2 = A.alloc(128, 1, F32)
            eqv = A.alloc(128, 8, F32)
            evv = A.alloc(128, 1, F32)
            svv = A.alloc(128, 1, F32)
            comb = A.alloc(128, 4, F32)
            psG = paf[0][:, 0:512]
            psU = paf[0][:, 512:1024]
            psD = [paf[1][:, 0:512], paf[1][:, 512:1024],
                   paf[2][:, 0:512], paf[2][:, 512:1024]]
            psR = paf[3][:, 512:1024]

            for _e in ("sync", "tensor", "vector", "scalar"):
                prog.wait(_e, prog.now("gp"))
                prog.wait(_e, prog.now("g_out0"))
                prog.wait(_e, prog.now("g_out1"))
                prog.wait(_e, prog.now("g_in0"))
                prog.wait(_e, prog.now("g_in1"))

            for c in range(NCH):
                prog.wait_all("sync", tk("rwf").war())
                tin = prog.op("sync", lambda e, c=c: e.dma_start(
                    out=rwf[:, :], in_=rw[c * 128:(c + 1) * 128, :]), inc16="d_rw")
                tk("rwf").wrote(tin)
                prog.wait("vector", tin)
                tcv = prog.op("vector", lambda e, c=c: e.tensor_copy(
                    rwb[:, c * 8:(c + 1) * 8], rwf[:, :]), inc="dv")
                tk("rwf").read(tcv)
            t_rwb = prog.now("dv")

            for blk in range(NOG):
                prog.wait_all("sync", tk("xnoB").war())
                for c in range(NCH):
                    txn = prog.op("sync", lambda e, c=c, blk=blk: e.dma_start(
                        out=xnoB[:, c * 512:(c + 1) * 512],
                        in_=xn_d[c][:, blk * 512:(blk + 1) * 512]), inc16="d_xb")
                tk("xnoB").wrote(txn)
                t_xn = prog.now("d_xb")

                for t4 in range(4):
                    prog.wait("tensor", t_xn)
                    prog.wait("tensor", t_rwb)
                    prog.wait_all("tensor", tk("psR").war())
                    for c in range(NCH):
                        trm = prog.op("tensor", lambda e, c=c, t4=t4: e.matmul(
                            psR[:, 0:8],
                            xnoB[:, c * 512 + t4 * 128: c * 512 + t4 * 128 + 128],
                            rwb[:, c * 8:(c + 1) * 8],
                            start=(c == 0), stop=(c == NCH - 1)),
                            inc="pe" if c == NCH - 1 else None)
                    tk("psR").wrote(trm)
                    tk("xnoB").read(trm)
                    prog.wait("vector", trm)
                    prog.wait_all("vector", tk("router").war())
                    tcp = prog.op("vector", lambda e: e.tensor_copy(
                        lgs[:, :], psR[:, 0:8]), inc="dv")
                    tk("psR").read(tcp)
                    prog.wait("vector", tcp)
                    tm1 = prog.op("vector", lambda e: e.reduce_max(
                        m1[:, :], lgs[:, :], axis=AX), inc="dv")
                    prog.wait("vector", tm1)
                    teq = prog.op("vector", lambda e: e.tensor_scalar(
                        eqv[:, :], lgs[:, :], m1[:, :], None, AluOpType.is_equal),
                        inc="dv")
                    prog.wait("vector", teq)
                    tms = prog.op("vector", lambda e: e.scalar_tensor_tensor(
                        eqv[:, :], eqv[:, :], -1e30, lgs[:, :],
                        AluOpType.mult, AluOpType.add), inc="dv")
                    prog.wait("vector", tms)
                    tm2 = prog.op("vector", lambda e: e.reduce_max(
                        m2[:, :], eqv[:, :], axis=AX), inc="dv")
                    prog.wait("vector", tm2)
                    tdd = prog.op("vector", lambda e: e.tensor_tensor(
                        m2[:, :], m2[:, :], m1[:, :], AluOpType.subtract), inc="dv")
                    prog.wait("scalar", tdd)
                    tev = prog.op("scalar", lambda e: e.activation(
                        evv[:, :], m2[:, :], AF.Exp), inc="ac")
                    prog.wait("vector", tev)
                    tsv = prog.op("vector", lambda e: e.tensor_scalar(
                        svv[:, :], evv[:, :], 1.0, None, AluOpType.add), inc="dv")
                    prog.wait("vector", tsv)
                    trc = prog.op("vector", lambda e: e.reciprocal(
                        svv[:, :], svv[:, :]), inc="dv")
                    prog.wait("vector", trc)
                    tcb = prog.op("vector", lambda e, t4=t4: e.scalar_tensor_tensor(
                        comb[:, t4:t4 + 1], evv[:, :], 1.0, svv[:, :],
                        AluOpType.add, AluOpType.mult), inc="dv")
                    tk("router").wrote(tcb)
                    tk(f"comb{t4}").wrote(tcb)

                def gateup(nic, gp_src, up_src, act_dst, actname):
                    prog.wait_all("vector", tk(actname).war())
                    for ic in range(nic):
                        wp = ic % WGN
                        wgt_b, wut_b = wgt[wp], wut[wp]
                        prog.wait_all("sync", tk(f"wgt{wp}").war())
                        prog.op("sync", lambda e, ic=ic, gp_src=gp_src, wgt_b=wgt_b:
                                e.dma_start(out=wgt_b, in_=gp_src[ic]),
                            inc16=f"d_g{wp}")
                        prog.op("sync", lambda e, ic=ic, up_src=up_src, wut_b=wut_b:
                                e.dma_start(out=wut_b, in_=up_src[ic]),
                            inc16=f"d_g{wp}")
                        tld = prog.now(f"d_g{wp}")
                        tk(f"wgt{wp}").wrote(tld)
                        prog.wait("tensor", tld)
                        prog.wait("tensor", t_xn)
                        prog.wait_all("tensor", tk("psG").war())
                        for c in range(NCH):
                            tgm = prog.op("tensor", lambda e, c=c, wgt_b=wgt_b:
                                          e.matmul(
                                psG[:, :], wgt_b[:, c * 128:(c + 1) * 128],
                                xnoB[:, c * 512:(c + 1) * 512],
                                start=(c == 0), stop=(c == NCH - 1)),
                                inc="pe" if c == NCH - 1 else None)
                        tk("psG").wrote(tgm)
                        prog.wait_all("tensor", tk("psU").war())
                        for c in range(NCH):
                            tum = prog.op("tensor", lambda e, c=c, wut_b=wut_b:
                                          e.matmul(
                                psU[:, :], wut_b[:, c * 128:(c + 1) * 128],
                                xnoB[:, c * 512:(c + 1) * 512],
                                start=(c == 0), stop=(c == NCH - 1)),
                                inc="pe" if c == NCH - 1 else None)
                        tk("psU").wrote(tum)
                        tk(f"wgt{wp}").read(tum)
                        prog.wait("scalar", tgm)
                        prog.wait_all("scalar", tk("sgb").war())
                        tsl = prog.op("scalar", lambda e: e.activation(
                            sgb[:, :], psG[:, :], AF.Silu), inc="ac")
                        tk("psG").read(tsl)
                        tk("sgb").wrote(tsl)
                        prog.wait("vector", tsl)
                        prog.wait("vector", tum)
                        tac = prog.op("vector", lambda e, ic=ic, act_dst=act_dst:
                                      e.tensor_tensor(
                            act_dst[:, ic * 512:(ic + 1) * 512], sgb[:, :], psU[:, :],
                            AluOpType.mult), inc="dv")
                        tk("psU").read(tac)
                        tk("sgb").read(tac)
                        tk(actname).wrote(tac)
                    return tac

                t_act = gateup(NIC, wg_p, wu_p, act, "act")
                t_ract = gateup(NRIC, rg_p, ru_p, ract, "ract")

                for od in range(4):
                    for ic in range(NIC):
                        wp = ic % WDN
                        wdt_b = wdt[wp]
                        prog.wait_all("scalar", tk(f"wdt{wp}").war())
                        tdl = prog.op("scalar", lambda e, ic=ic, od=od, wdt_b=wdt_b:
                                      e.dma_start(
                            out=wdt_b[:, :],
                            in_=wd_b[ic * 128:(ic + 1) * 128, od * 512:(od + 1) * 512]),
                            inc16=f"d_d{wp}")
                        tk(f"wdt{wp}").wrote(tdl)
                        prog.wait("tensor", tdl)
                        if ic == 0:
                            prog.wait("tensor", t_act)
                            for t4 in range(4):
                                prog.wait_all("tensor", tk(f"psD{t4}").war())
                        for t4 in range(4):
                            tdm = prog.op("tensor", lambda e, ic=ic, t4=t4, wdt_b=wdt_b:
                                          e.matmul(
                                psD[t4][:, :],
                                act[:, ic * 512 + t4 * 128: ic * 512 + t4 * 128 + 128],
                                wdt_b[:, :],
                                start=(ic == 0), stop=(ic == NIC - 1)),
                                inc="pe" if t4 == 3 else None)
                            tk(f"psD{t4}").wrote(tdm)
                            tk("act").read(tdm)
                        tk(f"wdt{wp}").read(tdm)
                    prog.wait("vector", tdm)
                    prog.wait_all("vector", tk("acc").war())
                    for t4 in range(4):
                        tevc = prog.op("vector", lambda e, t4=t4, od=od: e.tensor_copy(
                            acc[:, t4 * D + od * 512: t4 * D + od * 512 + 512],
                            psD[t4][:, :]), inc="dv")
                        tk(f"psD{t4}").read(tevc)
                    tk("acc").wrote(tevc)

                for od in range(4):
                    for ic in range(NRIC):
                        wp = ic % WDN
                        wdt_b = wdt[wp]
                        prog.wait_all("scalar", tk(f"wdt{wp}").war())
                        tdl = prog.op("scalar", lambda e, ic=ic, od=od, wdt_b=wdt_b:
                                      e.dma_start(
                            out=wdt_b[:, :],
                            in_=rd_b[ic * 128:(ic + 1) * 128, od * 512:(od + 1) * 512]),
                            inc16=f"d_d{wp}")
                        tk(f"wdt{wp}").wrote(tdl)
                        prog.wait("tensor", tdl)
                        if ic == 0:
                            prog.wait("tensor", t_ract)
                            for t4 in range(4):
                                prog.wait_all("tensor", tk(f"psD{t4}").war())
                        for t4 in range(4):
                            tdm = prog.op("tensor", lambda e, ic=ic, t4=t4, wdt_b=wdt_b:
                                          e.matmul(
                                psD[t4][:, :],
                                ract[:, ic * 512 + t4 * 128: ic * 512 + t4 * 128 + 128],
                                wdt_b[:, :],
                                start=(ic == 0), stop=(ic == NRIC - 1)),
                                inc="pe" if t4 == 3 else None)
                            tk(f"psD{t4}").wrote(tdm)
                            tk("ract").read(tdm)
                        tk(f"wdt{wp}").read(tdm)
                    prog.wait("vector", tdm)
                    for t4 in range(4):
                        prog.wait_all("vector", tk(f"comb{t4}").raw())
                        prog.wait_all("vector", tk("sgb2").war())
                        tsc = prog.op("vector", lambda e, t4=t4: e.tensor_scalar(
                            sgb2[:, :], psD[t4][:, :], comb[:, t4:t4 + 1], None,
                            AluOpType.mult), inc="dv")
                        tk(f"psD{t4}").read(tsc)
                        tk("sgb2").wrote(tsc)
                        prog.wait("vector", tsc)
                        prog.wait_all("vector", tk("acc").raw())
                        tadd = prog.op("vector", lambda e, t4=t4, od=od:
                                       e.tensor_tensor(
                            acc[:, t4 * D + od * 512: t4 * D + od * 512 + 512],
                            acc[:, t4 * D + od * 512: t4 * D + od * 512 + 512],
                            sgb2[:, :], AluOpType.add), inc="dv")
                        tk("sgb2").read(tadd)
                        tk("acc").wrote(tadd)

                prog.wait_all("sync", tk("afm").war())
                for c in range(NCH):
                    tal = prog.op("sync", lambda e, c=c, blk=blk: e.dma_start(
                        out=afm[:, c * 512:(c + 1) * 512],
                        in_=attn_d[c * 128:(c + 1) * 128,
                                   blk * 512:(blk + 1) * 512]), inc16="d_am")
                tk("afm").wrote(tal)
                t_afm = prog.now("d_am")

                for od in range(4):
                    for c in range(NCH):
                        wp = c % WON
                        wot_b = wot[wp]
                        prog.wait_all("scalar", tk(f"wot{wp}").war())
                        twl = prog.op("scalar", lambda e, c=c, od=od, wot_b=wot_b:
                                      e.dma_start(
                            out=wot_b[:, :],
                            in_=wo_b[c * 128:(c + 1) * 128, od * 512:(od + 1) * 512]),
                            inc16=f"d_o{wp}")
                        tk(f"wot{wp}").wrote(twl)
                        prog.wait("tensor", twl)
                        if c == 0:
                            prog.wait("tensor", t_afm)
                            for t4 in range(4):
                                prog.wait_all("tensor", tk(f"psD{t4}").war())
                        for t4 in range(4):
                            twm = prog.op("tensor", lambda e, c=c, t4=t4, wot_b=wot_b:
                                          e.matmul(
                                psD[t4][:, :],
                                afm[:, c * 512 + t4 * 128:
                                    c * 512 + t4 * 128 + 128],
                                wot_b[:, :],
                                start=(c == 0), stop=(c == NCH - 1)),
                                inc="pe" if t4 == 3 else None)
                            tk(f"psD{t4}").wrote(twm)
                            tk("afm").read(twm)
                        tk(f"wot{wp}").read(twm)
                    prog.wait("vector", twm)
                    prog.wait_all("vector", tk("acc").raw())
                    for t4 in range(4):
                        tadd = prog.op("vector", lambda e, t4=t4, od=od:
                                       e.tensor_tensor(
                            acc[:, t4 * D + od * 512: t4 * D + od * 512 + 512],
                            acc[:, t4 * D + od * 512: t4 * D + od * 512 + 512],
                            psD[t4][:, :], AluOpType.add), inc="dv")
                        tk(f"psD{t4}").read(tadd)
                    tk("acc").wrote(tadd)

                for t4 in range(4):
                    prog.wait_all("sync", tk("xob").war())
                    txl = prog.op("sync", lambda e, t4=t4, blk=blk: e.dma_start(
                        out=xob[:, :],
                        in_=xo[blk * 512 + t4 * 128: blk * 512 + t4 * 128 + 128, :]),
                        inc16="d_xo")
                    tk("xob").wrote(txl)
                    prog.wait("vector", txl)
                    prog.wait_all("vector", tk("acc").raw())
                    for od in range(4):
                        op_ = od % 2
                        prog.wait_all("vector", tk(f"outb{op_}").war())
                        tfin = prog.op("vector", lambda e, t4=t4, od=od, op_=op_:
                                       e.tensor_tensor(
                            outb[op_][:, :],
                            acc[:, t4 * D + od * 512: t4 * D + od * 512 + 512],
                            xob[:, od * 512:(od + 1) * 512],
                            AluOpType.add), inc="dv")
                        tk("xob").read(tfin)
                        tk("acc").read(tfin)
                        tk(f"outb{op_}").wrote(tfin)
                        prog.wait("scalar", tfin)
                        tyo = prog.op("scalar", lambda e, t4=t4, blk=blk, od=od,
                                      op_=op_: e.dma_start(
                            out=y[blk * 512 + t4 * 128: blk * 512 + t4 * 128 + 128,
                                  od * 512:(od + 1) * 512],
                            in_=outb[op_][:, :]), inc16=f"d_y{op_}")
                        tk(f"outb{op_}").read(tyo)

        prog.wait("sync", prog.now("d_y0"))
        prog.wait("sync", prog.now("d_y1"))
        if debug:
            prog.wait("sync", prog.now("d_dbg"))

        block = octx.enter_context(nc.Block())
        prog.emit(block)

    return nc



def _host_tables(cfg):
    S = cfg["S"]
    t = np.arange(S, dtype=np.float32)
    j = np.arange(0, 128, 2, dtype=np.float32)
    inv = (1.0 / (10000.0 ** (j / 128.0))).astype(np.float32)
    fr = t[:, None] * inv[None, :]
    cos = np.cos(fr).astype(np.float32)
    sin = np.sin(fr).astype(np.float32)
    masks = np.zeros((4, 16, 128, 512), dtype=ml_dtypes.bfloat16)
    for r in range(4):
        z = np.arange(16)[:, None, None]
        s = np.arange(128)[None, :, None]
        col = np.arange(512)[None, None, :]
        masks[r] = (128 * z + s <= 4 * col + r).astype(ml_dtypes.bfloat16)
    return cos, sin, masks


def make_in_maps(cfg, x, Wq, Wk, Wv, Wo, router_w, sh_gate, sh_up, sh_down,
                 r_gate, r_up, r_down):
    S = cfg["S"]
    cos, sin, masks = _host_tables(cfg)
    cosT = np.ascontiguousarray(cos.T)
    sinT = np.ascontiguousarray(sin.T)
    ident = np.eye(128, dtype=ml_dtypes.bfloat16)
    com = {
        "wq": np.ascontiguousarray(np.asarray(Wq, np.float32)),
        "wk": np.ascontiguousarray(np.asarray(Wk, np.float32)),
        "wv": np.ascontiguousarray(np.asarray(Wv, np.float32)),
        "wo": np.ascontiguousarray(np.asarray(Wo, np.float32)),
        "wg": np.ascontiguousarray(np.asarray(sh_gate, np.float32).T),
        "wu": np.ascontiguousarray(np.asarray(sh_up, np.float32).T),
        "wd": np.ascontiguousarray(np.asarray(sh_down, np.float32).T),
        "rg": np.ascontiguousarray(np.asarray(r_gate, np.float32).T),
        "ru": np.ascontiguousarray(np.asarray(r_up, np.float32).T),
        "rd": np.ascontiguousarray(np.asarray(r_down, np.float32).T),
        "rw": np.ascontiguousarray(np.asarray(router_w, np.float32).T),
        "cosf": cosT, "sinf": sinT, "ident": ident,
    }
    in_maps = []
    for c in range(8):
        b, r = c // 4, c % 4
        m = dict(com)
        m["xb"] = np.ascontiguousarray(x[b])
        m["xo"] = np.ascontiguousarray(x[b][r::4])
        m["coso"] = np.ascontiguousarray(cosT[:, r::4])
        m["sino"] = np.ascontiguousarray(sinT[:, r::4])
        m["maskt"] = np.ascontiguousarray(masks[r])
        in_maps.append(m)
    return in_maps


def kernel(L0_attn_x, L0_attn_Wq, L0_attn_Wk, L0_attn_Wv, L0_attn_Wo,
           L0_ffn_router_weight, L0_ffn_sh_gate, L0_ffn_sh_up, L0_ffn_sh_down,
           L0_ffn_r_gate, L0_ffn_r_up, L0_ffn_r_down):
    cfg = default_cfg()
    S, D = cfg["S"], cfg["D"]
    x = np.ascontiguousarray(np.asarray(L0_attn_x, dtype=np.float32))
    in_maps = make_in_maps(cfg, x, L0_attn_Wq, L0_attn_Wk, L0_attn_Wv, L0_attn_Wo,
                           L0_ffn_router_weight, L0_ffn_sh_gate, L0_ffn_sh_up,
                           L0_ffn_sh_down, L0_ffn_r_gate, L0_ffn_r_up, L0_ffn_r_down)
    nc = build(cfg)
    res = run_bass_kernel_spmd(nc, in_maps, core_ids=list(range(8)))
    out = np.empty((2, S, D), dtype=np.float32)
    for c in range(8):
        b, r = c // 4, c % 4
        out[b, r::4] = res.results[c]["y"]
    return out
